# version 06
# baseline (speedup 1.0000x reference)
import numpy as np
import ml_dtypes
from contextlib import ExitStack
import concourse.bass as bass
import concourse.mybir as mybir

F32 = mybir.dt.float32
BF16 = mybir.dt.bfloat16
U8 = mybir.dt.uint8
AF = mybir.ActivationFunctionType
ALU = mybir.AluOpType
EPS = 1e-5


class SemC:
    def __init__(self, h):
        self.h = h
        self.n = 0

    def tick(self, instr, k=1):
        instr.then_inc(self.h, k)
        self.n += k
        return self.n


class K:
    def __init__(self):
        self.nc = bass.Bass("TRN2", target_bir_lowering=False)
        self.ctx = ExitStack()
        self.sems = {}
        self._uid = 0
        self._scopes = []

    def _u(self, name):
        self._uid += 1
        return f"{name}_u{self._uid}"

    def push(self):
        self._scopes.append(ExitStack())

    def pop(self):
        self._scopes.pop().close()

    @property
    def _cur(self):
        return self._scopes[-1] if self._scopes else self.ctx

    def sbuf(self, name, shape, dt=F32):
        return self._cur.enter_context(
            self.nc.sbuf_tensor(self._u(name), shape, dt))

    def psum(self, name, shape, dt=F32):
        return self._cur.enter_context(
            self.nc.psum_tensor(self._u(name), shape, dt))

    def sem(self, name):
        name = self._u(name)
        h = self.ctx.enter_context(self.nc.semaphore(name))
        s = SemC(h)
        self.sems[name] = s
        return s

    def din(self, name, shape, dt=F32):
        return self.nc.dram_tensor(name, shape, dt, kind="ExternalInput").ap()

    def dout(self, name, shape, dt=F32):
        return self.nc.dram_tensor(name, shape, dt, kind="ExternalOutput").ap()

    def dscratch(self, name, shape, dt=F32, addr_space="Local"):
        return self.nc.dram_tensor(name, shape, dt, kind="Internal",
                                   addr_space=addr_space).ap()

    def setup_consts(self, values):
        vals = sorted(set(float(v) for v in values))
        self.cvals = vals
        data = np.tile(np.asarray(vals, np.float32)[None, :], (128, 1))
        self.cdram = self.nc.inline_tensor(data, "const_bank").ap()
        self.csb = self.sbuf("const_bank_sb", [128, len(vals)], F32)
        for i, v in enumerate(vals):
            self.nc.const_aps.register(F32, v, self.csb[:, i:i + 1])
        return self.cdram, self.csb

    def ccol(self, v):
        i = self.cvals.index(float(v))
        return self.csb[:, i:i + 1]

    def block(self):
        return self.ctx.enter_context(self.nc.Block())


def bfcast(x):
    return np.ascontiguousarray(x).astype(ml_dtypes.bfloat16)


def bf2f(x):
    return np.asarray(x).astype(np.float32)


class Stream:
    def __init__(self, k, name, depth=2, inc=16):
        self.sems = [k.sem(f"{name}{j}") for j in range(depth)]
        self.inc = inc
        self.depth = depth
        self.count = 0

    def tick(self, instr):
        s = self.sems[self.count % self.depth]
        instr.then_inc(s.h, self.inc)
        s.n += self.inc
        self.count += 1

    def wait(self, eng, i):
        if i < 0:
            return
        eng.wait_ge(self.sems[i % self.depth].h,
                    self.inc * (i // self.depth + 1))

    def wait_all(self, eng):
        for s in self.sems:
            if s.n:
                eng.wait_ge(s.h, s.n)


def seq(eng, s, inst):
    inst.then_inc(s.h, 1)
    s.n += 1
    eng.wait_ge(s.h, s.n)

def tile_w(wT, NK, NM):
    import numpy as np
    K, M = wT.shape
    A = wT.reshape(NK, 128, NM, 128)
    return np.ascontiguousarray(A.transpose(2, 1, 0, 3)).reshape(NM, 128, NK * 128)

def tile_a(x):
    import numpy as np
    R = x.shape[0] // 128
    A = x.reshape(R, 128, x.shape[1])
    return np.ascontiguousarray(A.transpose(1, 0, 2)).reshape(128, R * x.shape[1])


"""Launch builders for the unfused checkpoint.

Conventions:
- feature-major activations: [channels(128-tiles) x tokens] in SBUF
- weights in HBM pre-transposed to [in x out] fp32, DMA-cast to bf16
- residual stream fp32 in HBM
- DMA completion tracking via klib.Stream (parity-split semaphores)
"""


def build_l2(T=1024, upto='full'):
    k = K()
    nc = k.nc
    NM, NK1, NK3 = 32, 64, 32

    y_in = k.din("y_in", [128, 64 * T], BF16)
    x_in = k.din("x_in", [4096, T], F32)
    tln_in = k.din("tln_in", [4096, T], BF16)
    opT = k.din("opT", [32, 128, 64 * 128], F32)
    wT = k.din("wT", [32, 128, 32 * 128], F32)
    bias_in = k.din("bias_in", [128, 32], F32)
    aT = k.din("aT", [128, 32 * 128], F32)
    bT = k.din("bT", [128, 4096], F32)
    coef_in = k.din("coef_in", [128, 1], F32)
    x2_out = k.dout("x2_out", [4096, T], F32)

    ao_scr = (k.dout("ao_scr", [4096, T], BF16) if upto == 'p1'
              else k.dscratch("ao_scr", [4096, T], BF16))
    x1_scr = (k.dout("x1_scr", [4096, T], F32) if upto == 'p2'
              else k.dscratch("x1_scr", [4096, T], F32))

    cdram, csb = k.setup_consts([EPS, 1.0 / 4096.0, -1.0, -0.5, 2.0])

    y_sb = k.sbuf("y_sb", [128, 64 * T], BF16)
    wbuf = k.sbuf("wbuf", [128, 2 * 64 * 128], BF16)
    ab = k.sbuf("ab", [128, 11 * T], BF16)
    af = k.sbuf("af", [128, 6 * T], F32)
    bias_sb = k.sbuf("bias_sb", [128, 32], F32)
    coef_sb = k.sbuf("coef_sb", [128, 2], F32)
    ones_bf = k.sbuf("ones_bf", [128, 1], BF16)
    ones_f = k.sbuf("ones_f", [128, 1], F32)
    onesr_f = k.sbuf("onesr_f", [1, 128], F32)

    ao_t = [ab[:, 0:T], ab[:, T:2 * T]]
    sq_t = ab[:, 2 * T:3 * T]
    t1_b = [ab[:, 3 * T:4 * T], ab[:, 4 * T:5 * T]]
    bc1_b = ab[:, 5 * T:6 * T]
    bc2_b = ab[:, 6 * T:7 * T]
    lh_bf = None
    a_sb = ab[:, 7 * T:11 * T]
    bT_sb = ab[:, 7 * T:11 * T]
    x_t = [af[:, 0:T], af[:, T:2 * T]]
    x1f = af[:, 2 * T:3 * T]
    r_mean = af[0:1, 3 * T:4 * T]
    r_rstd = af[0:1, 4 * T:5 * T]
    r_tmp = af[0:1, 5 * T:6 * T]

    psA = k.psum("psA", [128, T], F32)
    psB = k.psum("psB", [128, T], F32)
    ps_sum = k.psum("ps_sum", [1, T], F32)
    ps_sq = k.psum("ps_sq", [1, T], F32)

    s_pe = k.sem("s_pe")
    s_dq = k.sem("s_dq")
    s_aq = k.sem("s_aq")
    s_dv = k.sem("s_dv")
    s_ac = k.sem("s_ac")
    stPre = Stream(k, "stPre", depth=1)
    stPreG = Stream(k, "stPreG", depth=1)
    stW = Stream(k, "stW")
    stX = Stream(k, "stX")
    stAO = Stream(k, "stAO")
    stSao = Stream(k, "stSao")
    stSx1 = Stream(k, "stSx1")
    stOut = Stream(k, "stOut")
    stT = Stream(k, "stT")

    block = k.block()

    PE_P1, PE_R1 = 0, 64
    PE_P2, PE_R2 = 65, 97
    PE_LH, PE_P3 = 98, 99
    PE_R3, PE_END = 163, 164
    DV_P1, DV_R1 = 0, 32
    DV_P2, DV_R2 = 35, 67
    DV_H1, DV_LH = 70, 102
    DV_R3, DV_P4 = 103, 106
    DV_END = 138
    AC_P1, AC_R1 = 0, 32
    AC_P2, AC_R2 = 33, 65
    AC_P3, AC_R3 = 66, 130
    AC_END = 131

    def rows_ln(wait_pe):
        @block.vector
        def _(eng):
            eng.wait_ge(s_pe.h, wait_pe)
            seq(eng, s_dq, eng.tensor_scalar(
                out=r_mean[:], in0=ps_sum[:], scalar1=1.0 / 4096.0,
                scalar2=None, op0=ALU.mult))
            seq(eng, s_dq, eng.tensor_scalar(
                out=r_tmp[:], in0=ps_sq[:], scalar1=1.0 / 4096.0,
                scalar2=None, op0=ALU.mult))
            seq(eng, s_dq, eng.tensor_tensor(
                out=r_rstd[:], in0=r_mean[:], in1=r_mean[:], op=ALU.mult))
            eng.tensor_tensor(out=r_rstd[:], in0=r_tmp[:], in1=r_rstd[:],
                              op=ALU.subtract).then_inc(s_dv.h, 1)
            s_dv.n += 1
        dv1 = s_dv.n

        @block.scalar
        def _(eng):
            eng.wait_ge(s_dv.h, dv1)
            seq(eng, s_aq, eng.activation(out=r_tmp[:], in_=r_rstd[:],
                                          func=AF.Ln, bias=EPS))
            eng.activation(out=r_rstd[:], in_=r_tmp[:], func=AF.Exp,
                           scale=-0.5).then_inc(s_ac.h, 1)
            s_ac.n += 1
        ac1 = s_ac.n

        @block.vector
        def _(eng):
            eng.wait_ge(s_ac.h, ac1)
            seq(eng, s_dq, eng.tensor_tensor(out=r_tmp[:], in0=r_mean[:],
                                             in1=r_rstd[:], op=ALU.mult))
            eng.tensor_scalar(out=r_tmp[:], in0=r_tmp[:], scalar1=-1.0,
                              scalar2=None, op0=ALU.mult).then_inc(s_dv.h, 1)
            s_dv.n += 1
        dv2 = s_dv.n

        @block.tensor
        def _(eng):
            eng.wait_ge(s_dv.h, dv2)
            eng.matmul(psA[:, 0:512], onesr_f[:1, :], r_rstd[:, 0:512],
                       start=True, stop=True)
            eng.matmul(psA[:, 512:T], onesr_f[:1, :], r_rstd[:, 512:T],
                       start=True, stop=True)
            eng.matmul(psB[:, 0:512], onesr_f[:1, :], r_tmp[:, 0:512],
                       start=True, stop=True)
            eng.matmul(psB[:, 512:T], onesr_f[:1, :], r_tmp[:, 512:T],
                       start=True, stop=True).then_inc(s_pe.h, 1)
            s_pe.n += 1
        pe1 = s_pe.n

        @block.vector
        def _(eng):
            eng.wait_ge(s_pe.h, pe1)
            seq(eng, s_dq, eng.tensor_copy(bc1_b[:], psA[:]))
            seq(eng, s_dq, eng.tensor_copy(bc2_b[:], psB[:]))
            eng.nop().then_inc(s_dv.h, 1)
            s_dv.n += 1

    lh_bf = wbuf[:, 8192:8192 + T]
    tln_t = [bc1_b, bc2_b]
    w4 = wbuf.ap().rearrange("p (j q c) -> p j q c", j=2, c=128)
    w4b = wbuf[:, 0:8192].rearrange("p (j q c) -> p j q c", j=2, c=128)
    yv = y_sb.ap().rearrange("p (q t) -> p q t", t=T)
    h1 = y_sb[:, 0:32 * T]
    ffn = y_sb[:, 32 * T:64 * T]

    @block.sync
    def _(eng):
        stPre.tick(eng.dma_start(out=csb[:], in_=cdram[:]))
        stPre.tick(eng.dma_start(out=bias_sb[:], in_=bias_in[:]))
        stPre.tick(eng.dma_start(out=coef_sb[:, 0:1], in_=coef_in[:]))
        stPre.tick(eng.dma_start(out=y_sb[:], in_=y_in[:]))

    @block.gpsimd
    def _(eng):
        eng.memset(ones_bf[:], 1.0)
        eng.memset(ones_f[:], 1.0)
        eng.memset(onesr_f[:], 1.0)
        stPreG.tick(eng.dma_start(out=a_sb, in_=aT[:]))

    @block.sync
    def _(eng):
        for m in range(NM):
            if m >= 2:
                eng.wait_ge(s_dv.h, DV_P1 + m - 1)
            stT.tick(eng.dma_start(out=tln_t[m % 2][:],
                                   in_=tln_in[m * 128:(m + 1) * 128, :]))

    @block.gpsimd
    def _(eng):
        for m in range(NM):
            if m >= 2:
                eng.wait_ge(s_pe.h, PE_P1 + 2 * (m - 2) + 1)
            stW.tick(eng.dma_start(out=wbuf[:, (m % 2) * 8192:
                                            (m % 2) * 8192 + 8192],
                                   in_=opT[m]))

    @block.tensor
    def _(eng):
        stPre.wait_all(eng)
        stPreG.wait_all(eng)
        for m in range(NM):
            ps = psA if m % 2 == 0 else psB
            if m >= 1:
                eng.wait_ge(s_dv.h, DV_P1 + m)
                eng.wait_ge(s_ac.h, AC_P1 + m)
                last = None
                for half in range(2):
                    sl = slice(half * 512, half * 512 + 512)
                    eng.matmul(ps_sum[:, sl], ones_bf[:, 0:1],
                               ao_t[(m - 1) % 2][:, sl],
                               start=(m == 1), stop=False)
                    last = eng.matmul(ps_sq[:, sl], ones_bf[:, 0:1],
                                      sq_t[:, sl], start=(m == 1), stop=False)
                last.then_inc(s_pe.h, 1)
                s_pe.n += 1
            stW.wait(eng, m)
            if m >= 2:
                eng.wait_ge(s_dv.h, DV_P1 + m - 1)
                eng.wait_ge(s_ac.h, AC_P1 + m - 1)
            last = None
            for kk in range(NK1):
                for half in range(2):
                    sl = slice(half * 512, half * 512 + 512)
                    last = eng.matmul(ps[:, sl], w4[:, m % 2, kk],
                                      yv[:, kk, sl],
                                      start=(kk == 0), stop=(kk == NK1 - 1))
            last.then_inc(s_pe.h, 1)
            s_pe.n += 1
        eng.wait_ge(s_dv.h, DV_P1 + NM)
        eng.wait_ge(s_ac.h, AC_P1 + NM)
        last = None
        for half in range(2):
            sl = slice(half * 512, half * 512 + 512)
            eng.matmul(ps_sum[:, sl], ones_bf[:, 0:1],
                       ao_t[(NM - 1) % 2][:, sl], start=False, stop=True)
            last = eng.matmul(ps_sq[:, sl], ones_bf[:, 0:1], sq_t[:, sl],
                              start=False, stop=True)
        last.then_inc(s_pe.h, 1)
        s_pe.n += 1
    assert s_pe.n == PE_R1, s_pe.n

    @block.vector
    def _(eng):
        for m in range(NM):
            eng.wait_ge(s_pe.h, PE_P1 + 2 * m + 1)
            stT.wait(eng, m)
            stSao.wait(eng, m - 2)
            if m >= 2:
                eng.wait_ge(s_ac.h, AC_P1 + m - 1)
            ps = psA if m % 2 == 0 else psB
            eng.tensor_tensor(out=ao_t[m % 2][:], in0=ps[:],
                              in1=tln_t[m % 2][:],
                              op=ALU.add).then_inc(s_dv.h, 1)
            s_dv.n += 1
    assert s_dv.n == DV_R1

    @block.scalar
    def _(eng):
        for m in range(NM):
            eng.wait_ge(s_dv.h, DV_P1 + m + 1)
            eng.activation(out=sq_t[:], in_=ao_t[m % 2][:],
                           func=AF.Square).then_inc(s_ac.h, 1)
            s_ac.n += 1
            stSao.tick(eng.dma_start(out=ao_scr[m * 128:(m + 1) * 128, :],
                                     in_=ao_t[m % 2][:]))
    assert s_ac.n == AC_R1

    if upto == 'p1':
        @block.scalar
        def _(eng):
            stSao.wait_all(eng)
        return k

    rows_ln(PE_R1)
    assert (s_dv.n, s_ac.n, s_pe.n) == (DV_P2, AC_P2, PE_P2)

    @block.sync
    def _(eng):
        for m in range(NM):
            if m < 2:
                eng.wait_ge(s_pe.h, PE_R1)
                stSao.wait(eng, NM - 2 + m)
            else:
                eng.wait_ge(s_dv.h, DV_P2 + m - 1)
            stSao.wait(eng, m)
            stX.tick(eng.dma_start(out=x_t[m % 2][:],
                                   in_=x_in[m * 128:(m + 1) * 128, :]))
            stAO.tick(eng.dma_start(out=ao_t[m % 2][:],
                                    in_=ao_scr[m * 128:(m + 1) * 128, :]))

    @block.vector
    def _(eng):
        for m in range(NM):
            stX.wait(eng, m)
            stAO.wait(eng, m)
            if m >= 1:
                stSx1.wait(eng, m - 1)
                eng.wait_ge(s_ac.h, AC_P2 + m)
                eng.wait_ge(s_pe.h, PE_P2 + m)
            if m >= 2:
                eng.wait_ge(s_dv.h, DV_P2 + m - 1)
            seq(eng, s_dq, eng.tensor_tensor(out=t1_b[m % 2][:],
                                             in0=ao_t[m % 2][:],
                                             in1=bc1_b[:], op=ALU.mult))
            seq(eng, s_dq, eng.tensor_tensor(out=x1f[:], in0=x_t[m % 2][:],
                                             in1=t1_b[m % 2][:], op=ALU.add))
            eng.tensor_tensor(out=x1f[:], in0=x1f[:], in1=bc2_b[:],
                              op=ALU.add).then_inc(s_dv.h, 1)
            s_dv.n += 1
    assert s_dv.n == DV_R2

    @block.scalar
    def _(eng):
        for m in range(NM):
            eng.wait_ge(s_dv.h, DV_P2 + m + 1)
            if m >= 1:
                eng.wait_ge(s_pe.h, PE_P2 + m)
            eng.activation(out=sq_t[:], in_=x1f[:],
                           func=AF.Square).then_inc(s_ac.h, 1)
            s_ac.n += 1
            stSx1.tick(eng.dma_start(out=x1_scr[m * 128:(m + 1) * 128, :],
                                     in_=x1f[:]))
    assert s_ac.n == AC_R2

    @block.tensor
    def _(eng):
        for m in range(NM):
            eng.wait_ge(s_dv.h, DV_P2 + m + 1)
            eng.wait_ge(s_ac.h, AC_P2 + m + 1)
            last = None
            for half in range(2):
                sl = slice(half * 512, half * 512 + 512)
                eng.matmul(ps_sum[:, sl], ones_f[:, 0:1], x1f[:, sl],
                           start=(m == 0), stop=(m == NM - 1))
                last = eng.matmul(ps_sq[:, sl], ones_bf[:, 0:1], sq_t[:, sl],
                                  start=(m == 0), stop=(m == NM - 1))
            last.then_inc(s_pe.h, 1)
            s_pe.n += 1
    assert s_pe.n == PE_R2

    if upto == 'p2':
        @block.scalar
        def _(eng):
            stSx1.wait_all(eng)
        return k

    rows_ln(PE_R2)
    assert (s_dv.n, s_ac.n, s_pe.n) == (DV_H1, AC_P3, PE_LH)

    @block.sync
    def _(eng):
        for m in range(NM):
            stSx1.wait(eng, m)
            if m < 2:
                eng.wait_ge(s_dv.h, DV_R2)
            else:
                eng.wait_ge(s_dv.h, DV_H1 + m - 1)
            stX.tick(eng.dma_start(out=x_t[m % 2][:],
                                   in_=x1_scr[m * 128:(m + 1) * 128, :]))

    @block.vector
    def _(eng):
        for m in range(NM):
            stX.wait(eng, NM + m)
            if m >= 2:
                eng.wait_ge(s_dv.h, DV_H1 + m - 1)
            seq(eng, s_dq, eng.tensor_tensor(out=t1_b[m % 2][:],
                                             in0=x_t[m % 2][:],
                                             in1=bc1_b[:], op=ALU.mult))
            eng.tensor_tensor(out=h1[:, m * T:(m + 1) * T], in0=t1_b[m % 2][:],
                              in1=bc2_b[:], op=ALU.add).then_inc(s_dv.h, 1)
            s_dv.n += 1
    assert s_dv.n == DV_LH

    @block.tensor
    def _(eng):
        eng.wait_ge(s_dv.h, DV_LH)
        last = None
        for kk in range(NK3):
            for half in range(2):
                c0 = kk * T + half * 512
                last = eng.matmul(psA[:, half * 512:half * 512 + 512],
                                  a_sb[:, kk * 128:(kk + 1) * 128],
                                  h1[:, c0:c0 + 512],
                                  start=(kk == 0), stop=(kk == NK3 - 1))
        last.then_inc(s_pe.h, 1)
        s_pe.n += 1
    assert s_pe.n == PE_P3

    @block.vector
    def _(eng):
        eng.wait_ge(s_pe.h, PE_P3)
        seq(eng, s_dq, eng.tensor_scalar(out=coef_sb[:, 1:2],
                                         in0=coef_sb[:, 0:1], scalar1=2.0,
                                         scalar2=None, op0=ALU.mult))
        eng.tensor_scalar(out=lh_bf[:], in0=psA[:], scalar1=coef_sb[:, 1:2],
                          scalar2=None, op0=ALU.mult).then_inc(s_dv.h, 1)
        s_dv.n += 1
    assert s_dv.n == DV_R3

    @block.gpsimd
    def _(eng):
        eng.wait_ge(s_pe.h, PE_P3)
        stPreG.tick(eng.dma_start(out=bT_sb[:], in_=bT[:]))
        for m in range(NM):
            if m >= 2:
                eng.wait_ge(s_pe.h, PE_P3 + 2 * (m - 2) + 1)
            stW.tick(eng.dma_start(out=wbuf[:, (m % 2) * 4096:
                                            (m % 2) * 4096 + 4096],
                                   in_=wT[m]))

    @block.tensor
    def _(eng):
        eng.wait_ge(s_dv.h, DV_R3)
        stPreG.wait_all(eng)
        for m in range(NM):
            ps = psA if m % 2 == 0 else psB
            if m >= 1:
                eng.wait_ge(s_ac.h, AC_P3 + 2 * m)
                last = None
                for half in range(2):
                    sl = slice(half * 512, half * 512 + 512)
                    c0 = (m - 1) * T + half * 512
                    eng.matmul(ps_sum[:, sl], ones_bf[:, 0:1],
                               ffn[:, c0:c0 + 512], start=(m == 1), stop=False)
                    last = eng.matmul(ps_sq[:, sl], ones_bf[:, 0:1],
                                      sq_t[:, sl], start=(m == 1), stop=False)
                last.then_inc(s_pe.h, 1)
                s_pe.n += 1
            stW.wait(eng, NM + m)
            if m >= 2:
                eng.wait_ge(s_ac.h, AC_P3 + 2 * (m - 1))
            for kk in range(NK3):
                for half in range(2):
                    c0 = kk * T + half * 512
                    eng.matmul(ps[:, half * 512:half * 512 + 512],
                               w4b[:, m % 2, kk], h1[:, c0:c0 + 512],
                               start=(kk == 0), stop=False)
            last = None
            for half in range(2):
                sl = slice(half * 512, half * 512 + 512)
                last = eng.matmul(ps[:, sl], bT_sb[:, m * 128:(m + 1) * 128],
                                  lh_bf[:, sl], start=False, stop=True)
            last.then_inc(s_pe.h, 1)
            s_pe.n += 1
        eng.wait_ge(s_ac.h, AC_P3 + 2 * NM)
        last = None
        for half in range(2):
            sl = slice(half * 512, half * 512 + 512)
            c0 = (NM - 1) * T + half * 512
            eng.matmul(ps_sum[:, sl], ones_bf[:, 0:1], ffn[:, c0:c0 + 512],
                       start=False, stop=True)
            last = eng.matmul(ps_sq[:, sl], ones_bf[:, 0:1], sq_t[:, sl],
                              start=False, stop=True)
        last.then_inc(s_pe.h, 1)
        s_pe.n += 1
    assert s_pe.n == PE_R3

    @block.scalar
    def _(eng):
        for m in range(NM):
            eng.wait_ge(s_pe.h, PE_P3 + 2 * m + 1)
            ps = psA if m % 2 == 0 else psB
            eng.activation(out=ffn[:, m * T:(m + 1) * T], in_=ps[:],
                           func=AF.Identity,
                           bias=bias_sb[:, m:m + 1]).then_inc(s_ac.h, 1)
            eng.activation(out=sq_t[:], in_=ps[:], func=AF.Square,
                           bias=bias_sb[:, m:m + 1]).then_inc(s_ac.h, 1)
            s_ac.n += 2
    assert s_ac.n == AC_R3

    rows_ln(PE_R3)
    assert (s_dv.n, s_ac.n, s_pe.n) == (DV_P4, AC_END, PE_END)

    @block.sync
    def _(eng):
        for m in range(NM):
            if m < 2:
                eng.wait_ge(s_dv.h, DV_P4)
            else:
                eng.wait_ge(s_dv.h, DV_P4 + m - 1)
            stX.tick(eng.dma_start(out=x_t[m % 2][:],
                                   in_=x1_scr[m * 128:(m + 1) * 128, :]))

    @block.vector
    def _(eng):
        for m in range(NM):
            stX.wait(eng, 2 * NM + m)
            if m >= 1:
                stOut.wait(eng, m - 1)
            if m >= 2:
                eng.wait_ge(s_dv.h, DV_P4 + m - 1)
            seq(eng, s_dq, eng.tensor_tensor(out=t1_b[m % 2][:],
                                             in0=ffn[:, m * T:(m + 1) * T],
                                             in1=bc1_b[:], op=ALU.mult))
            seq(eng, s_dq, eng.tensor_tensor(out=x1f[:], in0=x_t[m % 2][:],
                                             in1=t1_b[m % 2][:], op=ALU.add))
            eng.tensor_tensor(out=x1f[:], in0=x1f[:], in1=bc2_b[:],
                              op=ALU.add).then_inc(s_dv.h, 1)
            s_dv.n += 1
    assert s_dv.n == DV_END

    @block.scalar
    def _(eng):
        for m in range(NM):
            eng.wait_ge(s_dv.h, DV_P4 + m + 1)
            stOut.tick(eng.dma_start(out=x2_out[m * 128:(m + 1) * 128, :],
                                     in_=x1f[:]))
        stOut.wait_all(eng)

    return k


def build_l4(T=1024):
    k = K()
    nc = k.nc
    NM, NK = 32, 32

    ctx_in = k.din("ctx_in", [128, 32 * T], BF16)
    x_in = k.din("x_in", [4096, T], F32)
    woT = k.din("woT", [32, 128, 32 * 128], F32)
    wbT = k.din("wbT", [32, 128, 32 * 128], F32)
    bo_in = k.din("bo_in", [128, 32], F32)
    bb_in = k.din("bb_in", [128, 32], F32)
    encT_in = k.din("encT_in", [128, 32 * 40], F32)
    qt_in = k.din("qt_in", [8, 128, 32 * 512], BF16)
    c0_in = k.din("c0_in", [3, 128, 32 * 32], F32)
    c1_in = k.din("c1_in", [3, 128, 32 * 8], F32)
    m0_in = k.din("m0_in", [3, 128, 32 * 32], U8)
    m1_in = k.din("m1_in", [3, 128, 32 * 8], U8)
    x4_out = k.dout("x4_out", [4096, T], F32)

    ao_scr = k.dscratch("ao_scr4", [4096, T], BF16)
    x3_scr = k.dscratch("x3_scr", [4096, T], F32)

    cdram, csb = k.setup_consts([EPS, 1.0 / 4096.0, -1.0, -0.5, 0.5])

    big = k.sbuf("big", [128, 64 * T], BF16)
    wbuf = k.sbuf("wbuf4", [128, 2 * 4096], BF16)
    ab = k.sbuf("ab4", [128, 14 * T], BF16)
    af = k.sbuf("af4", [128, 8 * T], F32)
    bo_sb = k.sbuf("bo_sb", [128, 32], F32)
    bb_sb = k.sbuf("bb_sb", [128, 32], F32)
    enc_bf = k.sbuf("enc_bf", [128, 32 * 40], BF16)
    ones_bf = k.sbuf("ones_bf4", [128, 1], BF16)
    ones_f = k.sbuf("ones_f4", [128, 1], F32)
    onesr_f = k.sbuf("onesr_f4", [1, 128], F32)

    ao_t = [ab[:, 0:T], ab[:, T:2 * T]]
    sq_t = ab[:, 2 * T:3 * T]
    t1_b = [ab[:, 3 * T:4 * T], ab[:, 4 * T:5 * T]]
    bc1_b = ab[:, 5 * T:6 * T]
    bc2_b = ab[:, 6 * T:7 * T]
    e_bf = ab[:, 7 * T:8 * T]
    ws0_b = ab[:, 8 * T:9 * T]
    ws1_b = ab[:, 9 * T:9 * T + 256]
    gt_sb = ab[:, 10 * T:14 * T]

    x_t = [af[:, 0:T], af[:, T:2 * T]]
    x1f = af[:, 2 * T:3 * T]
    g1 = af[:, 3 * T:4 * T]
    g2 = af[:, 4 * T:5 * T]
    r_mean = af[0:1, 5 * T:6 * T]
    r_rstd = af[0:1, 6 * T:7 * T]
    r_tmp = af[0:1, 7 * T:8 * T]

    psA = k.psum("psA4", [128, T], F32)
    psB = k.psum("psB4", [128, T], F32)
    ps_sum = k.psum("ps_sum4", [1, T], F32)
    ps_sq = k.psum("ps_sq4", [1, T], F32)

    s_pe = k.sem("s_pe")
    s_dv = k.sem("s_dv")
    s_ac = k.sem("s_ac")
    s_dq = k.sem("s_dq")
    s_aq = k.sem("s_aq")
    stPre = Stream(k, "stPre", depth=1)
    stPreG = Stream(k, "stPreG", depth=1)
    stW = Stream(k, "stW")
    stQT = Stream(k, "stQT", depth=1)
    stGC = Stream(k, "stGC", depth=2)
    stGM = Stream(k, "stGM", depth=2)
    stX = Stream(k, "stX")
    stAO = Stream(k, "stAO")
    stSao = Stream(k, "stSao")
    stSx1 = Stream(k, "stSx1")
    stOut = Stream(k, "stOut")
    block = k.block()

    PE_P1, PE_R1 = 0, 64
    PE_P2, PE_R2 = 65, 97
    PE_GB, PE_P3 = 98, 107
    PE_R3, PE_END = 171, 172
    DV_P1, DV_R1 = 0, 32
    DV_P2, DV_R2 = 35, 67
    DV_H3, DV_GB = 70, 102
    DV_P3 = 116
    DV_R3, DV_P4 = 116, 119
    DV_END = 151
    AC_P1, AC_R1 = 0, 32
    AC_P2, AC_R2 = 33, 65
    AC_P3, AC_R3 = 66, 130
    AC_END = 131

    def rows_ln(wait_pe):
        @block.vector
        def _(eng):
            eng.wait_ge(s_pe.h, wait_pe)
            seq(eng, s_dq, eng.tensor_scalar(
                out=r_mean[:], in0=ps_sum[:], scalar1=1.0 / 4096.0,
                scalar2=None, op0=ALU.mult))
            seq(eng, s_dq, eng.tensor_scalar(
                out=r_tmp[:], in0=ps_sq[:], scalar1=1.0 / 4096.0,
                scalar2=None, op0=ALU.mult))
            seq(eng, s_dq, eng.tensor_tensor(
                out=r_rstd[:], in0=r_mean[:], in1=r_mean[:], op=ALU.mult))
            eng.tensor_tensor(out=r_rstd[:], in0=r_tmp[:], in1=r_rstd[:],
                              op=ALU.subtract).then_inc(s_dv.h, 1)
            s_dv.n += 1
        dv1 = s_dv.n

        @block.scalar
        def _(eng):
            eng.wait_ge(s_dv.h, dv1)
            seq(eng, s_aq, eng.activation(out=r_tmp[:], in_=r_rstd[:],
                                          func=AF.Ln, bias=EPS))
            eng.activation(out=r_rstd[:], in_=r_tmp[:], func=AF.Exp,
                           scale=-0.5).then_inc(s_ac.h, 1)
            s_ac.n += 1
        ac1 = s_ac.n

        @block.vector
        def _(eng):
            eng.wait_ge(s_ac.h, ac1)
            seq(eng, s_dq, eng.tensor_tensor(out=r_tmp[:], in0=r_mean[:],
                                             in1=r_rstd[:], op=ALU.mult))
            eng.tensor_scalar(out=r_tmp[:], in0=r_tmp[:], scalar1=-1.0,
                              scalar2=None, op0=ALU.mult).then_inc(s_dv.h, 1)
            s_dv.n += 1
        dv2 = s_dv.n

        @block.tensor
        def _(eng):
            eng.wait_ge(s_dv.h, dv2)
            eng.matmul(psA[:, 0:512], onesr_f[:1, :], r_rstd[:, 0:512],
                       start=True, stop=True)
            eng.matmul(psA[:, 512:T], onesr_f[:1, :], r_rstd[:, 512:T],
                       start=True, stop=True)
            eng.matmul(psB[:, 0:512], onesr_f[:1, :], r_tmp[:, 0:512],
                       start=True, stop=True)
            eng.matmul(psB[:, 512:T], onesr_f[:1, :], r_tmp[:, 512:T],
                       start=True, stop=True).then_inc(s_pe.h, 1)
            s_pe.n += 1
        pe1 = s_pe.n

        @block.vector
        def _(eng):
            eng.wait_ge(s_pe.h, pe1)
            seq(eng, s_dq, eng.tensor_copy(bc1_b[:], psA[:]))
            seq(eng, s_dq, eng.tensor_copy(bc2_b[:], psB[:]))
            eng.nop().then_inc(s_dv.h, 1)
            s_dv.n += 1

    ctxv = big[:, 0:32 * T].rearrange("p (q t) -> p q t", t=T)
    h3 = big[:, 0:32 * T]
    ffn = big[:, 32 * T:64 * T]
    qtb = big[:, 32 * T:48 * T]

    @block.sync
    def _(eng):
        stPre.tick(eng.dma_start(out=csb[:], in_=cdram[:]))
        stPre.tick(eng.dma_start(out=bo_sb[:], in_=bo_in[:]))
        stPre.tick(eng.dma_start(out=bb_sb[:], in_=bb_in[:]))
        stPre.tick(eng.dma_start(out=big[:, 0:32 * T], in_=ctx_in[:]))

    @block.gpsimd
    def _(eng):
        eng.memset(ones_bf[:], 1.0)
        eng.memset(ones_f[:], 1.0)
        eng.memset(onesr_f[:], 1.0)
        stPreG.tick(eng.dma_start(out=enc_bf[:], in_=encT_in[:]))

    @block.gpsimd
    def _(eng):
        for m in range(NM):
            if m >= 2:
                eng.wait_ge(s_pe.h, PE_P1 + 2 * (m - 2) + 1)
            stW.tick(eng.dma_start(out=wbuf[:, (m % 2) * 4096:
                                           (m % 2) * 4096 + 4096],
                                   in_=woT[m]))

    w4 = wbuf.ap().rearrange("p (j q c) -> p j q c", j=2, c=128)

    @block.tensor
    def _(eng):
        stPre.wait_all(eng)
        stPreG.wait_all(eng)
        for m in range(NM):
            ps = psA if m % 2 == 0 else psB
            if m >= 1:
                eng.wait_ge(s_dv.h, DV_P1 + m)
                eng.wait_ge(s_ac.h, AC_P1 + m)
                last = None
                for half in range(2):
                    sl = slice(half * 512, half * 512 + 512)
                    eng.matmul(ps_sum[:, sl], ones_bf[:, 0:1],
                               ao_t[(m - 1) % 2][:, sl],
                               start=(m == 1), stop=False)
                    last = eng.matmul(ps_sq[:, sl], ones_bf[:, 0:1],
                                      sq_t[:, sl], start=(m == 1), stop=False)
                last.then_inc(s_pe.h, 1)
                s_pe.n += 1
            stW.wait(eng, m)
            if m >= 2:
                eng.wait_ge(s_dv.h, DV_P1 + m - 1)
                eng.wait_ge(s_ac.h, AC_P1 + m - 1)
            last = None
            for kk in range(NK):
                for half in range(2):
                    sl = slice(half * 512, half * 512 + 512)
                    last = eng.matmul(ps[:, sl], w4[:, m % 2, kk],
                                      ctxv[:, kk, sl],
                                      start=(kk == 0), stop=(kk == NK - 1))
            last.then_inc(s_pe.h, 1)
            s_pe.n += 1
        eng.wait_ge(s_dv.h, DV_P1 + NM)
        eng.wait_ge(s_ac.h, AC_P1 + NM)
        last = None
        for half in range(2):
            sl = slice(half * 512, half * 512 + 512)
            eng.matmul(ps_sum[:, sl], ones_bf[:, 0:1],
                       ao_t[(NM - 1) % 2][:, sl], start=False, stop=True)
            last = eng.matmul(ps_sq[:, sl], ones_bf[:, 0:1], sq_t[:, sl],
                              start=False, stop=True)
        last.then_inc(s_pe.h, 1)
        s_pe.n += 1
    assert s_pe.n == PE_R1, s_pe.n

    @block.vector
    def _(eng):
        for m in range(NM):
            eng.wait_ge(s_pe.h, PE_P1 + 2 * m + 1)
            stSao.wait(eng, m - 2)
            if m >= 2:
                eng.wait_ge(s_ac.h, AC_P1 + m - 1)
            ps = psA if m % 2 == 0 else psB
            eng.tensor_scalar(out=ao_t[m % 2][:], in0=ps[:],
                              scalar1=bo_sb[:, m:m + 1], scalar2=None,
                              op0=ALU.add).then_inc(s_dv.h, 1)
            s_dv.n += 1
    assert s_dv.n == DV_R1

    @block.scalar
    def _(eng):
        for m in range(NM):
            eng.wait_ge(s_dv.h, DV_P1 + m + 1)
            eng.activation(out=sq_t[:], in_=ao_t[m % 2][:],
                           func=AF.Square).then_inc(s_ac.h, 1)
            s_ac.n += 1
            stSao.tick(eng.dma_start(out=ao_scr[m * 128:(m + 1) * 128, :],
                                     in_=ao_t[m % 2][:]))
    assert s_ac.n == AC_R1

    rows_ln(PE_R1)
    assert (s_dv.n, s_ac.n, s_pe.n) == (DV_P2, AC_P2, PE_P2)

    @block.sync
    def _(eng):
        for m in range(NM):
            if m < 2:
                eng.wait_ge(s_pe.h, PE_R1)
                stSao.wait(eng, NM - 2 + m)
            else:
                eng.wait_ge(s_dv.h, DV_P2 + m - 1)
            stSao.wait(eng, m)
            stX.tick(eng.dma_start(out=x_t[m % 2][:],
                                   in_=x_in[m * 128:(m + 1) * 128, :]))
            stAO.tick(eng.dma_start(out=ao_t[m % 2][:],
                                    in_=ao_scr[m * 128:(m + 1) * 128, :]))

    @block.vector
    def _(eng):
        for m in range(NM):
            stX.wait(eng, m)
            stAO.wait(eng, m)
            if m >= 1:
                stSx1.wait(eng, m - 1)
                eng.wait_ge(s_ac.h, AC_P2 + m)
                eng.wait_ge(s_pe.h, PE_P2 + m)
            if m >= 2:
                eng.wait_ge(s_dv.h, DV_P2 + m - 1)
            seq(eng, s_dq, eng.tensor_tensor(out=t1_b[m % 2][:],
                                             in0=ao_t[m % 2][:],
                                             in1=bc1_b[:], op=ALU.mult))
            seq(eng, s_dq, eng.tensor_tensor(out=x1f[:], in0=x_t[m % 2][:],
                                             in1=t1_b[m % 2][:], op=ALU.add))
            eng.tensor_tensor(out=x1f[:], in0=x1f[:], in1=bc2_b[:],
                              op=ALU.add).then_inc(s_dv.h, 1)
            s_dv.n += 1
    assert s_dv.n == DV_R2

    @block.scalar
    def _(eng):
        for m in range(NM):
            eng.wait_ge(s_dv.h, DV_P2 + m + 1)
            if m >= 1:
                eng.wait_ge(s_pe.h, PE_P2 + m)
            eng.activation(out=sq_t[:], in_=x1f[:],
                           func=AF.Square).then_inc(s_ac.h, 1)
            s_ac.n += 1
            stSx1.tick(eng.dma_start(out=x3_scr[m * 128:(m + 1) * 128, :],
                                     in_=x1f[:]))
    assert s_ac.n == AC_R2

    @block.tensor
    def _(eng):
        for m in range(NM):
            eng.wait_ge(s_dv.h, DV_P2 + m + 1)
            eng.wait_ge(s_ac.h, AC_P2 + m + 1)
            last = None
            for half in range(2):
                sl = slice(half * 512, half * 512 + 512)
                eng.matmul(ps_sum[:, sl], ones_f[:, 0:1], x1f[:, sl],
                           start=(m == 0), stop=(m == NM - 1))
                last = eng.matmul(ps_sq[:, sl], ones_bf[:, 0:1], sq_t[:, sl],
                                  start=(m == 0), stop=(m == NM - 1))
            last.then_inc(s_pe.h, 1)
            s_pe.n += 1
    assert s_pe.n == PE_R2

    rows_ln(PE_R2)
    assert (s_dv.n, s_ac.n, s_pe.n) == (DV_H3, AC_P3, PE_GB)

    @block.sync
    def _(eng):
        for m in range(NM):
            stSx1.wait(eng, m)
            if m < 2:
                eng.wait_ge(s_dv.h, DV_R2)
            else:
                eng.wait_ge(s_dv.h, DV_H3 + m - 1)
            stX.tick(eng.dma_start(out=x_t[m % 2][:],
                                   in_=x3_scr[m * 128:(m + 1) * 128, :]))

    @block.vector
    def _(eng):
        for m in range(NM):
            stX.wait(eng, NM + m)
            if m >= 2:
                eng.wait_ge(s_dv.h, DV_H3 + m - 1)
            seq(eng, s_dq, eng.tensor_tensor(out=t1_b[m % 2][:],
                                             in0=x_t[m % 2][:],
                                             in1=bc1_b[:], op=ALU.mult))
            eng.tensor_tensor(out=h3[:, m * T:(m + 1) * T], in0=t1_b[m % 2][:],
                              in1=bc2_b[:], op=ALU.add).then_inc(s_dv.h, 1)
            s_dv.n += 1
    assert s_dv.n == DV_GB

    @block.sync
    def _(eng):
        for e in range(3):
            eng.wait_ge(s_dv.h, DV_GB + e)
            stGC.tick(eng.dma_start(out=g1[:, 0:T], in_=c0_in[e]))
            stGC.tick(eng.dma_start(out=g2[:, 0:256], in_=c1_in[e]))

    @block.gpsimd
    def _(eng):
        for e in range(3):
            eng.wait_ge(s_dv.h, DV_GB + e)
            stGM.tick(eng.dma_start(out=x_t[0][:, 0:T], in_=m0_in[e]))
            stGM.tick(eng.dma_start(out=x_t[1][:, 0:256], in_=m1_in[e]))

    @block.vector
    def _(eng):
        for e in range(3):
            stGC.wait(eng, 2 * e)
            stGC.wait(eng, 2 * e + 1)
            stGM.wait(eng, 2 * e)
            stGM.wait(eng, 2 * e + 1)
            if e == 0:
                seq(eng, s_dq, eng.tensor_tensor(
                    out=x1f[:, 0:T], in0=g1[:, 0:T], in1=x_t[0][:, 0:T],
                    op=ALU.mult))
                seq(eng, s_dq, eng.tensor_tensor(
                    out=g2[:, 512:768], in0=g2[:, 0:256], in1=x_t[1][:, 0:256],
                    op=ALU.mult))
            else:
                seq(eng, s_dq, eng.tensor_tensor(
                    out=g1[:, 0:T], in0=g1[:, 0:T], in1=x_t[0][:, 0:T],
                    op=ALU.mult))
                seq(eng, s_dq, eng.tensor_tensor(
                    out=x1f[:, 0:T], in0=x1f[:, 0:T], in1=g1[:, 0:T],
                    op=ALU.add))
                seq(eng, s_dq, eng.tensor_tensor(
                    out=g2[:, 0:256], in0=g2[:, 0:256], in1=x_t[1][:, 0:256],
                    op=ALU.mult))
                seq(eng, s_dq, eng.tensor_tensor(
                    out=g2[:, 512:768], in0=g2[:, 512:768], in1=g2[:, 0:256],
                    op=ALU.add))
            eng.nop().then_inc(s_dv.h, 1)
            s_dv.n += 1
        eng.tensor_copy(ws0_b[:], x1f[:, 0:T]).then_inc(s_dv.h, 1)
        s_dv.n += 1
        eng.tensor_copy(ws1_b[:], g2[:, 512:768]).then_inc(s_dv.h, 1)
        s_dv.n += 1
    WS_DV = s_dv.n

    @block.sync
    def _(eng):
        for c in range(8):
            if c >= 1:
                eng.wait_ge(s_pe.h, PE_GB + c)
            else:
                eng.wait_ge(s_ac.h, AC_P3)
                eng.wait_ge(s_dv.h, WS_DV)
            stQT.tick(eng.dma_start(out=qtb[:], in_=qt_in[c]))

    ws0v = ws0_b.rearrange("p (q r) -> p q r", r=32)
    ws1v = ws1_b.rearrange("p (q r) -> p q r", r=8)
    qtv = qtb.rearrange("p (q c) -> p q c", c=512)
    encv = enc_bf.rearrange("p (q r) -> p q r", r=40)

    @block.tensor
    def _(eng):
        eng.wait_ge(s_dv.h, WS_DV)
        for c in range(8):
            stQT.wait(eng, c)
            if c >= 1:
                eng.wait_ge(s_dv.h, WS_DV + c)
            last = None
            for kk in range(NK):
                eng.matmul(psA[0:32, 0:512], ws0v[:, kk], qtv[:, kk],
                           start=(kk == 0), stop=(kk == NK - 1))
            for kk in range(NK):
                last = eng.matmul(psA[32:40, 0:512], ws1v[:, kk], qtv[:, kk],
                                  start=(kk == 0), stop=(kk == NK - 1))
            last.then_inc(s_pe.h, 1)
            s_pe.n += 1
        last = None
        for kk in range(NK):
            for half in range(2):
                c0_ = kk * T + half * 512
                last = eng.matmul(psB[0:40, half * 512:half * 512 + 512],
                                  encv[:, kk], h3[:, c0_:c0_ + 512],
                                  start=(kk == 0), stop=(kk == NK - 1))
        last.then_inc(s_pe.h, 1)
        s_pe.n += 1
    assert s_pe.n == PE_P3

    @block.vector
    def _(eng):
        for c in range(8):
            eng.wait_ge(s_pe.h, PE_GB + c + 1)
            eng.tensor_scalar(out=gt_sb[0:40, c * 512:(c + 1) * 512],
                              in0=psA[0:40, 0:512], scalar1=0.5, scalar2=None,
                              op0=ALU.mult).then_inc(s_dv.h, 1)
            s_dv.n += 1
        eng.wait_ge(s_pe.h, PE_P3)
        eng.tensor_copy(e_bf[0:40, :], psB[0:40, :]).then_inc(s_dv.h, 1)
        s_dv.n += 1
    assert s_dv.n == DV_P3, s_dv.n

    @block.gpsimd
    def _(eng):
        for m in range(NM):
            if m < 2:
                eng.wait_ge(s_pe.h, PE_P3)
            else:
                eng.wait_ge(s_pe.h, PE_P3 + 2 * (m - 2) + 1)
            stW.tick(eng.dma_start(out=wbuf[:, (m % 2) * 4096:
                                           (m % 2) * 4096 + 4096],
                                   in_=wbT[m]))

    @block.tensor
    def _(eng):
        eng.wait_ge(s_dv.h, DV_P3)
        for m in range(NM):
            ps = psA if m % 2 == 0 else psB
            if m >= 1:
                eng.wait_ge(s_ac.h, AC_P3 + 2 * m)
                last = None
                for half in range(2):
                    sl = slice(half * 512, half * 512 + 512)
                    c0_ = (m - 1) * T + half * 512
                    eng.matmul(ps_sum[:, sl], ones_bf[:, 0:1],
                               ffn[:, c0_:c0_ + 512], start=(m == 1),
                               stop=False)
                    last = eng.matmul(ps_sq[:, sl], ones_bf[:, 0:1],
                                      sq_t[:, sl], start=(m == 1), stop=False)
                last.then_inc(s_pe.h, 1)
                s_pe.n += 1
            stW.wait(eng, NM + m)
            if m >= 2:
                eng.wait_ge(s_ac.h, AC_P3 + 2 * (m - 1))
            for kk in range(NK):
                for half in range(2):
                    c0_ = kk * T + half * 512
                    eng.matmul(ps[:, half * 512:half * 512 + 512],
                               w4[:, m % 2, kk], h3[:, c0_:c0_ + 512],
                               start=(kk == 0), stop=False)
            last = None
            for half in range(2):
                sl = slice(half * 512, half * 512 + 512)
                last = eng.matmul(ps[:, sl],
                                  gt_sb[0:40, m * 128:(m + 1) * 128],
                                  e_bf[0:40, sl], start=False, stop=True)
            last.then_inc(s_pe.h, 1)
            s_pe.n += 1
        eng.wait_ge(s_ac.h, AC_P3 + 2 * NM)
        last = None
        for half in range(2):
            sl = slice(half * 512, half * 512 + 512)
            c0_ = (NM - 1) * T + half * 512
            eng.matmul(ps_sum[:, sl], ones_bf[:, 0:1], ffn[:, c0_:c0_ + 512],
                       start=False, stop=True)
            last = eng.matmul(ps_sq[:, sl], ones_bf[:, 0:1], sq_t[:, sl],
                              start=False, stop=True)
        last.then_inc(s_pe.h, 1)
        s_pe.n += 1
    assert s_pe.n == PE_R3

    @block.scalar
    def _(eng):
        for m in range(NM):
            eng.wait_ge(s_pe.h, PE_P3 + 2 * m + 1)
            ps = psA if m % 2 == 0 else psB
            eng.activation(out=ffn[:, m * T:(m + 1) * T],
                           in_=ps[:], func=AF.Identity,
                           bias=bb_sb[:, m:m + 1]).then_inc(s_ac.h, 1)
            eng.activation(out=sq_t[:], in_=ps[:], func=AF.Square,
                           bias=bb_sb[:, m:m + 1]).then_inc(s_ac.h, 1)
            s_ac.n += 2
    assert s_ac.n == AC_R3

    rows_ln(PE_R3)
    assert (s_dv.n, s_ac.n, s_pe.n) == (DV_P4, AC_END, PE_END)

    @block.sync
    def _(eng):
        for m in range(NM):
            if m < 2:
                eng.wait_ge(s_dv.h, DV_P4)
            else:
                eng.wait_ge(s_dv.h, DV_P4 + m - 1)
            stX.tick(eng.dma_start(out=x_t[m % 2][:],
                                   in_=x3_scr[m * 128:(m + 1) * 128, :]))

    @block.vector
    def _(eng):
        for m in range(NM):
            stX.wait(eng, 2 * NM + m)
            if m >= 1:
                stOut.wait(eng, m - 1)
            if m >= 2:
                eng.wait_ge(s_dv.h, DV_P4 + m - 1)
            seq(eng, s_dq, eng.tensor_tensor(out=t1_b[m % 2][:],
                                             in0=ffn[:, m * T:(m + 1) * T],
                                             in1=bc1_b[:], op=ALU.mult))
            seq(eng, s_dq, eng.tensor_tensor(out=x1f[:], in0=x_t[m % 2][:],
                                             in1=t1_b[m % 2][:], op=ALU.add))
            eng.tensor_tensor(out=x1f[:], in0=x1f[:], in1=bc2_b[:],
                              op=ALU.add).then_inc(s_dv.h, 1)
            s_dv.n += 1
    assert s_dv.n == DV_END

    @block.scalar
    def _(eng):
        for m in range(NM):
            eng.wait_ge(s_dv.h, DV_P4 + m + 1)
            stOut.tick(eng.dma_start(out=x4_out[m * 128:(m + 1) * 128, :],
                                     in_=x1f[:]))
        stOut.wait_all(eng)

    return k


def build_l3():
    k = K()
    nc = k.nc
    S, NCH, NH, TC, NTC = 4096, 32, 8, 512, 8
    TOPK = 819.0
    ISQ = float(1.0 / np.sqrt(128.0))

    x_in = k.din("x_in", [4096, S], F32)
    wqk = k.din("wqk", [16, 128, 32 * 128], F32)
    wvT = k.din("wvT", [32, 128, 1024], F32)
    bqk = k.din("bqk", [128, 16], F32)
    bv_row = k.din("bv_row", [1, 1024], F32)
    idn_in = k.din("idn_in", [128, 128], BF16)
    ctx_out = k.dout("ctx_out", [1024, S], BF16)

    q_scr = k.dscratch("q_scr", [1024, S], BF16)
    k_scr = k.dscratch("k_scr", [1024, S], BF16)
    vT_scr = k.dscratch("vT_scr", [NH, 128, 32 * 128], BF16)

    witer = [8.0 * 2.0 ** (-i) for i in range(12)]
    OFF12 = 8.0 * (1.0 - 2.0 ** -12)
    cdram, csb = k.setup_consts(
        [EPS, 1.0 / 4096.0, -1.0, -0.5, ISQ, -8.0, 1229.25, -OFF12]
        + [0.5 * w for w in witer])

    x_sb = k.sbuf("x_sb", [128, 32 * TC], F32)
    hc_sb = k.sbuf("hc_sb", [128, 32 * TC], BF16)
    wbuf = k.sbuf("wbuf3", [128, 2 * 4096], BF16)
    vbuf = k.sbuf("vbuf3", [128, 2 * 512], BF16)
    s4_sb = k.sbuf("s4_sb", [128, 4 * S], BF16)
    kh_sb = k.sbuf("kh_sb", [128, S], BF16)
    qh_sb = k.sbuf("qh_sb", [128, S], BF16)
    vt_sb = k.sbuf("vt_sb", [128, 32 * 128], BF16)
    hsc = k.sbuf("hsc", [128, 512], BF16)
    pmt_sb = k.sbuf("pmt_sb", [128, 32 * 128], BF16)
    ctx_sb = k.sbuf("ctx_sb", [128, 2 * 128], BF16)
    e_sb = hc_sb[:, 0:S]
    bv_bc = k.sbuf("bv_bc", [128, 1024], BF16)
    idn_sb = k.sbuf("idn_sb", [128, 128], BF16)
    sq_b = k.sbuf("sq_b3", [128, TC], BF16)
    qkev = k.sbuf("qkev", [128, 2 * TC], BF16)
    vtev = k.sbuf("vtev", [128, 2 * 512], BF16)
    bqk_sb = k.sbuf("bqk_sb", [128, 16], F32)
    bvr_sb = k.sbuf("bvr_sb", [1, 1024], F32)
    ones_bf = k.sbuf("ones_bf3", [128, 1], BF16)
    ones_f = k.sbuf("ones_f3", [128, 1], F32)
    onesr_f = k.sbuf("onesr_f3", [1, 128], F32)
    bc1_b = k.sbuf("bc1_b3", [128, TC], BF16)
    bc2_b = k.sbuf("bc2_b3", [128, TC], BF16)
    selw = k.sbuf("selw", [128, 48], F32)
    rows_f = k.sbuf("rows_f3", [1, 3 * TC], F32)

    lo_c = selw[:, 0:4]
    hi_c = selw[:, 4:8]
    mid_c = selw[:, 8:12]
    cnt_c = selw[:, 12:16]
    sel_c = selw[:, 16:20]
    ntau_c = selw[:, 20:24]
    z_c = selw[:, 24:28]
    iz_c = selw[:, 28:32]
    nmid_c = selw[:, 32:36]
    sgn_c = selw[:, 36:38]
    accr0 = selw[:, 38:42]
    accr1 = selw[:, 42:46]
    r_mean = rows_f[0:1, 0:TC]
    r_rstd = rows_f[0:1, TC:2 * TC]
    r_tmp = rows_f[0:1, 2 * TC:3 * TC]

    psF = k.psum("psF", [128, 3072], F32)
    psT = k.psum("psT", [128, 2048], BF16)
    psQ = psF[:, 0:512]
    psR = psF[:, 512:1024]
    psV = psF[:, 1024:2048]
    ps_sum = psF[0:1, 2048:2560]
    ps_sq = psF[0:1, 2560:3072]
    psS = psF[:, 0:2048]
    psC = psF[:, 2048:2176]

    s_pe = k.sem("s_pe")
    s_dv = k.sem("s_dv")
    s_ac = k.sem("s_ac")
    s_dq = k.sem("s_dq")
    s_aq = k.sem("s_aq")
    stPre = Stream(k, "stPre", depth=1)
    stX = Stream(k, "stX")
    stW = Stream(k, "stW")
    stV = Stream(k, "stV")
    stQK = Stream(k, "stQK")
    stVS = Stream(k, "stVS", depth=8)
    stHL = Stream(k, "stHL", depth=3)
    stCO = Stream(k, "stCO")
    block = k.block()

    @block.sync
    def _(eng):
        stPre.tick(eng.dma_start(out=csb[:], in_=cdram[:]))
        stPre.tick(eng.dma_start(out=bqk_sb[:], in_=bqk[:]))
        stPre.tick(eng.dma_start(out=bvr_sb[:], in_=bv_row[:]))
        stPre.tick(eng.dma_start(out=idn_sb[:], in_=idn_in[:]))

    s_in = k.sem("s_in")

    @block.gpsimd
    def _(eng):
        eng.memset(ones_bf[:], 1.0)
        eng.memset(ones_f[:], 1.0)
        eng.memset(onesr_f[:], 1.0).then_inc(s_in.h, 1)
        s_in.n += 1

    @block.tensor
    def _(eng):
        stPre.wait_all(eng)
        eng.wait_ge(s_in.h, 1)
        eng.matmul(psV[:, 0:512], onesr_f[:1, :], bvr_sb[:1, 0:512],
                   start=True, stop=True)
        eng.matmul(psV[:, 512:1024], onesr_f[:1, :], bvr_sb[:1, 512:1024],
                   start=True, stop=True).then_inc(s_pe.h, 1)
        s_pe.n += 1

    @block.vector
    def _(eng):
        eng.wait_ge(s_pe.h, 1)
        eng.tensor_copy(bv_bc[:], psV[:]).then_inc(s_dv.h, 1)
        s_dv.n += 1
    PE0, DV0, AC0 = 1, 1, 0

    PE_TC, DV_TC, AC_TC = 113, 59, 33

    def pe0(tc):
        return PE0 + tc * PE_TC

    def dv0(tc):
        return DV0 + tc * DV_TC

    def ac0(tc):
        return AC0 + tc * AC_TC

    @block.sync
    def _(eng):
        for tc in range(NTC):
            for m in range(NCH):
                if tc >= 1:
                    eng.wait_ge(s_dv.h, dv0(tc - 1) + 3 + m + 1)
                stX.tick(eng.dma_start(
                    out=x_sb[:, m * TC:(m + 1) * TC],
                    in_=x_in[m * 128:(m + 1) * 128, tc * TC:(tc + 1) * TC]))

    @block.gpsimd
    def _(eng):
        for tc in range(NTC):
            for m in range(16):
                i = tc * 16 + m
                if i >= 2:
                    tc2, m2 = divmod(i - 2, 16)
                    eng.wait_ge(s_pe.h, pe0(tc2) + 33 + m2 + 1)
                stW.tick(eng.dma_start(
                    out=wbuf[:, (m % 2) * 4096:(m % 2) * 4096 + 4096],
                    in_=wqk[m]))
            for half in range(2):
                for kk in range(32):
                    i = tc * 64 + half * 32 + kk
                    if i >= 2:
                        tc2, r2 = divmod(i - 2, 64)
                        eng.wait_ge(s_pe.h, pe0(tc2) + 49 + r2 + 1)
                    stV.tick(eng.dma_start(
                        out=vbuf[:, (i % 2) * 512:(i % 2) * 512 + 512],
                        in_=wvT[kk][:, half * 512:half * 512 + 512]))

    @block.scalar
    def _(eng):
        for tc in range(NTC):
            for m in range(NCH):
                if m == 0:
                    for sx in stX.sems:
                        eng.wait_ge(sx.h, 16 * 16 * (tc + 1))
                if not (tc == 0 and m == 0):
                    prev = pe0(tc) + m if m >= 1 else pe0(tc - 1) + NCH
                    eng.wait_ge(s_pe.h, prev)
                eng.activation(out=sq_b[:], in_=x_sb[:, m * TC:(m + 1) * TC],
                               func=AF.Square).then_inc(s_ac.h, 1)
                s_ac.n += 1
            eng.wait_ge(s_dv.h, dv0(tc) + 1)
            seq(eng, s_aq, eng.activation(out=r_tmp[:], in_=r_rstd[:],
                                          func=AF.Ln, bias=EPS))
            eng.activation(out=r_rstd[:], in_=r_tmp[:], func=AF.Exp,
                           scale=-0.5).then_inc(s_ac.h, 1)
            s_ac.n += 1
            for m in range(16):
                eng.wait_ge(s_dv.h, dv0(tc) + 3 + NCH + m + 1)
                scr = q_scr if m < 8 else k_scr
                row = (m % 8) * 128
                stQK.tick(eng.dma_start(
                    out=scr[row:row + 128, tc * TC:(tc + 1) * TC],
                    in_=qkev[:, (m % 2) * TC:(m % 2) * TC + TC]))
            for half in range(2):
                for tq in range(4):
                    eng.wait_ge(s_dv.h,
                                dv0(tc) + 3 + NCH + 16 + 4 * half + tq + 1)
                    for hh in range(4):
                        h = half * 4 + hh
                        stVS.tick(eng.dma_start(
                            out=vT_scr[h][:, (tc * 4 + tq) * 128:
                                          (tc * 4 + tq + 1) * 128],
                            in_=vtev[:, (tq % 2) * 512 + hh * 128:
                                     (tq % 2) * 512 + (hh + 1) * 128]))
    assert s_ac.n == AC0 + NTC * AC_TC

    @block.tensor
    def _(eng):
        for tc in range(NTC):
            for m in range(NCH):
                eng.wait_ge(s_ac.h, ac0(tc) + m + 1)
                if tc >= 1 and m == 0:
                    eng.wait_ge(s_dv.h, dv0(tc - 1) + 1)
                eng.matmul(ps_sum, ones_f[:, 0:1],
                           x_sb[:, m * TC:(m + 1) * TC],
                           start=(m == 0), stop=(m == NCH - 1))
                eng.matmul(ps_sq, ones_bf[:, 0:1], sq_b[:],
                           start=(m == 0),
                           stop=(m == NCH - 1)).then_inc(s_pe.h, 1)
                s_pe.n += 1
            eng.wait_ge(s_dv.h, dv0(tc) + 2)
            eng.matmul(psQ, onesr_f[:1, :], r_rstd[:, :],
                       start=True, stop=True)
            eng.matmul(psR, onesr_f[:1, :], r_tmp[:, :],
                       start=True, stop=True).then_inc(s_pe.h, 1)
            s_pe.n += 1
            for m in range(16):
                eng.wait_ge(s_dv.h, dv0(tc) + 3 + NCH)
                stW.wait(eng, tc * 16 + m)
                if m >= 2:
                    eng.wait_ge(s_dv.h, dv0(tc) + 3 + NCH + m - 1)
                elif m < 2:
                    pass
                ps = psQ if m % 2 == 0 else psR
                last = None
                for kk in range(32):
                    last = eng.matmul(
                        ps, wbuf[:, (m % 2) * 4096 + kk * 128:
                                 (m % 2) * 4096 + (kk + 1) * 128],
                        hc_sb[:, kk * TC:kk * TC + 512],
                        start=(kk == 0), stop=(kk == 31))
                last.then_inc(s_pe.h, 1)
                s_pe.n += 1
            for half in range(2):
                eng.wait_ge(s_dv.h, dv0(tc) + 3 + NCH + 16 + 4 * half)
                for kk in range(32):
                    i = tc * 64 + half * 32 + kk
                    stV.wait(eng, i)
                    last = None
                    for tq in range(4):
                        last = eng.matmul(
                            psF[:, tq * 512:(tq + 1) * 512],
                            hc_sb[:, kk * TC + tq * 128:
                                  kk * TC + (tq + 1) * 128],
                            vbuf[:, (i % 2) * 512:(i % 2) * 512 + 512],
                            start=(kk == 0), stop=(kk == 31))
                    last.then_inc(s_pe.h, 1)
                    s_pe.n += 1
    assert s_pe.n == PE0 + NTC * PE_TC

    @block.vector
    def _(eng):
        for tc in range(NTC):
            eng.wait_ge(s_pe.h, pe0(tc) + NCH)
            seq(eng, s_dq, eng.tensor_scalar(
                out=r_mean[:], in0=ps_sum, scalar1=1.0 / 4096.0,
                scalar2=None, op0=ALU.mult))
            seq(eng, s_dq, eng.tensor_scalar(
                out=r_tmp[:], in0=ps_sq, scalar1=1.0 / 4096.0,
                scalar2=None, op0=ALU.mult))
            seq(eng, s_dq, eng.tensor_tensor(
                out=r_rstd[:], in0=r_mean[:], in1=r_mean[:], op=ALU.mult))
            eng.tensor_tensor(out=r_rstd[:], in0=r_tmp[:], in1=r_rstd[:],
                              op=ALU.subtract).then_inc(s_dv.h, 1)
            s_dv.n += 1
            eng.wait_ge(s_ac.h, ac0(tc) + NCH + 1)
            seq(eng, s_dq, eng.tensor_tensor(out=r_tmp[:], in0=r_mean[:],
                                             in1=r_rstd[:], op=ALU.mult))
            eng.tensor_scalar(out=r_tmp[:], in0=r_tmp[:], scalar1=-1.0,
                              scalar2=None, op0=ALU.mult).then_inc(s_dv.h, 1)
            s_dv.n += 1
            eng.wait_ge(s_pe.h, pe0(tc) + NCH + 1)
            seq(eng, s_dq, eng.tensor_copy(bc1_b[:], psQ))
            seq(eng, s_dq, eng.tensor_copy(bc2_b[:], psR))
            eng.nop().then_inc(s_dv.h, 1)
            s_dv.n += 1
            for m in range(NCH):
                if tc >= 1 and m == 0:
                    eng.wait_ge(s_pe.h, pe0(tc - 1) + PE_TC)
                if m >= 1:
                    eng.wait_ge(s_dv.h, dv0(tc) + 3 + m)
                seq(eng, s_dq, eng.tensor_tensor(
                    out=hsc[:], in0=x_sb[:, m * TC:(m + 1) * TC],
                    in1=bc1_b[:], op=ALU.mult))
                eng.tensor_tensor(out=hc_sb[:, m * TC:(m + 1) * TC],
                                  in0=hsc[:], in1=bc2_b[:],
                                  op=ALU.add).then_inc(s_dv.h, 1)
                s_dv.n += 1
            for m in range(16):
                eng.wait_ge(s_pe.h, pe0(tc) + NCH + 1 + m + 1)
                stQK.wait(eng, tc * 16 + m - 2)
                ps = psQ if m % 2 == 0 else psR
                if m < 8:
                    eng.tensor_scalar(out=qkev[:, (m % 2) * TC:
                                               (m % 2) * TC + TC],
                                      in0=ps, scalar1=bqk_sb[:, m:m + 1],
                                      scalar2=ISQ, op0=ALU.add,
                                      op1=ALU.mult).then_inc(s_dv.h, 1)
                else:
                    eng.tensor_scalar(out=qkev[:, (m % 2) * TC:
                                               (m % 2) * TC + TC],
                                      in0=ps, scalar1=bqk_sb[:, m:m + 1],
                                      scalar2=None,
                                      op0=ALU.add).then_inc(s_dv.h, 1)
                s_dv.n += 1
            for half in range(2):
                eng.wait_ge(s_pe.h, pe0(tc) + 49 + 32 * (half + 1))
                for tq in range(4):
                    u = (tc * 2 + half) * 4 + tq
                    for hh in range(4):
                        stVS.wait(eng, (u - 2) * 4 + hh)
                    eng.tensor_tensor(
                        out=vtev[:, (tq % 2) * 512:(tq % 2) * 512 + 512],
                        in0=psF[:, tq * 512:(tq + 1) * 512],
                        in1=bv_bc[:, half * 512:half * 512 + 512],
                        op=ALU.add).then_inc(s_dv.h, 1)
                    s_dv.n += 1
    assert s_dv.n == DV0 + NTC * DV_TC

    PE_QKV = PE0 + NTC * PE_TC
    DV_QKV = DV0 + NTC * DV_TC
    AC_QKV = AC0 + NTC * AC_TC

    ITER = 12
    SEL = ITER + 1
    PE_G, DV_G, AC_G = 28, 8 + (ITER + 1) + 28, 1 + 8
    nmid_marks = [0] * (NH * 8)

    def pe1(h, g):
        return PE_QKV + (h * 8 + g) * PE_G

    def dv1(h, g):
        return DV_QKV + (h * 8 + g) * DV_G

    def ac1(h, g):
        return AC_QKV + (h * 8 + g) * AC_G

    def sbase(h, g):
        return PE_QKV + (h * 8 + g) * PE_G

    def tbase(h, g):
        return PE_QKV + (h * 8 + g) * PE_G + 8

    @block.sync
    def _(eng):
        for h in range(NH):
            if h == 0:
                stQK.wait_all(eng)
                stVS.wait_all(eng)
            else:
                eng.wait_ge(s_pe.h, pe1(h, 0))
            stHL.tick(eng.dma_start(out=kh_sb[:],
                                    in_=k_scr[h * 128:(h + 1) * 128, :]))
            stHL.tick(eng.dma_start(out=qh_sb[:],
                                    in_=q_scr[h * 128:(h + 1) * 128, :]))
            stHL.tick(eng.dma_start(out=vt_sb[:], in_=vT_scr[h]))

    @block.tensor
    def _(eng):
        def emit_scores(h, g):
            base_dv = dv1(h, g)
            for j in range(4):
                qb = g * 4 + j
                for r in range(2):
                    k8 = 2 * j + r
                    if k8 == 0:
                        pair = h * 8 + g
                        eng.wait_ge(s_dv.h,
                                    DV_QKV + (pair - 1) * DV_G + 8
                                    if pair else DV_QKV)
                    else:
                        eng.wait_ge(s_dv.h, base_dv + k8)
                    last = None
                    for c in range(4):
                        cc = r * 4 + c
                        last = eng.matmul(
                            psS[:, c * 512:(c + 1) * 512],
                            qh_sb[:, qb * 128:(qb + 1) * 128],
                            kh_sb[:, cc * 512:(cc + 1) * 512],
                            start=True, stop=True)
                    last.then_inc(s_pe.h, 1)
                    s_pe.n += 1

        def emit_trctx(h, g):
            base_dv = dv1(h, g)
            for j in range(4):
                for tg in range(4):
                    eng.wait_ge(s_ac.h, ac1(h, g) + 1 + 2 * j + 2)
                    if h + g + j + tg > 0:
                        eng.wait_ge(s_dv.h,
                                    base_dv + 8 + SEL + 7 * j + 2 + tg - 1)
                    last = None
                    for t8 in range(8):
                        kt = tg * 8 + t8
                        last = eng.transpose(
                            psT[:, (tg % 2) * 1024 + t8 * 128:
                                (tg % 2) * 1024 + (t8 + 1) * 128],
                            s4_sb[:, (g * 4 + j) % 4 * S + kt * 128:
                                  (g * 4 + j) % 4 * S + (kt + 1) * 128],
                            idn_sb[:])
                    last.then_inc(s_pe.h, 1)
                    s_pe.n += 1
                eng.wait_ge(s_dv.h, base_dv + 8 + SEL + 7 * j + 6)
                last = None
                for kt in range(32):
                    last = eng.matmul(psC,
                                      vt_sb[:, kt * 128:(kt + 1) * 128],
                                      pmt_sb[:, kt * 128:(kt + 1) * 128],
                                      start=(kt == 0), stop=(kt == 31))
                last.then_inc(s_pe.h, 1)
                s_pe.n += 1

        for h in range(NH):
            for i in range(3):
                stHL.wait(eng, 3 * h + i)
            for g in range(8):
                assert s_pe.n == sbase(h, g), (h, g, s_pe.n)
                emit_scores(h, g)
                assert s_pe.n == tbase(h, g), (h, g, s_pe.n)
                emit_trctx(h, g)
    assert s_pe.n == PE_QKV + NH * 8 * PE_G

    @block.vector
    def _(eng):
        for h in range(NH):
            for g in range(8):
                base_pe = pe1(h, g)
                base_dv = dv1(h, g)
                for j in range(4):
                    for r in range(2):
                        eng.wait_ge(s_pe.h, sbase(h, g) + 2 * j + r + 1)
                        acc = (accr0 if r == 0 else accr1)[:, j:j + 1]
                        eng.tensor_scalar(
                            out=s4_sb[:, j * S + r * 2048:j * S + r * 2048 + 2048],
                            in0=psS, scalar1=1.0, scalar2=0.0,
                            op0=ALU.mult, op1=ALU.add,
                            accum_out=acc).then_inc(s_dv.h, 1)
                        s_dv.n += 1
                eng.wait_ge(s_dv.h, dv1(h, g) + 8)
                seq(eng, s_dq, eng.tensor_tensor(
                    out=lo_c, in0=accr0, in1=accr1, op=ALU.add))
                seq(eng, s_dq, eng.tensor_scalar(
                    out=lo_c, in0=lo_c, scalar1=1.0 / 4096.0, scalar2=-2.0,
                    op0=ALU.mult, op1=ALU.add))
                nmid_marks[h * 8 + g] = s_dq.n
                lo01 = lo_c[:, 0:2]
                mid01 = mid_c[:, 0:2]
                sel01 = sel_c[:, 0:2]
                z01 = z_c[:, 0:2]
                for it in range(ITER):
                    w = 8.0 * 2.0 ** (-it)
                    seq(eng, s_dq, eng.tensor_scalar(
                        out=mid01, in0=lo01, scalar1=w, scalar2=None,
                        op0=ALU.add))
                    for j in range(2):
                        seq(eng, s_dq, eng.tensor_scalar(
                            out=e_sb[:], in0=s4_sb[:, j * S:(j + 1) * S],
                            scalar1=mid_c[:, j:j + 1], scalar2=0.0,
                            op0=ALU.is_ge, op1=ALU.add,
                            accum_out=cnt_c[:, j:j + 1]))
                    seq(eng, s_dq, eng.tensor_scalar(
                        out=sel01, in0=cnt_c[:, 0:2], scalar1=TOPK,
                        scalar2=None, op0=ALU.is_ge))
                    seq(eng, s_dq, eng.scalar_tensor_tensor(
                        out=lo01, in0=sel01, scalar=w, in1=lo01,
                        op0=ALU.mult, op1=ALU.add))
                    eng.nop().then_inc(s_dv.h, 1)
                    s_dv.n += 1
                seq(eng, s_dq, eng.tensor_scalar(
                    out=ntau_c[:, 0:2], in0=lo01, scalar1=-1.0, scalar2=None,
                    op0=ALU.mult))
                eng.nop().then_inc(s_dv.h, 1)
                s_dv.n += 1
                for j in range(4):
                    qb = g * 4 + j
                    eng.wait_ge(s_ac.h, ac1(h, g) + 1 + 2 * j + 1)
                    seq(eng, s_dq, eng.scalar_tensor_tensor(
                        out=s4_sb[:, j * S:(j + 1) * S], in0=e_sb[:],
                        scalar=1.0, in1=e_sb[:],
                        op0=ALU.is_ge, op1=ALU.mult,
                        accum_out=z_c[:, j:j + 1]))
                    eng.nop().then_inc(s_dv.h, 1)
                    s_dv.n += 1
                    seq(eng, s_dq, eng.reciprocal(iz_c[:, j:j + 1],
                                                  z_c[:, j:j + 1]))
                    eng.nop().then_inc(s_dv.h, 1)
                    s_dv.n += 1
                    for tg in range(4):
                        eng.wait_ge(s_pe.h,
                                    tbase(h, g) + 5 * j + tg + 1)
                        eng.tensor_copy(
                            pmt_sb[:, tg * 1024:(tg + 1) * 1024],
                            psT[:, (tg % 2) * 1024:(tg % 2) * 1024 + 1024]
                        ).then_inc(s_dv.h, 1)
                        s_dv.n += 1
                    eng.wait_ge(s_pe.h, tbase(h, g) + 5 * j + 5)
                    stCO.wait(eng, (h * 32 + qb) - 2)
                    eng.tensor_copy(ctx_sb[:, (qb % 2) * 128:
                                           (qb % 2) * 128 + 128],
                                    psC).then_inc(s_dv.h, 1)
                    s_dv.n += 1
    assert s_dv.n == DV_QKV + NH * 8 * DV_G, s_dv.n

    @block.scalar
    def _(eng):
        for h in range(NH):
            for g in range(8):
                eng.wait_ge(s_dv.h, dv1(h, g) + 8)
                eng.wait_ge(s_dq.h, nmid_marks[h * 8 + g])
                lo23 = lo_c[:, 2:4]
                lo23b = selw[:, 46:48]
                cur = lo23
                for it in range(ITER):
                    w = 8.0 * 2.0 ** (-it)
                    nxt = lo23b if cur is lo23 else lo23
                    seq(eng, s_aq, eng.activation(
                        out=nmid_c[:, 2:4], in_=cur,
                        func=AF.Identity, scale=-1.0, bias=-8.0))
                    seq(eng, s_aq, eng.activation(
                        out=x_sb[:, 0:4096], in_=s4_sb[:, 2 * S:3 * S],
                        func=AF.Sign, bias=nmid_c[:, 2:3],
                        accum_out=sgn_c[:, 0:1]))
                    seq(eng, s_aq, eng.activation(
                        out=x_sb[:, 4096:8192], in_=s4_sb[:, 3 * S:4 * S],
                        func=AF.Sign, bias=nmid_c[:, 3:4],
                        accum_out=sgn_c[:, 1:2]))
                    seq(eng, s_aq, eng.activation(
                        out=sel_c[:, 2:4], in_=sgn_c,
                        func=AF.Sign, scale=0.5, bias=1229.25))
                    seq(eng, s_aq, eng.activation(
                        out=nxt[:, 0:1], in_=sel_c[:, 2:3],
                        func=AF.Identity, scale=0.5 * w, bias=cur[:, 0:1]))
                    seq(eng, s_aq, eng.activation(
                        out=nxt[:, 1:2], in_=sel_c[:, 3:4],
                        func=AF.Identity, scale=0.5 * w, bias=cur[:, 1:2]))
                    cur = nxt
                seq(eng, s_aq, eng.activation(
                    out=ntau_c[:, 2:4], in_=cur,
                    func=AF.Identity, scale=-1.0, bias=-OFF12))
                eng.nop().then_inc(s_ac.h, 1)
                s_ac.n += 1
                for j in range(4):
                    qb = g * 4 + j
                    eng.wait_ge(s_dv.h, dv1(h, g) + 8 + SEL)
                    if j >= 1:
                        eng.wait_ge(s_dv.h, dv1(h, g) + 8 + SEL + 7 * (j - 1) + 1)
                    eng.activation(out=e_sb[:], in_=s4_sb[:, j * S:(j + 1) * S],
                                   func=AF.Exp,
                                   bias=ntau_c[:, j:j + 1]).then_inc(s_ac.h, 1)
                    s_ac.n += 1
                    eng.wait_ge(s_dv.h, dv1(h, g) + 8 + SEL + 7 * j + 2)
                    eng.activation(out=s4_sb[:, j * S:(j + 1) * S],
                                   in_=s4_sb[:, j * S:(j + 1) * S],
                                   func=AF.Identity,
                                   scale=iz_c[:, j:j + 1]).then_inc(s_ac.h, 1)
                    s_ac.n += 1
                    eng.wait_ge(s_dv.h, dv1(h, g) + 8 + SEL + 7 * j + 7)
                    stCO.tick(eng.dma_start(
                        out=ctx_out[h * 128:(h + 1) * 128,
                                    qb * 128:(qb + 1) * 128],
                        in_=ctx_sb[:, (qb % 2) * 128:(qb % 2) * 128 + 128]))
        stCO.wait_all(eng)
    assert s_ac.n == AC_QKV + NH * 8 * AC_G

    return k


def build_l1a(T=1024):
    k = K()
    nc = k.nc
    NM = 32

    x_in = k.din("x_in", [4096, T], F32)
    nw_in = k.din("nw_in", [128, 32], F32)
    h_out = k.dout("h_out", [4096, T], BF16)
    t_out = k.dout("t_out", [4096, T], BF16)

    cdram, csb = k.setup_consts([EPS, 1.0 / 4096.0, -1.0, -0.5])

    t_sb = k.sbuf("t_sb", [128, 32 * T], BF16)
    af = k.sbuf("af1", [128, 6 * T], F32)
    sq_t = k.sbuf("sq_t1", [128, T], BF16)
    hsc = k.sbuf("hsc1", [128, T], BF16)
    nw_sb = k.sbuf("nw_sb", [128, 32], F32)
    ones_bf = k.sbuf("ones_bf1", [128, 1], BF16)
    ones_f = k.sbuf("ones_f1", [128, 1], F32)
    onesr_f = k.sbuf("onesr_f1", [1, 128], F32)
    bc1_b = k.sbuf("bc1_b1", [128, T], BF16)
    bc2_b = k.sbuf("bc2_b1", [128, T], BF16)

    x_t = [af[:, 0:T], af[:, T:2 * T]]
    r_mean = af[0:1, 3 * T:4 * T]
    r_rstd = af[0:1, 4 * T:5 * T]
    r_tmp = af[0:1, 5 * T:6 * T]

    psA = k.psum("psA1", [128, T], F32)
    psB = k.psum("psB1", [128, T], F32)
    ps_sum = k.psum("ps_sum1", [1, T], F32)
    ps_sq = k.psum("ps_sq1", [1, T], F32)

    s_pe = k.sem("s_pe")
    s_dv = k.sem("s_dv")
    s_ac = k.sem("s_ac")
    s_dq = k.sem("s_dq")
    s_aq = k.sem("s_aq")
    s_in = k.sem("s_in")
    stX = Stream(k, "stX")
    stPre = Stream(k, "stPre", depth=1)
    stOut = Stream(k, "stOut")
    stT = Stream(k, "stT")
    block = k.block()


    @block.sync
    def _(eng):
        stPre.tick(eng.dma_start(out=csb[:], in_=cdram[:]))
        stPre.tick(eng.dma_start(out=nw_sb[:], in_=nw_in[:]))
        for m in range(NM):
            if m >= 2:
                eng.wait_ge(s_pe.h, m - 1)
            stX.tick(eng.dma_start(out=x_t[m % 2][:],
                                   in_=x_in[m * 128:(m + 1) * 128, :]))
        for m in range(NM):
            if m < 2:
                eng.wait_ge(s_pe.h, NM)
            else:
                eng.wait_ge(s_dv.h, 3 + m - 1)
            stX.tick(eng.dma_start(out=x_t[m % 2][:],
                                   in_=x_in[m * 128:(m + 1) * 128, :]))

    @block.gpsimd
    def _(eng):
        eng.memset(ones_bf[:], 1.0)
        eng.memset(ones_f[:], 1.0)
        eng.memset(onesr_f[:], 1.0).then_inc(s_in.h, 1)
        s_in.n += 1

    @block.scalar
    def _(eng):
        for m in range(NM):
            stX.wait(eng, m)
            if m >= 1:
                eng.wait_ge(s_pe.h, m)
            eng.activation(out=sq_t[:], in_=x_t[m % 2][:],
                           func=AF.Square).then_inc(s_ac.h, 1)
            s_ac.n += 1
        eng.wait_ge(s_dv.h, 1)
        seq(eng, s_aq, eng.activation(out=r_tmp[:], in_=r_rstd[:],
                                      func=AF.Ln, bias=EPS))
        eng.activation(out=r_rstd[:], in_=r_tmp[:], func=AF.Exp,
                       scale=-0.5).then_inc(s_ac.h, 1)
        s_ac.n += 1
        for m in range(NM):
            eng.wait_ge(s_dv.h, 3 + m + 1)
            if m >= 1:
                eng.wait_ge(s_pe.h, 33 + m)
            eng.activation(out=sq_t[:], in_=t_sb[:, m * T:(m + 1) * T],
                           func=AF.Square).then_inc(s_ac.h, 1)
            s_ac.n += 1
        eng.wait_ge(s_dv.h, 3 + 32 + 1)
        seq(eng, s_aq, eng.activation(out=r_tmp[:], in_=r_rstd[:],
                                      func=AF.Ln, bias=EPS))
        eng.activation(out=r_rstd[:], in_=r_tmp[:], func=AF.Exp,
                       scale=-0.5).then_inc(s_ac.h, 1)
        s_ac.n += 1

    @block.tensor
    def _(eng):
        stPre.wait_all(eng)
        eng.wait_ge(s_in.h, 1)
        for m in range(NM):
            eng.wait_ge(s_ac.h, m + 1)
            stX.wait(eng, m)
            last = None
            for half in range(2):
                sl = slice(half * 512, half * 512 + 512)
                eng.matmul(ps_sum[:, sl], ones_f[:, 0:1], x_t[m % 2][:, sl],
                           start=(m == 0), stop=(m == NM - 1))
                last = eng.matmul(ps_sq[:, sl], ones_bf[:, 0:1], sq_t[:, sl],
                                  start=(m == 0), stop=(m == NM - 1))
            last.then_inc(s_pe.h, 1)
            s_pe.n += 1
        eng.wait_ge(s_dv.h, 2)
        eng.matmul(psA[:, 0:512], onesr_f[:1, :], r_rstd[:, 0:512],
                   start=True, stop=True)
        eng.matmul(psA[:, 512:T], onesr_f[:1, :], r_rstd[:, 512:T],
                   start=True, stop=True)
        eng.matmul(psB[:, 0:512], onesr_f[:1, :], r_tmp[:, 0:512],
                   start=True, stop=True)
        eng.matmul(psB[:, 512:T], onesr_f[:1, :], r_tmp[:, 512:T],
                   start=True, stop=True).then_inc(s_pe.h, 1)
        s_pe.n += 1
        for m in range(NM):
            eng.wait_ge(s_ac.h, 33 + m + 1)
            eng.wait_ge(s_dv.h, 3 + m + 1)
            if m == 0:
                eng.wait_ge(s_dv.h, 3 + 1)
            last = None
            for half in range(2):
                sl = slice(half * 512, half * 512 + 512)
                last = eng.matmul(ps_sq[:, sl], ones_bf[:, 0:1], sq_t[:, sl],
                                  start=(m == 0), stop=(m == NM - 1))
            last.then_inc(s_pe.h, 1)
            s_pe.n += 1
        eng.wait_ge(s_dv.h, 3 + 32 + 2)
        eng.matmul(psA[:, 0:512], onesr_f[:1, :], r_rstd[:, 0:512],
                   start=True, stop=True)
        eng.matmul(psA[:, 512:T], onesr_f[:1, :], r_rstd[:, 512:T],
                   start=True, stop=True).then_inc(s_pe.h, 1)
        s_pe.n += 1

    @block.vector
    def _(eng):
        eng.wait_ge(s_pe.h, NM)
        seq(eng, s_dq, eng.tensor_scalar(
            out=r_mean[:], in0=ps_sum[:], scalar1=1.0 / 4096.0,
            scalar2=None, op0=ALU.mult))
        seq(eng, s_dq, eng.tensor_scalar(
            out=r_tmp[:], in0=ps_sq[:], scalar1=1.0 / 4096.0,
            scalar2=None, op0=ALU.mult))
        seq(eng, s_dq, eng.tensor_tensor(
            out=r_rstd[:], in0=r_mean[:], in1=r_mean[:], op=ALU.mult))
        eng.tensor_tensor(out=r_rstd[:], in0=r_tmp[:], in1=r_rstd[:],
                          op=ALU.subtract).then_inc(s_dv.h, 1)
        s_dv.n += 1
        eng.wait_ge(s_ac.h, 33)
        seq(eng, s_dq, eng.tensor_tensor(out=r_tmp[:], in0=r_mean[:],
                                         in1=r_rstd[:], op=ALU.mult))
        eng.tensor_scalar(out=r_tmp[:], in0=r_tmp[:], scalar1=-1.0,
                          scalar2=None, op0=ALU.mult).then_inc(s_dv.h, 1)
        s_dv.n += 1
        eng.wait_ge(s_pe.h, NM + 1)
        seq(eng, s_dq, eng.tensor_copy(bc1_b[:], psA[:]))
        seq(eng, s_dq, eng.tensor_copy(bc2_b[:], psB[:]))
        eng.nop().then_inc(s_dv.h, 1)
        s_dv.n += 1
        for m in range(NM):
            stX.wait(eng, NM + m)
            if m >= 1:
                eng.wait_ge(s_dv.h, 3 + m)
            seq(eng, s_dq, eng.tensor_tensor(
                out=hsc[:], in0=x_t[m % 2][:], in1=bc1_b[:], op=ALU.mult))
            eng.tensor_tensor(out=t_sb[:, m * T:(m + 1) * T], in0=hsc[:],
                              in1=bc2_b[:], op=ALU.add).then_inc(s_dv.h, 1)
            s_dv.n += 1
        eng.wait_ge(s_pe.h, NM + 1 + NM)
        seq(eng, s_dq, eng.tensor_scalar(
            out=r_rstd[:], in0=ps_sq[:], scalar1=1.0 / 4096.0,
            scalar2=None, op0=ALU.mult))
        eng.nop().then_inc(s_dv.h, 1)
        s_dv.n += 1
        eng.wait_ge(s_ac.h, 66)
        eng.nop().then_inc(s_dv.h, 1)
        s_dv.n += 1
        eng.wait_ge(s_pe.h, NM + 1 + NM + 1)
        seq(eng, s_dq, eng.tensor_copy(bc1_b[:], psA[:]))
        eng.nop().then_inc(s_dv.h, 1)
        s_dv.n += 1
        for m in range(NM):
            if m >= 1:
                stOut.wait(eng, m - 1)
            seq(eng, s_dq, eng.scalar_tensor_tensor(
                out=hsc[:], in0=t_sb[:, m * T:(m + 1) * T],
                scalar=nw_sb[:, m:m + 1], in1=bc1_b[:],
                op0=ALU.mult, op1=ALU.mult))
            eng.nop().then_inc(s_dv.h, 1)
            s_dv.n += 1

    @block.scalar
    def _(eng):
        for m in range(NM):
            eng.wait_ge(s_dv.h, 3 + 32 + 3 + m + 1)
            stOut.tick(eng.dma_start(out=h_out[m * 128:(m + 1) * 128, :],
                                     in_=hsc[:]))
            stT.tick(eng.dma_start(out=t_out[m * 128:(m + 1) * 128, :],
                                   in_=t_sb[:, m * T:(m + 1) * T]))
        stOut.wait_all(eng)
        stT.wait_all(eng)

    return k


def build_l1b():
    k = K()
    nc = k.nc
    NMT = 19
    GRPS = [(0, 3), (3, 3), (6, 3), (9, 3), (12, 3), (15, 3), (18, 1)]
    NG = len(GRPS)
    TB = 1024
    NTB = 8
    NH = 16
    BIG = -1.0e30

    h_in = k.din("h_in", [128, 32 * 8192], BF16)
    wg = k.din("wg", [19, 128, 32 * 128], F32)
    convw = k.din("convw", [128, 40], F32)
    convb = k.din("convb", [128, 10], F32)
    dtb = k.din("dtb", [128, 1], F32)
    alog = k.din("alog", [128, 1], F32)
    dp_row = k.din("dp_row", [1, 16], F32)
    gnw = k.din("gnw", [128, 8], F32)
    triu = k.din("triu", [128, 128], F32)
    mtril = k.din("mtril", [128, 128], F32)
    idn_f = k.din("idn_f", [128, 128], F32)
    idn_b = k.din("idn_b", [128, 128], BF16)
    y_out = k.dout("y_out", [1024, 8192], BF16)

    cdram, csb = k.setup_consts([EPS, 1.0 / 1024.0, -1.0, -0.5, 1.0])

    proj = k.sbuf("proj", [128, NMT * TB], BF16)
    wbuf = k.sbuf("wbufM", [128, 2 * 3 * 4096], BF16)
    hk = k.sbuf("hkM", [128, 2 * TB], BF16)
    cin = k.sbuf("cin", [128, 1027 + 1], BF16)
    tails = k.sbuf("tails", [128, 30], BF16)
    acc_f = k.sbuf("acc_f", [128, TB], F32)
    convw_sb = k.sbuf("convw_sb", [128, 40], F32)
    convb_sb = k.sbuf("convb_sb", [128, 10], F32)
    dtb_sb = k.sbuf("dtb_sb", [128, 1], F32)
    acol_sb = k.sbuf("acol_sb", [128, 2], F32)
    dtf = k.sbuf("dtf", [128, TB], F32)
    dtp_f = k.sbuf("dtp_f", [128, TB], F32)
    adt_f = k.sbuf("adt_f", [128, TB], F32)
    dp_sb = k.sbuf("dp_sb", [1, 16], F32)
    dpbc_sb = k.sbuf("dpbc_sb", [128, 16], F32)
    gnw_sb = k.sbuf("gnw_sb", [128, 8], F32)
    triu_sb = k.sbuf("triu_sb", [128, 128], F32)
    mtril_sb = k.sbuf("mtril_sb", [128, 128], F32)
    idnf_sb = k.sbuf("idnf_sb", [128, 128], F32)
    idnb_sb = k.sbuf("idnb_sb", [128, 128], BF16)
    adtT_sb = k.sbuf("adtT_sb", [128, 16], F32)
    dtpT_sb = k.sbuf("dtpT_sb", [128, 16], F32)
    acsT_sb = k.sbuf("acsT_sb", [128, 16], F32)
    nacsT_sb = k.sbuf("nacsT_sb", [128, 16], F32)
    acsbc_sb = k.sbuf("acsbc_sb", [128, 16], F32)
    dsT_sb = k.sbuf("dsT_sb", [128, 16], F32)
    eAT_sb = k.sbuf("eAT_sb", [128, 16], F32)
    egbc_sb = k.sbuf("egbc_sb", [128, 16], F32)
    acsl_sb = k.sbuf("acsl_sb", [128, 1], F32)
    rowh_sb = k.sbuf("rowh_sb", [1, 128], F32)
    junk16 = k.sbuf("junk16", [128, 128], F32)
    MT_b = k.sbuf("MT_b", [128, 128], BF16)
    LT_b = k.sbuf("LT_b", [128, 128], BF16)
    MLT_b = k.sbuf("MLT_b", [128, 128], BF16)
    BT_sb = k.sbuf("BT_sb", [128, 128], BF16)
    BTds_b = k.sbuf("BTds_b", [128, 128], BF16)
    hidT_b = k.sbuf("hidT_b", [128, 64], BF16)
    hdtT_b = k.sbuf("hdtT_b", [128, 64], BF16)
    ys1_b = k.sbuf("ys1_b", [128, 64], BF16)
    yT_b = k.sbuf("yT_b", [128, 64], BF16)
    prev_f = k.sbuf("prev_f", [128, NH * 64], F32)
    prev_b = k.sbuf("prev_b", [128, NH * 64], BF16)
    y_asm = k.sbuf("y_asm", [128, 8 * TB], BF16)
    grow = k.sbuf("grow", [1, 2 * TB], F32)
    gbc_b = k.sbuf("gbc_b", [128, TB], BF16)
    gsq_b = k.sbuf("gsq_b", [128, TB], BF16)
    yst_b = k.sbuf("yst_b", [128, 2 * TB], BF16)
    gsil_b = k.sbuf("gsil_b", [128, 2 * TB], BF16)
    ones_bf = k.sbuf("ones_bfM", [128, 1], BF16)
    onesr_f = k.sbuf("onesr_fM", [1, 128], F32)

    g_ms = grow[0:1, 0:TB]
    g_tmp = grow[0:1, TB:2 * TB]

    psP = k.psum("psP", [128, 2048], F32)
    psQ2 = k.psum("psQ2", [128, 1024], F32)
    psR2 = k.psum("psR2", [128, 512], F32)
    psT_b = k.psum("psT_bM", [128, 1024], BF16)
    ps_bc = psP[:, 0:128]
    ps_mt = psP[:, 512:640]
    ps_yd = psP[:, 1024:1152]
    ps_yo = psP[:, 1536:1664]
    ps_s = psQ2[:, 0:128]
    ps_m16 = psQ2[:, 512:640]
    ps_row = psR2[0:1, 0:128]
    ps_gs = psQ2[0:1, 0:1024]
    ps_gb = psP[:, 0:1024]

    s_pe = k.sem("s_pe")
    s_dv = k.sem("s_dv")
    s_ac = k.sem("s_ac")
    s_dq = k.sem("s_dq")
    s_aq = k.sem("s_aq")
    s_in = k.sem("s_in")
    stPre = Stream(k, "stPre", depth=1)
    stWg = Stream(k, "stWg", depth=6)
    stHk = Stream(k, "stHk", depth=2)
    stYo = Stream(k, "stYo", depth=2)
    block = k.block()

    PE_TB = 224 + 696 + 9
    DV_TB = 7 + 10 + 1 + 696 + 8 + 10
    AC_TB = 10 + 1 + 136 + 8 + 9
    PE0, DV0, AC0 = 1, 2, 1

    def peB(tb):
        return PE0 + tb * PE_TB

    def dvB(tb):
        return DV0 + tb * DV_TB

    def acB(tb):
        return AC0 + tb * AC_TB

    wg_idx = {}
    _ct = 0
    for _tb in range(NTB):
        for _gi, (_m0, _gl) in enumerate(GRPS):
            for _mm in range(_gl):
                wg_idx[(_tb, _gi, _mm)] = _ct
                _ct += 1

    def peD(tb, c8):
        return peB(tb) + 224 + c8 * 87

    def dvD(tb, c8):
        return dvB(tb) + 18 + c8 * 87

    def acD(tb, c8):
        return acB(tb) + 11 + c8 * 17

    @block.sync
    def _(eng):
        stPre.tick(eng.dma_start(out=csb[:], in_=cdram[:]))
        stPre.tick(eng.dma_start(out=convw_sb[:], in_=convw[:]))
        stPre.tick(eng.dma_start(out=convb_sb[:], in_=convb[:]))
        stPre.tick(eng.dma_start(out=dtb_sb[:], in_=dtb[:]))
        stPre.tick(eng.dma_start(out=acol_sb[:, 0:1], in_=alog[:]))
        stPre.tick(eng.dma_start(out=dp_sb[:], in_=dp_row[:]))
        stPre.tick(eng.dma_start(out=gnw_sb[:], in_=gnw[:]))
        stPre.tick(eng.dma_start(out=triu_sb[:], in_=triu[:]))
        stPre.tick(eng.dma_start(out=mtril_sb[:], in_=mtril[:]))
        stPre.tick(eng.dma_start(out=idnf_sb[:], in_=idn_f[:]))
        stPre.tick(eng.dma_start(out=idnb_sb[:], in_=idn_b[:]))

    @block.gpsimd
    def _(eng):
        eng.memset(ones_bf[:], 1.0)
        eng.memset(onesr_f[:], 1.0).then_inc(s_in.h, 1)
        s_in.n += 1

    @block.scalar
    def _(eng):
        stPre.wait_all(eng)
        eng.activation(out=acol_sb[:, 1:2], in_=acol_sb[:, 0:1],
                       func=AF.Exp).then_inc(s_ac.h, 1)
        s_ac.n += 1

    @block.tensor
    def _(eng):
        stPre.wait_all(eng)
        eng.wait_ge(s_in.h, 1)
        eng.matmul(ps_m16[:, 0:16], onesr_f[:1, :], dp_sb[:1, :],
                   start=True, stop=True).then_inc(s_pe.h, 1)
        s_pe.n += 1

    @block.vector
    def _(eng):
        eng.wait_ge(s_pe.h, 1)
        eng.wait_ge(s_ac.h, 1)
        seq(eng, s_dq, eng.tensor_copy(dpbc_sb[:], ps_m16[:, 0:16]))
        eng.nop().then_inc(s_dv.h, 1)
        s_dv.n += 1
        seq(eng, s_dq, eng.tensor_scalar(out=acol_sb[:, 1:2],
                                         in0=acol_sb[:, 1:2], scalar1=-1.0,
                                         scalar2=None, op0=ALU.mult))
        seq(eng, s_dq, eng.memset(prev_f[:], 0.0))
        seq(eng, s_dq, eng.memset(prev_b[:], 0.0))
        seq(eng, s_dq, eng.memset(tails[:], 0.0))
        eng.nop().then_inc(s_dv.h, 1)
        s_dv.n += 1
    assert (s_pe.n, s_dv.n, s_ac.n) == (PE0, DV0, AC0)

    @block.gpsimd
    def _(eng):
        for tb in range(NTB):
            for gi, (m0, gl) in enumerate(GRPS):
                u = tb * NG + gi
                if u >= 2:
                    tb2, g2 = divmod(u - 2, NG)
                    eng.wait_ge(s_pe.h, peB(tb2) + (g2 + 1) * 32)
                for mm in range(gl):
                    stWg.tick(eng.dma_start(
                        out=wbuf[:, (u % 2) * 12288 + mm * 4096:
                                 (u % 2) * 12288 + (mm + 1) * 4096],
                        in_=wg[m0 + mm]))

    @block.sync
    def _(eng):
        for tb in range(NTB):
            for gi in range(NG):
                for kk in range(32):
                    i = (tb * NG + gi) * 32 + kk
                    if i >= 2:
                        tb2, r2 = divmod(i - 2, NG * 32)
                        eng.wait_ge(s_pe.h, peB(tb2) + r2 + 1)
                    stHk.tick(eng.dma_start(
                        out=hk[:, (i % 2) * TB:(i % 2) * TB + TB],
                        in_=h_in[:, kk * 8192 + tb * TB:
                                 kk * 8192 + (tb + 1) * TB]))

    @block.tensor
    def _(eng):
        for tb in range(NTB):
            bnew = (tb % 4 == 0)
            for gi, (m0, gl) in enumerate(GRPS):
                u = tb * NG + gi
                for mm in range(gl):
                    stWg.wait(eng, wg_idx[(tb, gi, mm)])
                if gi == 0:
                    if tb >= 1:
                        eng.wait_ge(s_dv.h, dvB(tb - 1) + DV_TB)
                    else:
                        eng.wait_ge(s_dv.h, DV0)
                else:
                    eng.wait_ge(s_dv.h, dvB(tb) + gi)
                for kk in range(32):
                    i = u * 32 + kk
                    stHk.wait(eng, i)
                    last = None
                    for mm in range(gl):
                        pv = (psP[:, 0:1024] if mm == 0 else
                              psP[:, 1024:2048] if mm == 1 else psQ2[:, :])
                        for half in range(2):
                            last = eng.matmul(
                                pv[:, half * 512:half * 512 + 512],
                                wbuf[:, (u % 2) * 12288 + mm * 4096 +
                                     kk * 128:(u % 2) * 12288 + mm * 4096 +
                                     (kk + 1) * 128],
                                hk[:, (i % 2) * TB + half * 512:
                                   (i % 2) * TB + half * 512 + 512],
                                start=(kk == 0), stop=(kk == 31))
                    last.then_inc(s_pe.h, 1)
                    s_pe.n += 1
            for c8 in range(8):
                pe_base = peD(tb, c8)
                dv_base = dvD(tb, c8)
                ac_base = acD(tb, c8)
                t0 = c8 * 128
                Bt = proj[:, 8 * TB + t0:8 * TB + t0 + 128]
                Ct = proj[:, 9 * TB + t0:9 * TB + t0 + 128]
                eng.wait_ge(s_dv.h, dvB(tb) + NG + 10 + 1)
                if tb + c8 > 0:
                    eng.wait_ge(s_dv.h, dv_base)
                eng.transpose(ps_m16[:, 0:16], adt_f[0:16, t0:t0 + 128],
                              idnf_sb[0:16, 0:16]).then_inc(s_pe.h, 1)
                s_pe.n += 1
                eng.wait_ge(s_dv.h, dv_base + 1)
                eng.transpose(ps_m16[:, 0:16], dtp_f[0:16, t0:t0 + 128],
                              idnf_sb[0:16, 0:16]).then_inc(s_pe.h, 1)
                s_pe.n += 1
                eng.wait_ge(s_dv.h, dv_base + 2)
                eng.matmul(ps_m16[:, 0:16], triu_sb[:], adtT_sb[:, 0:16],
                           start=True, stop=True).then_inc(s_pe.h, 1)
                s_pe.n += 1
                eng.wait_ge(s_dv.h, dv_base + 3)
                eng.transpose(ps_row[0:1, 0:16], acsl_sb[0:16, 0:1],
                              idnf_sb[0:16, 0:16]).then_inc(s_pe.h, 1)
                s_pe.n += 1
                eng.wait_ge(s_dv.h, dv_base + 4)
                eng.matmul(ps_m16[:, 0:16], onesr_f[:1, :],
                           rowh_sb[:1, 0:16],
                           start=True, stop=True).then_inc(s_pe.h, 1)
                s_pe.n += 1
                eng.wait_ge(s_dv.h, dv_base + 5)
                eng.matmul(ps_mt[:], Bt, Ct,
                           start=True, stop=True).then_inc(s_pe.h, 1)
                s_pe.n += 1
                eng.wait_ge(s_dv.h, dv_base + 6)
                eng.transpose(psT_b[:, 0:128], Bt,
                              idnb_sb[:]).then_inc(s_pe.h, 1)
                s_pe.n += 1
                for h in range(NH):
                    hb = dv_base + 7 + h * 5
                    pb = pe_base + 7 + h * 5
                    eng.wait_ge(s_dv.h, dv_base + 7)
                    if h >= 1:
                        eng.wait_ge(s_dv.h, hb - 4)
                    eng.matmul(ps_row[0:1, :], adtT_sb[:, h:h + 1],
                               triu_sb[:],
                               start=True, stop=True).then_inc(s_pe.h, 1)
                    s_pe.n += 1
                    eng.wait_ge(s_dv.h, hb + 1)
                    if h >= 1:
                        eng.wait_ge(s_ac.h, ac_base + 1 + h)
                    eng.matmul(ps_bc[:], onesr_f[:1, :], rowh_sb[:1, :],
                               start=True, stop=False)
                    eng.matmul(ps_bc[:], idnf_sb[:], mtril_sb[:],
                               start=False, stop=True).then_inc(s_pe.h, 1)
                    s_pe.n += 1
                    eng.wait_ge(s_dv.h, hb + 2)
                    mt = h // 2
                    rb = (h % 2) * 64
                    eng.transpose(psT_b[:, 256:320],
                                  proj[rb:rb + 64, mt * TB + t0:
                                       mt * TB + t0 + 128],
                                  idnb_sb[rb:rb + 64, rb:rb + 64]
                                  ).then_inc(s_pe.h, 1)
                    s_pe.n += 1
                    eng.wait_ge(s_dv.h, hb + 3)
                    eng.matmul(ps_yd[:, 0:64], MLT_b[:], hdtT_b[:],
                               start=True, stop=True)
                    eng.matmul(ps_yo[:, 0:64], Ct, prev_b[:, h * 64:
                                                          (h + 1) * 64],
                               start=True, stop=True)
                    eng.matmul(ps_s[:, 0:64], BTds_b[:], hdtT_b[:],
                               start=True, stop=True).then_inc(s_pe.h, 1)
                    s_pe.n += 1
                    eng.wait_ge(s_dv.h, hb + 4)
                    eng.transpose(psT_b[0:64, 512:640], yT_b[:],
                                  idnb_sb[:]).then_inc(s_pe.h, 1)
                    s_pe.n += 1
            for mt in range(8):
                eng.wait_ge(s_ac.h, acB(tb) + 11 + 136 + 8 + mt + 1)
                last = None
                for half in range(2):
                    sl = slice(half * 512, half * 512 + 512)
                    last = eng.matmul(ps_gs[:, sl], ones_bf[:, 0:1],
                                      gsq_b[:, sl], start=(mt == 0),
                                      stop=(mt == 7))
                last.then_inc(s_pe.h, 1)
                s_pe.n += 1
            eng.wait_ge(s_dv.h, dvB(tb) + 18 + 696 + 8 + 1)
            eng.wait_ge(s_ac.h, acB(tb) + 11 + 136 + 8 + 9)
            eng.matmul(ps_gb[:, 0:512], onesr_f[:1, :], g_ms[:, 0:512],
                       start=True, stop=True)
            eng.matmul(ps_gb[:, 512:1024], onesr_f[:1, :], g_ms[:, 512:1024],
                       start=True, stop=True).then_inc(s_pe.h, 1)
            s_pe.n += 1
    assert s_pe.n == PE0 + NTB * PE_TB, s_pe.n

    @block.vector
    def _(eng):
        for tb in range(NTB):
            if tb == 4:
                seq(eng, s_dq, eng.memset(prev_f[:], 0.0))
                seq(eng, s_dq, eng.memset(prev_b[:], 0.0))
                seq(eng, s_dq, eng.memset(tails[:], 0.0))
            for gi, (m0, gl) in enumerate(GRPS):
                eng.wait_ge(s_pe.h, peB(tb) + (gi + 1) * 32)
                for mm in range(gl):
                    pv = (psP[:, 0:1024] if mm == 0 else
                          psP[:, 1024:2048] if mm == 1 else psQ2[:, :])
                    m = m0 + mm
                    seq(eng, s_dq, eng.tensor_copy(
                        proj[:, m * TB:(m + 1) * TB], pv[:]))
                eng.nop().then_inc(s_dv.h, 1)
                s_dv.n += 1
            for ct in range(10):
                if ct >= 1:
                    eng.wait_ge(s_ac.h, acB(tb) + ct)
                seq(eng, s_dq, eng.tensor_copy(cin[:, 0:3],
                                               tails[:, ct * 3:ct * 3 + 3]))
                seq(eng, s_dq, eng.tensor_copy(cin[:, 3:1027],
                                               proj[:, ct * TB:(ct + 1) * TB]))
                seq(eng, s_dq, eng.tensor_copy(tails[:, ct * 3:ct * 3 + 3],
                                               cin[:, 1024:1027]))
                seq(eng, s_dq, eng.tensor_scalar(
                    out=acc_f[:], in0=cin[:, 0:1024],
                    scalar1=convw_sb[:, ct * 4:ct * 4 + 1], scalar2=None,
                    op0=ALU.mult))
                for tap in range(1, 4):
                    seq(eng, s_dq, eng.scalar_tensor_tensor(
                        out=acc_f[:], in0=cin[:, tap:tap + 1024],
                        scalar=convw_sb[:, ct * 4 + tap:ct * 4 + tap + 1],
                        in1=acc_f[:], op0=ALU.mult, op1=ALU.add))
                eng.nop().then_inc(s_dv.h, 1)
                s_dv.n += 1
            eng.wait_ge(s_ac.h, acB(tb) + 11)
            seq(eng, s_dq, eng.tensor_scalar(
                out=adt_f[0:16, :], in0=dtp_f[0:16, :],
                scalar1=acol_sb[0:16, 1:2], scalar2=None, op0=ALU.mult))
            eng.nop().then_inc(s_dv.h, 1)
            s_dv.n += 1
            for c8 in range(8):
                dv_base = dvD(tb, c8)
                pe_base = peD(tb, c8)
                ac_base = acD(tb, c8)
                t0 = c8 * 128
                eng.wait_ge(s_pe.h, pe_base + 1)
                seq(eng, s_dq, eng.tensor_copy(adtT_sb[:, 0:16],
                                               ps_m16[:, 0:16]))
                eng.nop().then_inc(s_dv.h, 1)
                s_dv.n += 1
                eng.wait_ge(s_pe.h, pe_base + 2)
                seq(eng, s_dq, eng.tensor_copy(dtpT_sb[:, 0:16],
                                               ps_m16[:, 0:16]))
                eng.nop().then_inc(s_dv.h, 1)
                s_dv.n += 1
                eng.wait_ge(s_pe.h, pe_base + 3)
                seq(eng, s_dq, eng.tensor_copy(acsT_sb[:, 0:16],
                                               ps_m16[:, 0:16]))
                seq(eng, s_dq, eng.tensor_scalar(
                    out=nacsT_sb[:, 0:16], in0=acsT_sb[:, 0:16], scalar1=-1.0,
                    scalar2=None, op0=ALU.mult))
                seq(eng, s_dq, eng.tensor_scalar(
                    out=junk16[0:16, :], in0=adt_f[0:16, t0:t0 + 128],
                    scalar1=1.0, scalar2=0.0, op0=ALU.mult, op1=ALU.add,
                    accum_out=acsl_sb[0:16, 0:1]))
                eng.nop().then_inc(s_dv.h, 1)
                s_dv.n += 1
                eng.wait_ge(s_pe.h, pe_base + 4)
                seq(eng, s_dq, eng.tensor_copy(rowh_sb[0:1, 0:16],
                                               ps_row[0:1, 0:16]))
                eng.nop().then_inc(s_dv.h, 1)
                s_dv.n += 1
                eng.wait_ge(s_pe.h, pe_base + 5)
                seq(eng, s_dq, eng.tensor_copy(acsbc_sb[:, 0:16],
                                               ps_m16[:, 0:16]))
                seq(eng, s_dq, eng.tensor_tensor(
                    out=junk16[:, 0:16], in0=acsbc_sb[:, 0:16],
                    in1=acsT_sb[:, 0:16], op=ALU.subtract))
                eng.nop().then_inc(s_dv.h, 1)
                s_dv.n += 1
                eng.wait_ge(s_pe.h, pe_base + 6)
                seq(eng, s_dq, eng.tensor_copy(MT_b[:], ps_mt[:]))
                eng.nop().then_inc(s_dv.h, 1)
                s_dv.n += 1
                eng.wait_ge(s_pe.h, pe_base + 7)
                seq(eng, s_dq, eng.tensor_copy(BT_sb[:], psT_b[:, 0:128]))
                eng.nop().then_inc(s_dv.h, 1)
                s_dv.n += 1
                for h in range(NH):
                    hb = dv_base + 7 + h * 5
                    pb = pe_base + 7 + h * 5
                    eng.wait_ge(s_pe.h, pb + 1)
                    seq(eng, s_dq, eng.tensor_copy(rowh_sb[:], ps_row[:]))
                    eng.nop().then_inc(s_dv.h, 1)
                    s_dv.n += 1
                    eng.wait_ge(s_ac.h, ac_base + 1 + h + 1)
                    seq(eng, s_dq, eng.tensor_tensor(
                        out=MLT_b[:], in0=MT_b[:], in1=LT_b[:], op=ALU.mult))
                    eng.nop().then_inc(s_dv.h, 1)
                    s_dv.n += 1
                    eng.wait_ge(s_pe.h, pb + 3)
                    seq(eng, s_dq, eng.tensor_copy(hidT_b[:],
                                                   psT_b[:, 256:320]))
                    seq(eng, s_dq, eng.tensor_scalar(
                        out=hdtT_b[:], in0=hidT_b[:],
                        scalar1=dtpT_sb[:, h:h + 1], scalar2=None,
                        op0=ALU.mult))
                    seq(eng, s_dq, eng.tensor_scalar(
                        out=BTds_b[:], in0=BT_sb[:],
                        scalar1=dsT_sb[:, h:h + 1], scalar2=None,
                        op0=ALU.mult))
                    eng.nop().then_inc(s_dv.h, 1)
                    s_dv.n += 1
                    eng.wait_ge(s_pe.h, pb + 4)
                    seq(eng, s_dq, eng.scalar_tensor_tensor(
                        out=ys1_b[:], in0=hidT_b[:],
                        scalar=dpbc_sb[:, h:h + 1], in1=ps_yd[:, 0:64],
                        op0=ALU.mult, op1=ALU.add))
                    seq(eng, s_dq, eng.scalar_tensor_tensor(
                        out=yT_b[:], in0=ps_yo[:, 0:64],
                        scalar=eAT_sb[:, h:h + 1], in1=ys1_b[:],
                        op0=ALU.mult, op1=ALU.add))
                    seq(eng, s_dq, eng.scalar_tensor_tensor(
                        out=prev_f[:, h * 64:(h + 1) * 64],
                        in0=prev_f[:, h * 64:(h + 1) * 64],
                        scalar=egbc_sb[:, h:h + 1], in1=ps_s[:, 0:64],
                        op0=ALU.mult, op1=ALU.add))
                    seq(eng, s_dq, eng.tensor_copy(
                        prev_b[:, h * 64:(h + 1) * 64],
                        prev_f[:, h * 64:(h + 1) * 64]))
                    eng.nop().then_inc(s_dv.h, 1)
                    s_dv.n += 1
                    eng.wait_ge(s_pe.h, pb + 5)
                    mt = h // 2
                    rb = (h % 2) * 64
                    seq(eng, s_dq, eng.tensor_copy(
                        y_asm[rb:rb + 64, mt * TB + t0:mt * TB + t0 + 128],
                        psT_b[0:64, 512:640]))
                    eng.nop().then_inc(s_dv.h, 1)
                    s_dv.n += 1
            for mt in range(8):
                eng.wait_ge(s_ac.h, acB(tb) + 11 + 136 + mt + 1)
                seq(eng, s_dq, eng.tensor_tensor(
                    out=y_asm[:, mt * TB:(mt + 1) * TB],
                    in0=y_asm[:, mt * TB:(mt + 1) * TB],
                    in1=gsil_b[:, (mt % 2) * TB:(mt % 2) * TB + TB],
                    op=ALU.mult))
                eng.nop().then_inc(s_dv.h, 1)
                s_dv.n += 1
            eng.wait_ge(s_pe.h, peB(tb) + 224 + 696 + 8)
            seq(eng, s_dq, eng.tensor_scalar(
                out=g_ms[:], in0=ps_gs[:], scalar1=1.0 / 1024.0,
                scalar2=None, op0=ALU.mult))
            eng.nop().then_inc(s_dv.h, 1)
            s_dv.n += 1
            eng.wait_ge(s_pe.h, peB(tb) + PE_TB)
            seq(eng, s_dq, eng.tensor_copy(gbc_b[:], ps_gb[:, 0:1024]))
            eng.nop().then_inc(s_dv.h, 1)
            s_dv.n += 1
            for mt in range(8):
                stYo.wait(eng, tb * 8 + mt - 2)
                seq(eng, s_dq, eng.scalar_tensor_tensor(
                    out=yst_b[:, (mt % 2) * TB:(mt % 2) * TB + TB],
                    in0=y_asm[:, mt * TB:(mt + 1) * TB],
                    scalar=gnw_sb[:, mt:mt + 1], in1=gbc_b[:],
                    op0=ALU.mult, op1=ALU.mult))
                eng.nop().then_inc(s_dv.h, 1)
                s_dv.n += 1
    assert s_dv.n == DV0 + NTB * DV_TB, s_dv.n

    @block.scalar
    def _(eng):
        for tb in range(NTB):
            for ct in range(10):
                eng.wait_ge(s_dv.h, dvB(tb) + NG + ct + 1)
                eng.activation(out=proj[:, ct * TB:(ct + 1) * TB],
                               in_=acc_f[:], func=AF.Silu,
                               bias=convb_sb[:, ct:ct + 1]).then_inc(s_ac.h, 1)
                s_ac.n += 1
            seq(eng, s_aq, eng.activation(
                out=dtf[0:16, :], in_=proj[0:16, 10 * TB:11 * TB],
                func=AF.Exp, bias=dtb_sb[0:16, 0:1]))
            seq(eng, s_aq, eng.activation(
                out=dtp_f[0:16, :], in_=dtf[0:16, :], func=AF.Ln, bias=1.0))
            eng.nop().then_inc(s_ac.h, 1)
            s_ac.n += 1
            for c8 in range(8):
                dv_base = dvD(tb, c8)
                ac_base = acD(tb, c8)
                pe_base = peD(tb, c8)
                eng.wait_ge(s_dv.h, dv_base + 5)
                seq(eng, s_aq, eng.activation(out=dsT_sb[:, 0:16],
                                              in_=junk16[:, 0:16],
                                              func=AF.Exp))
                seq(eng, s_aq, eng.activation(out=eAT_sb[:, 0:16],
                                              in_=acsT_sb[:, 0:16],
                                              func=AF.Exp))
                seq(eng, s_aq, eng.activation(out=egbc_sb[:, 0:16],
                                              in_=acsbc_sb[:, 0:16],
                                              func=AF.Exp))
                eng.nop().then_inc(s_ac.h, 1)
                s_ac.n += 1
                for h in range(NH):
                    pb = pe_base + 7 + h * 5
                    eng.wait_ge(s_pe.h, pb + 2)
                    eng.activation(out=LT_b[:], in_=ps_bc[:], func=AF.Exp,
                                   bias=nacsT_sb[:, h:h + 1]
                                   ).then_inc(s_ac.h, 1)
                    s_ac.n += 1
            for mt in range(8):
                eng.wait_ge(s_dv.h, dvB(tb) + 18 + 696)
                if mt >= 2:
                    eng.wait_ge(s_dv.h, dvB(tb) + 18 + 696 + mt - 1)
                eng.activation(out=gsil_b[:, (mt % 2) * TB:(mt % 2) * TB + TB],
                               in_=proj[:, (11 + mt) * TB:(12 + mt) * TB],
                               func=AF.Silu).then_inc(s_ac.h, 1)
                s_ac.n += 1
            for mt in range(8):
                eng.wait_ge(s_dv.h, dvB(tb) + 18 + 696 + mt + 1)
                if mt >= 1:
                    eng.wait_ge(s_pe.h, peB(tb) + 224 + 696 + mt)
                eng.activation(out=gsq_b[:], in_=y_asm[:, mt * TB:(mt + 1) * TB],
                               func=AF.Square).then_inc(s_ac.h, 1)
                s_ac.n += 1
            eng.wait_ge(s_dv.h, dvB(tb) + 18 + 696 + 8 + 1)
            seq(eng, s_aq, eng.activation(out=g_tmp[:], in_=g_ms[:],
                                          func=AF.Ln, bias=EPS))
            eng.activation(out=g_ms[:], in_=g_tmp[:], func=AF.Exp,
                           scale=-0.5).then_inc(s_ac.h, 1)
            s_ac.n += 1
            for mt in range(8):
                eng.wait_ge(s_dv.h, dvB(tb) + 18 + 696 + 8 + 2 + mt + 1)
                stYo.tick(eng.dma_start(
                    out=y_out[mt * 128:(mt + 1) * 128,
                              tb * TB:(tb + 1) * TB],
                    in_=yst_b[:, (mt % 2) * TB:(mt % 2) * TB + TB]))
        stYo.wait_all(eng)
    assert s_ac.n == AC0 + NTB * AC_TB, s_ac.n

    return k


from concourse.bass_utils import run_bass_kernel_spmd

NCORES = 8


def build_Q(n=4096):
    t = np.arange(n)
    Q = np.zeros((n, n), np.float64)
    Q[:, 0] = 1.0 / np.sqrt(n)
    f = np.arange(1, n // 2)
    ang = 2 * np.pi * np.outer(t, f) / n
    Q[:, 2 * f - 1] = np.sqrt(2.0 / n) * np.cos(ang)
    Q[:, 2 * f] = np.sqrt(2.0 / n) * np.sin(ang)
    Q[:, n - 1] = ((-1.0) ** t) / np.sqrt(n)
    return Q.astype(np.float32)


def _run(build_fn, in_maps):
    k = build_fn()
    res = run_bass_kernel_spmd(k.nc, in_maps, core_ids=list(range(NCORES)))
    return res.results


def kernel(**inp):
    f32 = np.float32
    g = lambda n: np.ascontiguousarray(np.asarray(inp[n], f32))
    x = g("L0_attn_x")
    x_fm = np.ascontiguousarray(x.reshape(8192, 4096).T)
    xs = [np.ascontiguousarray(x_fm[:, c * 1024:(c + 1) * 1024])
          for c in range(NCORES)]

    nw = np.ascontiguousarray(g("L0_attn_norm_w").reshape(32, 128).T)
    maps = [{"x_in": xs[c], "nw_in": nw} for c in range(NCORES)]
    res = _run(lambda: build_l1a(1024), maps)
    h_full = np.concatenate([np.asarray(r["h_out"]) for r in res], axis=1)
    t_parts = [np.asarray(r["t_out"]) for r in res]
    h_tiled = np.ascontiguousarray(
        h_full.reshape(32, 128, 8192).transpose(1, 0, 2)).reshape(128, 32 * 8192)

    ipw = g("L0_attn_in_proj_w")
    cw = g("L0_attn_conv_w")
    cb = g("L0_attn_conv_b")
    dtbias = g("L0_attn_dt_bias")
    alog = g("L0_attn_A_log")
    Dp = g("L0_attn_Dp")
    gnw = g("L0_attn_gnorm_w")
    triu = (np.arange(128)[:, None] <= np.arange(128)[None, :]).astype(f32)
    mtrilc = np.where(np.arange(128)[:, None] <= np.arange(128)[None, :],
                      0.0, -1e30).astype(f32)
    idnf = np.eye(128, dtype=f32)
    idnb = np.eye(128).astype(ml_dtypes.bfloat16)
    pad = lambda a, n: np.pad(a, ((0, n - a.shape[0]),) +
                              ((0, 0),) * (a.ndim - 1))
    maps = []
    for gi in range(8):
        hid = ipw[8192 + gi * 1024:8192 + (gi + 1) * 1024]
        Bg = ipw[16384 + gi * 128:16384 + (gi + 1) * 128]
        Cg = ipw[17408 + gi * 128:17408 + (gi + 1) * 128]
        dtg = ipw[18432 + gi * 16:18432 + (gi + 1) * 16]
        gate = ipw[gi * 1024:(gi + 1) * 1024]
        wg_rows = np.concatenate([hid, Bg, Cg, pad(dtg, 128), gate], 0)
        cwg = np.concatenate([cw[gi * 1024:(gi + 1) * 1024],
                              cw[8192 + gi * 128:8192 + (gi + 1) * 128],
                              cw[9216 + gi * 128:9216 + (gi + 1) * 128]], 0)
        cbg = np.concatenate([cb[gi * 1024:(gi + 1) * 1024],
                              cb[8192 + gi * 128:8192 + (gi + 1) * 128],
                              cb[9216 + gi * 128:9216 + (gi + 1) * 128]], 0)
        maps.append({
            "h_in": h_tiled,
            "wg": tile_w(np.ascontiguousarray(wg_rows.T), 32, 19),
            "convw": np.ascontiguousarray(
                cwg.reshape(10, 128, 4).transpose(1, 0, 2).reshape(128, 40)),
            "convb": np.ascontiguousarray(cbg.reshape(10, 128).T),
            "dtb": pad(dtbias[gi * 16:(gi + 1) * 16].reshape(16, 1), 128),
            "alog": pad(alog[gi * 16:(gi + 1) * 16].reshape(16, 1), 128),
            "dp_row": np.ascontiguousarray(
                Dp[gi * 16:(gi + 1) * 16].reshape(1, 16)),
            "gnw": np.ascontiguousarray(
                gnw[gi * 1024:(gi + 1) * 1024].reshape(8, 128).T),
            "triu": triu, "mtril": mtrilc, "idn_f": idnf, "idn_b": idnb,
        })
    res = _run(build_l1b, maps)
    y_full = np.concatenate([np.asarray(r["y_out"]) for r in res], axis=0)

    opT = tile_w(np.ascontiguousarray(g("L0_attn_out_proj_w").T), 64, 32)
    wT2 = tile_w(np.ascontiguousarray(g("L0_ffn_W").T), 32, 32)
    bias2 = np.ascontiguousarray(g("L0_ffn_bias").reshape(32, 128).T)
    A2 = g("L0_ffn_lora_A").reshape(128, 4096)
    B2 = np.ascontiguousarray(
        g("L0_ffn_lora_B").transpose(0, 2, 1).reshape(128, 4096))
    aT2 = tile_a(np.ascontiguousarray(A2.T))
    coef2 = np.ascontiguousarray(
        np.repeat(g("L0_ffn_lora_weights"), 16).reshape(128, 1))
    maps = []
    for c in range(NCORES):
        ysl = np.ascontiguousarray(y_full[:, c * 1024:(c + 1) * 1024])
        y_t = np.ascontiguousarray(
            ysl.reshape(64, 128, 1024).transpose(1, 0, 2)).reshape(128, 64 * 1024)
        maps.append({"y_in": y_t, "x_in": xs[c], "tln_in": t_parts[c],
                     "opT": opT, "wT": wT2,
                     "bias_in": bias2, "aT": aT2, "bT": B2,
                     "coef_in": coef2})
    res = _run(lambda: build_l2(1024), maps)
    x2_full = np.concatenate([np.asarray(r["x2_out"]) for r in res], axis=1)
    x2s = [np.ascontiguousarray(x2_full[:, c * 1024:(c + 1) * 1024])
           for c in range(NCORES)]

    wq, wk, wv = g("L1_attn_wq"), g("L1_attn_wk"), g("L1_attn_wv")
    bq, bk, bv = g("L1_attn_bq"), g("L1_attn_bk"), g("L1_attn_bv")
    maps = []
    for c in range(NCORES):
        b = c // 4
        hb = 8 * (c % 4)
        r0, r1 = hb * 128, hb * 128 + 1024
        wqk_rows = np.concatenate([wq[r0:r1], wk[r0:r1]], 0)
        maps.append({
            "x_in": np.ascontiguousarray(x2_full[:, b * 4096:(b + 1) * 4096]),
            "wqk": tile_w(np.ascontiguousarray(wqk_rows.T), 32, 16),
            "wvT": np.ascontiguousarray(
                np.ascontiguousarray(wv[r0:r1].T).reshape(32, 128, 1024)),
            "bqk": np.ascontiguousarray(
                np.concatenate([bq[r0:r1], bk[r0:r1]]).reshape(16, 128).T),
            "bv_row": np.ascontiguousarray(bv[r0:r1].reshape(1, 1024)),
            "idn_in": idnb,
        })
    res = _run(build_l3, maps)
    ctx_full = [np.zeros((4096, 4096), ml_dtypes.bfloat16) for _ in range(2)]
    for c in range(NCORES):
        b = c // 4
        ctx_full[b][(c % 4) * 1024:(c % 4 + 1) * 1024, :] = \
            np.asarray(res[c]["ctx_out"])

    Q = build_Q(4096)
    QT4 = np.ascontiguousarray(Q.T)
    qt = np.ascontiguousarray(
        QT4.reshape(32, 128, 8, 512).transpose(2, 1, 0, 3)
    ).reshape(8, 128, 32 * 512).astype(ml_dtypes.bfloat16)
    woT = tile_w(np.ascontiguousarray(g("L1_attn_wo").T), 32, 32)
    wbT4 = tile_w(np.ascontiguousarray(g("L1_ffn_W_base").T), 32, 32)
    bo4 = np.ascontiguousarray(g("L1_attn_bo").reshape(32, 128).T)
    bb4 = np.ascontiguousarray(g("L1_ffn_b_base").reshape(32, 128).T)
    encT = np.ascontiguousarray(
        np.concatenate([g("L1_ffn_enc0"), g("L1_ffn_enc1")], 0).T)
    c0 = g("L1_ffn_coefs0")
    c1 = g("L1_ffn_coefs1")
    m0 = np.asarray(inp["L1_ffn_mask0"]).astype(np.uint8)
    m1 = np.asarray(inp["L1_ffn_mask1"]).astype(np.uint8)
    c0t = np.stack([tile_a(np.ascontiguousarray(c0[e])) for e in range(3)])
    c1t = np.stack([tile_a(np.ascontiguousarray(c1[e])) for e in range(3)])
    m0t = np.stack([tile_a(np.ascontiguousarray(m0[e])) for e in range(3)])
    m1t = np.stack([tile_a(np.ascontiguousarray(m1[e])) for e in range(3)])
    encTt = tile_a(encT)
    maps = []
    for c in range(NCORES):
        b = c // 4
        csl = np.ascontiguousarray(
            ctx_full[b][:, (c % 4) * 1024:(c % 4 + 1) * 1024])
        ctx_t = np.ascontiguousarray(
            csl.reshape(32, 128, 1024).transpose(1, 0, 2)
        ).reshape(128, 32 * 1024)
        maps.append({"ctx_in": ctx_t, "x_in": x2s[c], "woT": woT,
                     "wbT": wbT4, "bo_in": bo4, "bb_in": bb4,
                     "encT_in": encTt, "qt_in": qt, "c0_in": c0t,
                     "c1_in": c1t, "m0_in": m0t, "m1_in": m1t})
    res = _run(lambda: build_l4(1024), maps)
    out_fm = np.concatenate(
        [np.asarray(r["x4_out"], dtype=np.float32) for r in res], axis=1)
    return np.ascontiguousarray(out_fm.T).reshape(2, 4096, 4096)


# version 07
# speedup vs baseline: 1.2112x; 1.2112x over previous
import numpy as np
import ml_dtypes
from contextlib import ExitStack
import concourse.bass as bass
import concourse.mybir as mybir

F32 = mybir.dt.float32
BF16 = mybir.dt.bfloat16
U8 = mybir.dt.uint8
AF = mybir.ActivationFunctionType
ALU = mybir.AluOpType
EPS = 1e-5


class SemC:
    def __init__(self, h):
        self.h = h
        self.n = 0

    def tick(self, instr, k=1):
        instr.then_inc(self.h, k)
        self.n += k
        return self.n


class K:
    def __init__(self):
        self.nc = bass.Bass("TRN2", target_bir_lowering=False)
        self.ctx = ExitStack()
        self.sems = {}
        self._uid = 0
        self._scopes = []

    def _u(self, name):
        self._uid += 1
        return f"{name}_u{self._uid}"

    def push(self):
        self._scopes.append(ExitStack())

    def pop(self):
        self._scopes.pop().close()

    @property
    def _cur(self):
        return self._scopes[-1] if self._scopes else self.ctx

    def sbuf(self, name, shape, dt=F32):
        return self._cur.enter_context(
            self.nc.sbuf_tensor(self._u(name), shape, dt))

    def psum(self, name, shape, dt=F32):
        return self._cur.enter_context(
            self.nc.psum_tensor(self._u(name), shape, dt))

    def sem(self, name):
        name = self._u(name)
        h = self.ctx.enter_context(self.nc.semaphore(name))
        s = SemC(h)
        self.sems[name] = s
        return s

    def din(self, name, shape, dt=F32):
        return self.nc.dram_tensor(name, shape, dt, kind="ExternalInput").ap()

    def dout(self, name, shape, dt=F32):
        return self.nc.dram_tensor(name, shape, dt, kind="ExternalOutput").ap()

    def dscratch(self, name, shape, dt=F32, addr_space="Local"):
        return self.nc.dram_tensor(name, shape, dt, kind="Internal",
                                   addr_space=addr_space).ap()

    def setup_consts(self, values):
        vals = sorted(set(float(v) for v in values))
        self.cvals = vals
        data = np.tile(np.asarray(vals, np.float32)[None, :], (128, 1))
        self.cdram = self.nc.inline_tensor(data, "const_bank").ap()
        self.csb = self.sbuf("const_bank_sb", [128, len(vals)], F32)
        for i, v in enumerate(vals):
            self.nc.const_aps.register(F32, v, self.csb[:, i:i + 1])
        return self.cdram, self.csb

    def ccol(self, v):
        i = self.cvals.index(float(v))
        return self.csb[:, i:i + 1]

    def block(self):
        return self.ctx.enter_context(self.nc.Block())


def bfcast(x):
    return np.ascontiguousarray(x).astype(ml_dtypes.bfloat16)


def bf2f(x):
    return np.asarray(x).astype(np.float32)


class Stream:
    def __init__(self, k, name, depth=2, inc=16):
        self.sems = [k.sem(f"{name}{j}") for j in range(depth)]
        self.inc = inc
        self.depth = depth
        self.count = 0

    def tick(self, instr):
        s = self.sems[self.count % self.depth]
        instr.then_inc(s.h, self.inc)
        s.n += self.inc
        self.count += 1

    def wait(self, eng, i):
        if i < 0:
            return
        eng.wait_ge(self.sems[i % self.depth].h,
                    self.inc * (i // self.depth + 1))

    def wait_all(self, eng):
        for s in self.sems:
            if s.n:
                eng.wait_ge(s.h, s.n)


def seq(eng, s, inst):
    inst.then_inc(s.h, 1)
    s.n += 1
    eng.wait_ge(s.h, s.n)

def tile_w(wT, NK, NM):
    import numpy as np
    K, M = wT.shape
    A = wT.reshape(NK, 128, NM, 128)
    return np.ascontiguousarray(A.transpose(2, 1, 0, 3)).reshape(NM, 128, NK * 128)

def tile_a(x):
    import numpy as np
    R = x.shape[0] // 128
    A = x.reshape(R, 128, x.shape[1])
    return np.ascontiguousarray(A.transpose(1, 0, 2)).reshape(128, R * x.shape[1])


"""Launch builders for the unfused checkpoint.

Conventions:
- feature-major activations: [channels(128-tiles) x tokens] in SBUF
- weights in HBM pre-transposed to [in x out] fp32, DMA-cast to bf16
- residual stream fp32 in HBM
- DMA completion tracking via klib.Stream (parity-split semaphores)
"""


def build_l2(T=1024, upto='full'):
    k = K()
    nc = k.nc
    NM, NK1, NK3 = 32, 64, 32

    y_in = k.din("y_in", [128, 64 * T], BF16)
    x_in = k.din("x_in", [4096, T], F32)
    tln_in = k.din("tln_in", [4096, T], BF16)
    opT = k.din("opT", [32, 128, 64 * 128], F32)
    wT = k.din("wT", [32, 128, 32 * 128], F32)
    bias_in = k.din("bias_in", [128, 32], F32)
    aT = k.din("aT", [128, 32 * 128], F32)
    bT = k.din("bT", [128, 4096], F32)
    coef_in = k.din("coef_in", [128, 1], F32)
    x2_out = k.dout("x2_out", [4096, T], F32)

    ao_scr = (k.dout("ao_scr", [4096, T], BF16) if upto == 'p1'
              else k.dscratch("ao_scr", [4096, T], BF16))
    x1_scr = (k.dout("x1_scr", [4096, T], F32) if upto == 'p2'
              else k.dscratch("x1_scr", [4096, T], F32))

    cdram, csb = k.setup_consts([EPS, 1.0 / 4096.0, -1.0, -0.5, 2.0])

    y_sb = k.sbuf("y_sb", [128, 64 * T], BF16)
    wbuf = k.sbuf("wbuf", [128, 2 * 64 * 128], BF16)
    ab = k.sbuf("ab", [128, 11 * T], BF16)
    af = k.sbuf("af", [128, 6 * T], F32)
    bias_sb = k.sbuf("bias_sb", [128, 32], F32)
    coef_sb = k.sbuf("coef_sb", [128, 2], F32)
    ones_bf = k.sbuf("ones_bf", [128, 1], BF16)
    ones_f = k.sbuf("ones_f", [128, 1], F32)
    onesr_f = k.sbuf("onesr_f", [1, 128], F32)

    ao_t = [ab[:, 0:T], ab[:, T:2 * T]]
    sq_t = ab[:, 2 * T:3 * T]
    t1_b = [ab[:, 3 * T:4 * T], ab[:, 4 * T:5 * T]]
    bc1_b = ab[:, 5 * T:6 * T]
    bc2_b = ab[:, 6 * T:7 * T]
    lh_bf = None
    a_sb = ab[:, 7 * T:11 * T]
    bT_sb = ab[:, 7 * T:11 * T]
    x_t = [af[:, 0:T], af[:, T:2 * T]]
    x1f = af[:, 2 * T:3 * T]
    r_mean = af[0:1, 3 * T:4 * T]
    r_rstd = af[0:1, 4 * T:5 * T]
    r_tmp = af[0:1, 5 * T:6 * T]

    psA = k.psum("psA", [128, T], F32)
    psB = k.psum("psB", [128, T], F32)
    ps_sum = k.psum("ps_sum", [1, T], F32)
    ps_sq = k.psum("ps_sq", [1, T], F32)

    s_pe = k.sem("s_pe")
    s_dq = k.sem("s_dq")
    s_aq = k.sem("s_aq")
    s_dv = k.sem("s_dv")
    s_ac = k.sem("s_ac")
    stPre = Stream(k, "stPre", depth=1)
    stPreG = Stream(k, "stPreG", depth=1)
    stW = Stream(k, "stW")
    stX = Stream(k, "stX")
    stAO = Stream(k, "stAO")
    stSao = Stream(k, "stSao")
    stSx1 = Stream(k, "stSx1")
    stOut = Stream(k, "stOut")
    stT = Stream(k, "stT")

    block = k.block()

    PE_P1, PE_R1 = 0, 64
    PE_P2, PE_R2 = 65, 97
    PE_LH, PE_P3 = 98, 99
    PE_R3, PE_END = 163, 164
    DV_P1, DV_R1 = 0, 32
    DV_P2, DV_R2 = 35, 67
    DV_H1, DV_LH = 70, 102
    DV_R3, DV_P4 = 103, 106
    DV_END = 138
    AC_P1, AC_R1 = 0, 32
    AC_P2, AC_R2 = 33, 65
    AC_P3, AC_R3 = 66, 130
    AC_END = 131

    def rows_ln(wait_pe):
        @block.vector
        def _(eng):
            eng.wait_ge(s_pe.h, wait_pe)
            seq(eng, s_dq, eng.tensor_scalar(
                out=r_mean[:], in0=ps_sum[:], scalar1=1.0 / 4096.0,
                scalar2=None, op0=ALU.mult))
            seq(eng, s_dq, eng.tensor_scalar(
                out=r_tmp[:], in0=ps_sq[:], scalar1=1.0 / 4096.0,
                scalar2=None, op0=ALU.mult))
            seq(eng, s_dq, eng.tensor_tensor(
                out=r_rstd[:], in0=r_mean[:], in1=r_mean[:], op=ALU.mult))
            eng.tensor_tensor(out=r_rstd[:], in0=r_tmp[:], in1=r_rstd[:],
                              op=ALU.subtract).then_inc(s_dv.h, 1)
            s_dv.n += 1
        dv1 = s_dv.n

        @block.scalar
        def _(eng):
            eng.wait_ge(s_dv.h, dv1)
            seq(eng, s_aq, eng.activation(out=r_tmp[:], in_=r_rstd[:],
                                          func=AF.Ln, bias=EPS))
            eng.activation(out=r_rstd[:], in_=r_tmp[:], func=AF.Exp,
                           scale=-0.5).then_inc(s_ac.h, 1)
            s_ac.n += 1
        ac1 = s_ac.n

        @block.vector
        def _(eng):
            eng.wait_ge(s_ac.h, ac1)
            seq(eng, s_dq, eng.tensor_tensor(out=r_tmp[:], in0=r_mean[:],
                                             in1=r_rstd[:], op=ALU.mult))
            eng.tensor_scalar(out=r_tmp[:], in0=r_tmp[:], scalar1=-1.0,
                              scalar2=None, op0=ALU.mult).then_inc(s_dv.h, 1)
            s_dv.n += 1
        dv2 = s_dv.n

        @block.tensor
        def _(eng):
            eng.wait_ge(s_dv.h, dv2)
            eng.matmul(psA[:, 0:512], onesr_f[:1, :], r_rstd[:, 0:512],
                       start=True, stop=True)
            eng.matmul(psA[:, 512:T], onesr_f[:1, :], r_rstd[:, 512:T],
                       start=True, stop=True)
            eng.matmul(psB[:, 0:512], onesr_f[:1, :], r_tmp[:, 0:512],
                       start=True, stop=True)
            eng.matmul(psB[:, 512:T], onesr_f[:1, :], r_tmp[:, 512:T],
                       start=True, stop=True).then_inc(s_pe.h, 1)
            s_pe.n += 1
        pe1 = s_pe.n

        @block.vector
        def _(eng):
            eng.wait_ge(s_pe.h, pe1)
            seq(eng, s_dq, eng.tensor_copy(bc1_b[:], psA[:]))
            seq(eng, s_dq, eng.tensor_copy(bc2_b[:], psB[:]))
            eng.nop().then_inc(s_dv.h, 1)
            s_dv.n += 1

    lh_bf = wbuf[:, 8192:8192 + T]
    tln_t = [bc1_b, bc2_b]
    w4 = wbuf.ap().rearrange("p (j q c) -> p j q c", j=2, c=128)
    w4b = wbuf[:, 0:8192].rearrange("p (j q c) -> p j q c", j=2, c=128)
    yv = y_sb.ap().rearrange("p (q t) -> p q t", t=T)
    h1 = y_sb[:, 0:32 * T]
    ffn = y_sb[:, 32 * T:64 * T]

    @block.sync
    def _(eng):
        stPre.tick(eng.dma_start(out=csb[:], in_=cdram[:]))
        stPre.tick(eng.dma_start(out=bias_sb[:], in_=bias_in[:]))
        stPre.tick(eng.dma_start(out=coef_sb[:, 0:1], in_=coef_in[:]))
        stPre.tick(eng.dma_start(out=y_sb[:], in_=y_in[:]))

    @block.gpsimd
    def _(eng):
        eng.memset(ones_bf[:], 1.0)
        eng.memset(ones_f[:], 1.0)
        eng.memset(onesr_f[:], 1.0)
        stPreG.tick(eng.dma_start(out=a_sb, in_=aT[:]))

    @block.sync
    def _(eng):
        for m in range(NM):
            if m >= 2:
                eng.wait_ge(s_dv.h, DV_P1 + m - 1)
            stT.tick(eng.dma_start(out=tln_t[m % 2][:],
                                   in_=tln_in[m * 128:(m + 1) * 128, :]))

    @block.gpsimd
    def _(eng):
        for m in range(NM):
            if m >= 2:
                eng.wait_ge(s_pe.h, PE_P1 + 2 * (m - 2) + 1)
            stW.tick(eng.dma_start(out=wbuf[:, (m % 2) * 8192:
                                            (m % 2) * 8192 + 8192],
                                   in_=opT[m]))

    @block.tensor
    def _(eng):
        stPre.wait_all(eng)
        stPreG.wait_all(eng)
        for m in range(NM):
            ps = psA if m % 2 == 0 else psB
            if m >= 1:
                eng.wait_ge(s_dv.h, DV_P1 + m)
                eng.wait_ge(s_ac.h, AC_P1 + m)
                last = None
                for half in range(2):
                    sl = slice(half * 512, half * 512 + 512)
                    eng.matmul(ps_sum[:, sl], ones_bf[:, 0:1],
                               ao_t[(m - 1) % 2][:, sl],
                               start=(m == 1), stop=False)
                    last = eng.matmul(ps_sq[:, sl], ones_bf[:, 0:1],
                                      sq_t[:, sl], start=(m == 1), stop=False)
                last.then_inc(s_pe.h, 1)
                s_pe.n += 1
            stW.wait(eng, m)
            if m >= 2:
                eng.wait_ge(s_dv.h, DV_P1 + m - 1)
                eng.wait_ge(s_ac.h, AC_P1 + m - 1)
            last = None
            for kk in range(NK1):
                for half in range(2):
                    sl = slice(half * 512, half * 512 + 512)
                    last = eng.matmul(ps[:, sl], w4[:, m % 2, kk],
                                      yv[:, kk, sl],
                                      start=(kk == 0), stop=(kk == NK1 - 1))
            last.then_inc(s_pe.h, 1)
            s_pe.n += 1
        eng.wait_ge(s_dv.h, DV_P1 + NM)
        eng.wait_ge(s_ac.h, AC_P1 + NM)
        last = None
        for half in range(2):
            sl = slice(half * 512, half * 512 + 512)
            eng.matmul(ps_sum[:, sl], ones_bf[:, 0:1],
                       ao_t[(NM - 1) % 2][:, sl], start=False, stop=True)
            last = eng.matmul(ps_sq[:, sl], ones_bf[:, 0:1], sq_t[:, sl],
                              start=False, stop=True)
        last.then_inc(s_pe.h, 1)
        s_pe.n += 1
    assert s_pe.n == PE_R1, s_pe.n

    @block.vector
    def _(eng):
        for m in range(NM):
            eng.wait_ge(s_pe.h, PE_P1 + 2 * m + 1)
            stT.wait(eng, m)
            stSao.wait(eng, m - 2)
            if m >= 2:
                eng.wait_ge(s_ac.h, AC_P1 + m - 1)
            ps = psA if m % 2 == 0 else psB
            eng.tensor_tensor(out=ao_t[m % 2][:], in0=ps[:],
                              in1=tln_t[m % 2][:],
                              op=ALU.add).then_inc(s_dv.h, 1)
            s_dv.n += 1
    assert s_dv.n == DV_R1

    @block.scalar
    def _(eng):
        for m in range(NM):
            eng.wait_ge(s_dv.h, DV_P1 + m + 1)
            eng.activation(out=sq_t[:], in_=ao_t[m % 2][:],
                           func=AF.Square).then_inc(s_ac.h, 1)
            s_ac.n += 1
            stSao.tick(eng.dma_start(out=ao_scr[m * 128:(m + 1) * 128, :],
                                     in_=ao_t[m % 2][:]))
    assert s_ac.n == AC_R1

    if upto == 'p1':
        @block.scalar
        def _(eng):
            stSao.wait_all(eng)
        return k

    rows_ln(PE_R1)
    assert (s_dv.n, s_ac.n, s_pe.n) == (DV_P2, AC_P2, PE_P2)

    @block.sync
    def _(eng):
        for m in range(NM):
            if m < 2:
                eng.wait_ge(s_pe.h, PE_R1)
                stSao.wait(eng, NM - 2 + m)
            else:
                eng.wait_ge(s_dv.h, DV_P2 + m - 1)
            stSao.wait(eng, m)
            stX.tick(eng.dma_start(out=x_t[m % 2][:],
                                   in_=x_in[m * 128:(m + 1) * 128, :]))
            stAO.tick(eng.dma_start(out=ao_t[m % 2][:],
                                    in_=ao_scr[m * 128:(m + 1) * 128, :]))

    @block.vector
    def _(eng):
        for m in range(NM):
            stX.wait(eng, m)
            stAO.wait(eng, m)
            if m >= 1:
                stSx1.wait(eng, m - 1)
                eng.wait_ge(s_ac.h, AC_P2 + m)
                eng.wait_ge(s_pe.h, PE_P2 + m)
            if m >= 2:
                eng.wait_ge(s_dv.h, DV_P2 + m - 1)
            seq(eng, s_dq, eng.tensor_tensor(out=t1_b[m % 2][:],
                                             in0=ao_t[m % 2][:],
                                             in1=bc1_b[:], op=ALU.mult))
            seq(eng, s_dq, eng.tensor_tensor(out=x1f[:], in0=x_t[m % 2][:],
                                             in1=t1_b[m % 2][:], op=ALU.add))
            eng.tensor_tensor(out=x1f[:], in0=x1f[:], in1=bc2_b[:],
                              op=ALU.add).then_inc(s_dv.h, 1)
            s_dv.n += 1
    assert s_dv.n == DV_R2

    @block.scalar
    def _(eng):
        for m in range(NM):
            eng.wait_ge(s_dv.h, DV_P2 + m + 1)
            if m >= 1:
                eng.wait_ge(s_pe.h, PE_P2 + m)
            eng.activation(out=sq_t[:], in_=x1f[:],
                           func=AF.Square).then_inc(s_ac.h, 1)
            s_ac.n += 1
            stSx1.tick(eng.dma_start(out=x1_scr[m * 128:(m + 1) * 128, :],
                                     in_=x1f[:]))
    assert s_ac.n == AC_R2

    @block.tensor
    def _(eng):
        for m in range(NM):
            eng.wait_ge(s_dv.h, DV_P2 + m + 1)
            eng.wait_ge(s_ac.h, AC_P2 + m + 1)
            last = None
            for half in range(2):
                sl = slice(half * 512, half * 512 + 512)
                eng.matmul(ps_sum[:, sl], ones_f[:, 0:1], x1f[:, sl],
                           start=(m == 0), stop=(m == NM - 1))
                last = eng.matmul(ps_sq[:, sl], ones_bf[:, 0:1], sq_t[:, sl],
                                  start=(m == 0), stop=(m == NM - 1))
            last.then_inc(s_pe.h, 1)
            s_pe.n += 1
    assert s_pe.n == PE_R2

    if upto == 'p2':
        @block.scalar
        def _(eng):
            stSx1.wait_all(eng)
        return k

    rows_ln(PE_R2)
    assert (s_dv.n, s_ac.n, s_pe.n) == (DV_H1, AC_P3, PE_LH)

    @block.sync
    def _(eng):
        for m in range(NM):
            stSx1.wait(eng, m)
            if m < 2:
                eng.wait_ge(s_dv.h, DV_R2)
            else:
                eng.wait_ge(s_dv.h, DV_H1 + m - 1)
            stX.tick(eng.dma_start(out=x_t[m % 2][:],
                                   in_=x1_scr[m * 128:(m + 1) * 128, :]))

    @block.vector
    def _(eng):
        for m in range(NM):
            stX.wait(eng, NM + m)
            if m >= 2:
                eng.wait_ge(s_dv.h, DV_H1 + m - 1)
            seq(eng, s_dq, eng.tensor_tensor(out=t1_b[m % 2][:],
                                             in0=x_t[m % 2][:],
                                             in1=bc1_b[:], op=ALU.mult))
            eng.tensor_tensor(out=h1[:, m * T:(m + 1) * T], in0=t1_b[m % 2][:],
                              in1=bc2_b[:], op=ALU.add).then_inc(s_dv.h, 1)
            s_dv.n += 1
    assert s_dv.n == DV_LH

    @block.tensor
    def _(eng):
        eng.wait_ge(s_dv.h, DV_LH)
        last = None
        for kk in range(NK3):
            for half in range(2):
                c0 = kk * T + half * 512
                last = eng.matmul(psA[:, half * 512:half * 512 + 512],
                                  a_sb[:, kk * 128:(kk + 1) * 128],
                                  h1[:, c0:c0 + 512],
                                  start=(kk == 0), stop=(kk == NK3 - 1))
        last.then_inc(s_pe.h, 1)
        s_pe.n += 1
    assert s_pe.n == PE_P3

    @block.vector
    def _(eng):
        eng.wait_ge(s_pe.h, PE_P3)
        seq(eng, s_dq, eng.tensor_scalar(out=coef_sb[:, 1:2],
                                         in0=coef_sb[:, 0:1], scalar1=2.0,
                                         scalar2=None, op0=ALU.mult))
        eng.tensor_scalar(out=lh_bf[:], in0=psA[:], scalar1=coef_sb[:, 1:2],
                          scalar2=None, op0=ALU.mult).then_inc(s_dv.h, 1)
        s_dv.n += 1
    assert s_dv.n == DV_R3

    @block.gpsimd
    def _(eng):
        eng.wait_ge(s_pe.h, PE_P3)
        stPreG.tick(eng.dma_start(out=bT_sb[:], in_=bT[:]))
        for m in range(NM):
            if m >= 2:
                eng.wait_ge(s_pe.h, PE_P3 + 2 * (m - 2) + 1)
            stW.tick(eng.dma_start(out=wbuf[:, (m % 2) * 4096:
                                            (m % 2) * 4096 + 4096],
                                   in_=wT[m]))

    @block.tensor
    def _(eng):
        eng.wait_ge(s_dv.h, DV_R3)
        stPreG.wait_all(eng)
        for m in range(NM):
            ps = psA if m % 2 == 0 else psB
            if m >= 1:
                eng.wait_ge(s_ac.h, AC_P3 + 2 * m)
                last = None
                for half in range(2):
                    sl = slice(half * 512, half * 512 + 512)
                    c0 = (m - 1) * T + half * 512
                    eng.matmul(ps_sum[:, sl], ones_bf[:, 0:1],
                               ffn[:, c0:c0 + 512], start=(m == 1), stop=False)
                    last = eng.matmul(ps_sq[:, sl], ones_bf[:, 0:1],
                                      sq_t[:, sl], start=(m == 1), stop=False)
                last.then_inc(s_pe.h, 1)
                s_pe.n += 1
            stW.wait(eng, NM + m)
            if m >= 2:
                eng.wait_ge(s_ac.h, AC_P3 + 2 * (m - 1))
            for kk in range(NK3):
                for half in range(2):
                    c0 = kk * T + half * 512
                    eng.matmul(ps[:, half * 512:half * 512 + 512],
                               w4b[:, m % 2, kk], h1[:, c0:c0 + 512],
                               start=(kk == 0), stop=False)
            last = None
            for half in range(2):
                sl = slice(half * 512, half * 512 + 512)
                last = eng.matmul(ps[:, sl], bT_sb[:, m * 128:(m + 1) * 128],
                                  lh_bf[:, sl], start=False, stop=True)
            last.then_inc(s_pe.h, 1)
            s_pe.n += 1
        eng.wait_ge(s_ac.h, AC_P3 + 2 * NM)
        last = None
        for half in range(2):
            sl = slice(half * 512, half * 512 + 512)
            c0 = (NM - 1) * T + half * 512
            eng.matmul(ps_sum[:, sl], ones_bf[:, 0:1], ffn[:, c0:c0 + 512],
                       start=False, stop=True)
            last = eng.matmul(ps_sq[:, sl], ones_bf[:, 0:1], sq_t[:, sl],
                              start=False, stop=True)
        last.then_inc(s_pe.h, 1)
        s_pe.n += 1
    assert s_pe.n == PE_R3

    @block.scalar
    def _(eng):
        for m in range(NM):
            eng.wait_ge(s_pe.h, PE_P3 + 2 * m + 1)
            ps = psA if m % 2 == 0 else psB
            eng.activation(out=ffn[:, m * T:(m + 1) * T], in_=ps[:],
                           func=AF.Identity,
                           bias=bias_sb[:, m:m + 1]).then_inc(s_ac.h, 1)
            eng.activation(out=sq_t[:], in_=ps[:], func=AF.Square,
                           bias=bias_sb[:, m:m + 1]).then_inc(s_ac.h, 1)
            s_ac.n += 2
    assert s_ac.n == AC_R3

    rows_ln(PE_R3)
    assert (s_dv.n, s_ac.n, s_pe.n) == (DV_P4, AC_END, PE_END)

    @block.sync
    def _(eng):
        for m in range(NM):
            if m < 2:
                eng.wait_ge(s_dv.h, DV_P4)
            else:
                eng.wait_ge(s_dv.h, DV_P4 + m - 1)
            stX.tick(eng.dma_start(out=x_t[m % 2][:],
                                   in_=x1_scr[m * 128:(m + 1) * 128, :]))

    @block.vector
    def _(eng):
        for m in range(NM):
            stX.wait(eng, 2 * NM + m)
            if m >= 1:
                stOut.wait(eng, m - 1)
            if m >= 2:
                eng.wait_ge(s_dv.h, DV_P4 + m - 1)
            seq(eng, s_dq, eng.tensor_tensor(out=t1_b[m % 2][:],
                                             in0=ffn[:, m * T:(m + 1) * T],
                                             in1=bc1_b[:], op=ALU.mult))
            seq(eng, s_dq, eng.tensor_tensor(out=x1f[:], in0=x_t[m % 2][:],
                                             in1=t1_b[m % 2][:], op=ALU.add))
            eng.tensor_tensor(out=x1f[:], in0=x1f[:], in1=bc2_b[:],
                              op=ALU.add).then_inc(s_dv.h, 1)
            s_dv.n += 1
    assert s_dv.n == DV_END

    @block.scalar
    def _(eng):
        for m in range(NM):
            eng.wait_ge(s_dv.h, DV_P4 + m + 1)
            stOut.tick(eng.dma_start(out=x2_out[m * 128:(m + 1) * 128, :],
                                     in_=x1f[:]))
        stOut.wait_all(eng)

    return k


def build_l4(T=1024):
    k = K()
    nc = k.nc
    NM, NK = 32, 32

    ctx_in = k.din("ctx_in", [128, 32 * T], BF16)
    x_in = k.din("x_in", [4096, T], F32)
    woT = k.din("woT", [32, 128, 32 * 128], F32)
    wbT = k.din("wbT", [32, 128, 32 * 128], F32)
    bo_in = k.din("bo_in", [128, 32], F32)
    bb_in = k.din("bb_in", [128, 32], F32)
    encT_in = k.din("encT_in", [128, 32 * 40], F32)
    qt_in = k.din("qt_in", [8, 128, 32 * 512], BF16)
    c0_in = k.din("c0_in", [3, 128, 32 * 32], F32)
    c1_in = k.din("c1_in", [3, 128, 32 * 8], F32)
    m0_in = k.din("m0_in", [3, 128, 32 * 32], U8)
    m1_in = k.din("m1_in", [3, 128, 32 * 8], U8)
    x4_out = k.dout("x4_out", [4096, T], F32)

    ao_scr = k.dscratch("ao_scr4", [4096, T], BF16)
    x3_scr = k.dscratch("x3_scr", [4096, T], F32)

    cdram, csb = k.setup_consts([EPS, 1.0 / 4096.0, -1.0, -0.5, 0.5])

    big = k.sbuf("big", [128, 64 * T], BF16)
    wbuf = k.sbuf("wbuf4", [128, 2 * 4096], BF16)
    ab = k.sbuf("ab4", [128, 14 * T], BF16)
    af = k.sbuf("af4", [128, 8 * T], F32)
    bo_sb = k.sbuf("bo_sb", [128, 32], F32)
    bb_sb = k.sbuf("bb_sb", [128, 32], F32)
    enc_bf = k.sbuf("enc_bf", [128, 32 * 40], BF16)
    ones_bf = k.sbuf("ones_bf4", [128, 1], BF16)
    ones_f = k.sbuf("ones_f4", [128, 1], F32)
    onesr_f = k.sbuf("onesr_f4", [1, 128], F32)

    ao_t = [ab[:, 0:T], ab[:, T:2 * T]]
    sq_t = ab[:, 2 * T:3 * T]
    t1_b = [ab[:, 3 * T:4 * T], ab[:, 4 * T:5 * T]]
    bc1_b = ab[:, 5 * T:6 * T]
    bc2_b = ab[:, 6 * T:7 * T]
    e_bf = ab[:, 7 * T:8 * T]
    ws0_b = ab[:, 8 * T:9 * T]
    ws1_b = ab[:, 9 * T:9 * T + 256]
    gt_sb = ab[:, 10 * T:14 * T]

    x_t = [af[:, 0:T], af[:, T:2 * T]]
    x1f = af[:, 2 * T:3 * T]
    g1 = af[:, 3 * T:4 * T]
    g2 = af[:, 4 * T:5 * T]
    r_mean = af[0:1, 5 * T:6 * T]
    r_rstd = af[0:1, 6 * T:7 * T]
    r_tmp = af[0:1, 7 * T:8 * T]

    psA = k.psum("psA4", [128, T], F32)
    psB = k.psum("psB4", [128, T], F32)
    ps_sum = k.psum("ps_sum4", [1, T], F32)
    ps_sq = k.psum("ps_sq4", [1, T], F32)

    s_pe = k.sem("s_pe")
    s_dv = k.sem("s_dv")
    s_ac = k.sem("s_ac")
    s_dq = k.sem("s_dq")
    s_aq = k.sem("s_aq")
    stPre = Stream(k, "stPre", depth=1)
    stPreG = Stream(k, "stPreG", depth=1)
    stW = Stream(k, "stW")
    stQT = Stream(k, "stQT", depth=1)
    stGC = Stream(k, "stGC", depth=2)
    stGM = Stream(k, "stGM", depth=2)
    stX = Stream(k, "stX")
    stAO = Stream(k, "stAO")
    stSao = Stream(k, "stSao")
    stSx1 = Stream(k, "stSx1")
    stOut = Stream(k, "stOut")
    block = k.block()

    PE_P1, PE_R1 = 0, 64
    PE_P2, PE_R2 = 65, 97
    PE_GB, PE_P3 = 98, 107
    PE_R3, PE_END = 171, 172
    DV_P1, DV_R1 = 0, 32
    DV_P2, DV_R2 = 35, 67
    DV_H3, DV_GB = 70, 102
    DV_P3 = 116
    DV_R3, DV_P4 = 116, 119
    DV_END = 151
    AC_P1, AC_R1 = 0, 32
    AC_P2, AC_R2 = 33, 65
    AC_P3, AC_R3 = 66, 130
    AC_END = 131

    def rows_ln(wait_pe):
        @block.vector
        def _(eng):
            eng.wait_ge(s_pe.h, wait_pe)
            seq(eng, s_dq, eng.tensor_scalar(
                out=r_mean[:], in0=ps_sum[:], scalar1=1.0 / 4096.0,
                scalar2=None, op0=ALU.mult))
            seq(eng, s_dq, eng.tensor_scalar(
                out=r_tmp[:], in0=ps_sq[:], scalar1=1.0 / 4096.0,
                scalar2=None, op0=ALU.mult))
            seq(eng, s_dq, eng.tensor_tensor(
                out=r_rstd[:], in0=r_mean[:], in1=r_mean[:], op=ALU.mult))
            eng.tensor_tensor(out=r_rstd[:], in0=r_tmp[:], in1=r_rstd[:],
                              op=ALU.subtract).then_inc(s_dv.h, 1)
            s_dv.n += 1
        dv1 = s_dv.n

        @block.scalar
        def _(eng):
            eng.wait_ge(s_dv.h, dv1)
            seq(eng, s_aq, eng.activation(out=r_tmp[:], in_=r_rstd[:],
                                          func=AF.Ln, bias=EPS))
            eng.activation(out=r_rstd[:], in_=r_tmp[:], func=AF.Exp,
                           scale=-0.5).then_inc(s_ac.h, 1)
            s_ac.n += 1
        ac1 = s_ac.n

        @block.vector
        def _(eng):
            eng.wait_ge(s_ac.h, ac1)
            seq(eng, s_dq, eng.tensor_tensor(out=r_tmp[:], in0=r_mean[:],
                                             in1=r_rstd[:], op=ALU.mult))
            eng.tensor_scalar(out=r_tmp[:], in0=r_tmp[:], scalar1=-1.0,
                              scalar2=None, op0=ALU.mult).then_inc(s_dv.h, 1)
            s_dv.n += 1
        dv2 = s_dv.n

        @block.tensor
        def _(eng):
            eng.wait_ge(s_dv.h, dv2)
            eng.matmul(psA[:, 0:512], onesr_f[:1, :], r_rstd[:, 0:512],
                       start=True, stop=True)
            eng.matmul(psA[:, 512:T], onesr_f[:1, :], r_rstd[:, 512:T],
                       start=True, stop=True)
            eng.matmul(psB[:, 0:512], onesr_f[:1, :], r_tmp[:, 0:512],
                       start=True, stop=True)
            eng.matmul(psB[:, 512:T], onesr_f[:1, :], r_tmp[:, 512:T],
                       start=True, stop=True).then_inc(s_pe.h, 1)
            s_pe.n += 1
        pe1 = s_pe.n

        @block.vector
        def _(eng):
            eng.wait_ge(s_pe.h, pe1)
            seq(eng, s_dq, eng.tensor_copy(bc1_b[:], psA[:]))
            seq(eng, s_dq, eng.tensor_copy(bc2_b[:], psB[:]))
            eng.nop().then_inc(s_dv.h, 1)
            s_dv.n += 1

    ctxv = big[:, 0:32 * T].rearrange("p (q t) -> p q t", t=T)
    h3 = big[:, 0:32 * T]
    ffn = big[:, 32 * T:64 * T]
    qtb = big[:, 32 * T:48 * T]

    @block.sync
    def _(eng):
        stPre.tick(eng.dma_start(out=csb[:], in_=cdram[:]))
        stPre.tick(eng.dma_start(out=bo_sb[:], in_=bo_in[:]))
        stPre.tick(eng.dma_start(out=bb_sb[:], in_=bb_in[:]))
        stPre.tick(eng.dma_start(out=big[:, 0:32 * T], in_=ctx_in[:]))

    @block.gpsimd
    def _(eng):
        eng.memset(ones_bf[:], 1.0)
        eng.memset(ones_f[:], 1.0)
        eng.memset(onesr_f[:], 1.0)
        stPreG.tick(eng.dma_start(out=enc_bf[:], in_=encT_in[:]))

    @block.gpsimd
    def _(eng):
        for m in range(NM):
            if m >= 2:
                eng.wait_ge(s_pe.h, PE_P1 + 2 * (m - 2) + 1)
            stW.tick(eng.dma_start(out=wbuf[:, (m % 2) * 4096:
                                           (m % 2) * 4096 + 4096],
                                   in_=woT[m]))

    w4 = wbuf.ap().rearrange("p (j q c) -> p j q c", j=2, c=128)

    @block.tensor
    def _(eng):
        stPre.wait_all(eng)
        stPreG.wait_all(eng)
        for m in range(NM):
            ps = psA if m % 2 == 0 else psB
            if m >= 1:
                eng.wait_ge(s_dv.h, DV_P1 + m)
                eng.wait_ge(s_ac.h, AC_P1 + m)
                last = None
                for half in range(2):
                    sl = slice(half * 512, half * 512 + 512)
                    eng.matmul(ps_sum[:, sl], ones_bf[:, 0:1],
                               ao_t[(m - 1) % 2][:, sl],
                               start=(m == 1), stop=False)
                    last = eng.matmul(ps_sq[:, sl], ones_bf[:, 0:1],
                                      sq_t[:, sl], start=(m == 1), stop=False)
                last.then_inc(s_pe.h, 1)
                s_pe.n += 1
            stW.wait(eng, m)
            if m >= 2:
                eng.wait_ge(s_dv.h, DV_P1 + m - 1)
                eng.wait_ge(s_ac.h, AC_P1 + m - 1)
            last = None
            for kk in range(NK):
                for half in range(2):
                    sl = slice(half * 512, half * 512 + 512)
                    last = eng.matmul(ps[:, sl], w4[:, m % 2, kk],
                                      ctxv[:, kk, sl],
                                      start=(kk == 0), stop=(kk == NK - 1))
            last.then_inc(s_pe.h, 1)
            s_pe.n += 1
        eng.wait_ge(s_dv.h, DV_P1 + NM)
        eng.wait_ge(s_ac.h, AC_P1 + NM)
        last = None
        for half in range(2):
            sl = slice(half * 512, half * 512 + 512)
            eng.matmul(ps_sum[:, sl], ones_bf[:, 0:1],
                       ao_t[(NM - 1) % 2][:, sl], start=False, stop=True)
            last = eng.matmul(ps_sq[:, sl], ones_bf[:, 0:1], sq_t[:, sl],
                              start=False, stop=True)
        last.then_inc(s_pe.h, 1)
        s_pe.n += 1
    assert s_pe.n == PE_R1, s_pe.n

    @block.vector
    def _(eng):
        for m in range(NM):
            eng.wait_ge(s_pe.h, PE_P1 + 2 * m + 1)
            stSao.wait(eng, m - 2)
            if m >= 2:
                eng.wait_ge(s_ac.h, AC_P1 + m - 1)
            ps = psA if m % 2 == 0 else psB
            eng.tensor_scalar(out=ao_t[m % 2][:], in0=ps[:],
                              scalar1=bo_sb[:, m:m + 1], scalar2=None,
                              op0=ALU.add).then_inc(s_dv.h, 1)
            s_dv.n += 1
    assert s_dv.n == DV_R1

    @block.scalar
    def _(eng):
        for m in range(NM):
            eng.wait_ge(s_dv.h, DV_P1 + m + 1)
            eng.activation(out=sq_t[:], in_=ao_t[m % 2][:],
                           func=AF.Square).then_inc(s_ac.h, 1)
            s_ac.n += 1
            stSao.tick(eng.dma_start(out=ao_scr[m * 128:(m + 1) * 128, :],
                                     in_=ao_t[m % 2][:]))
    assert s_ac.n == AC_R1

    rows_ln(PE_R1)
    assert (s_dv.n, s_ac.n, s_pe.n) == (DV_P2, AC_P2, PE_P2)

    @block.sync
    def _(eng):
        for m in range(NM):
            if m < 2:
                eng.wait_ge(s_pe.h, PE_R1)
                stSao.wait(eng, NM - 2 + m)
            else:
                eng.wait_ge(s_dv.h, DV_P2 + m - 1)
            stSao.wait(eng, m)
            stX.tick(eng.dma_start(out=x_t[m % 2][:],
                                   in_=x_in[m * 128:(m + 1) * 128, :]))
            stAO.tick(eng.dma_start(out=ao_t[m % 2][:],
                                    in_=ao_scr[m * 128:(m + 1) * 128, :]))

    @block.vector
    def _(eng):
        for m in range(NM):
            stX.wait(eng, m)
            stAO.wait(eng, m)
            if m >= 1:
                stSx1.wait(eng, m - 1)
                eng.wait_ge(s_ac.h, AC_P2 + m)
                eng.wait_ge(s_pe.h, PE_P2 + m)
            if m >= 2:
                eng.wait_ge(s_dv.h, DV_P2 + m - 1)
            seq(eng, s_dq, eng.tensor_tensor(out=t1_b[m % 2][:],
                                             in0=ao_t[m % 2][:],
                                             in1=bc1_b[:], op=ALU.mult))
            seq(eng, s_dq, eng.tensor_tensor(out=x1f[:], in0=x_t[m % 2][:],
                                             in1=t1_b[m % 2][:], op=ALU.add))
            eng.tensor_tensor(out=x1f[:], in0=x1f[:], in1=bc2_b[:],
                              op=ALU.add).then_inc(s_dv.h, 1)
            s_dv.n += 1
    assert s_dv.n == DV_R2

    @block.scalar
    def _(eng):
        for m in range(NM):
            eng.wait_ge(s_dv.h, DV_P2 + m + 1)
            if m >= 1:
                eng.wait_ge(s_pe.h, PE_P2 + m)
            eng.activation(out=sq_t[:], in_=x1f[:],
                           func=AF.Square).then_inc(s_ac.h, 1)
            s_ac.n += 1
            stSx1.tick(eng.dma_start(out=x3_scr[m * 128:(m + 1) * 128, :],
                                     in_=x1f[:]))
    assert s_ac.n == AC_R2

    @block.tensor
    def _(eng):
        for m in range(NM):
            eng.wait_ge(s_dv.h, DV_P2 + m + 1)
            eng.wait_ge(s_ac.h, AC_P2 + m + 1)
            last = None
            for half in range(2):
                sl = slice(half * 512, half * 512 + 512)
                eng.matmul(ps_sum[:, sl], ones_f[:, 0:1], x1f[:, sl],
                           start=(m == 0), stop=(m == NM - 1))
                last = eng.matmul(ps_sq[:, sl], ones_bf[:, 0:1], sq_t[:, sl],
                                  start=(m == 0), stop=(m == NM - 1))
            last.then_inc(s_pe.h, 1)
            s_pe.n += 1
    assert s_pe.n == PE_R2

    rows_ln(PE_R2)
    assert (s_dv.n, s_ac.n, s_pe.n) == (DV_H3, AC_P3, PE_GB)

    @block.sync
    def _(eng):
        for m in range(NM):
            stSx1.wait(eng, m)
            if m < 2:
                eng.wait_ge(s_dv.h, DV_R2)
            else:
                eng.wait_ge(s_dv.h, DV_H3 + m - 1)
            stX.tick(eng.dma_start(out=x_t[m % 2][:],
                                   in_=x3_scr[m * 128:(m + 1) * 128, :]))

    @block.vector
    def _(eng):
        for m in range(NM):
            stX.wait(eng, NM + m)
            if m >= 2:
                eng.wait_ge(s_dv.h, DV_H3 + m - 1)
            seq(eng, s_dq, eng.tensor_tensor(out=t1_b[m % 2][:],
                                             in0=x_t[m % 2][:],
                                             in1=bc1_b[:], op=ALU.mult))
            eng.tensor_tensor(out=h3[:, m * T:(m + 1) * T], in0=t1_b[m % 2][:],
                              in1=bc2_b[:], op=ALU.add).then_inc(s_dv.h, 1)
            s_dv.n += 1
    assert s_dv.n == DV_GB

    @block.sync
    def _(eng):
        for e in range(3):
            eng.wait_ge(s_dv.h, DV_GB + e)
            stGC.tick(eng.dma_start(out=g1[:, 0:T], in_=c0_in[e]))
            stGC.tick(eng.dma_start(out=g2[:, 0:256], in_=c1_in[e]))

    @block.gpsimd
    def _(eng):
        for e in range(3):
            eng.wait_ge(s_dv.h, DV_GB + e)
            stGM.tick(eng.dma_start(out=x_t[0][:, 0:T], in_=m0_in[e]))
            stGM.tick(eng.dma_start(out=x_t[1][:, 0:256], in_=m1_in[e]))

    @block.vector
    def _(eng):
        for e in range(3):
            stGC.wait(eng, 2 * e)
            stGC.wait(eng, 2 * e + 1)
            stGM.wait(eng, 2 * e)
            stGM.wait(eng, 2 * e + 1)
            if e == 0:
                seq(eng, s_dq, eng.tensor_tensor(
                    out=x1f[:, 0:T], in0=g1[:, 0:T], in1=x_t[0][:, 0:T],
                    op=ALU.mult))
                seq(eng, s_dq, eng.tensor_tensor(
                    out=g2[:, 512:768], in0=g2[:, 0:256], in1=x_t[1][:, 0:256],
                    op=ALU.mult))
            else:
                seq(eng, s_dq, eng.tensor_tensor(
                    out=g1[:, 0:T], in0=g1[:, 0:T], in1=x_t[0][:, 0:T],
                    op=ALU.mult))
                seq(eng, s_dq, eng.tensor_tensor(
                    out=x1f[:, 0:T], in0=x1f[:, 0:T], in1=g1[:, 0:T],
                    op=ALU.add))
                seq(eng, s_dq, eng.tensor_tensor(
                    out=g2[:, 0:256], in0=g2[:, 0:256], in1=x_t[1][:, 0:256],
                    op=ALU.mult))
                seq(eng, s_dq, eng.tensor_tensor(
                    out=g2[:, 512:768], in0=g2[:, 512:768], in1=g2[:, 0:256],
                    op=ALU.add))
            eng.nop().then_inc(s_dv.h, 1)
            s_dv.n += 1
        eng.tensor_copy(ws0_b[:], x1f[:, 0:T]).then_inc(s_dv.h, 1)
        s_dv.n += 1
        eng.tensor_copy(ws1_b[:], g2[:, 512:768]).then_inc(s_dv.h, 1)
        s_dv.n += 1
    WS_DV = s_dv.n

    @block.sync
    def _(eng):
        for c in range(8):
            if c >= 1:
                eng.wait_ge(s_pe.h, PE_GB + c)
            else:
                eng.wait_ge(s_ac.h, AC_P3)
                eng.wait_ge(s_dv.h, WS_DV)
            stQT.tick(eng.dma_start(out=qtb[:], in_=qt_in[c]))

    ws0v = ws0_b.rearrange("p (q r) -> p q r", r=32)
    ws1v = ws1_b.rearrange("p (q r) -> p q r", r=8)
    qtv = qtb.rearrange("p (q c) -> p q c", c=512)
    encv = enc_bf.rearrange("p (q r) -> p q r", r=40)

    @block.tensor
    def _(eng):
        eng.wait_ge(s_dv.h, WS_DV)
        for c in range(8):
            stQT.wait(eng, c)
            if c >= 1:
                eng.wait_ge(s_dv.h, WS_DV + c)
            last = None
            for kk in range(NK):
                eng.matmul(psA[0:32, 0:512], ws0v[:, kk], qtv[:, kk],
                           start=(kk == 0), stop=(kk == NK - 1))
            for kk in range(NK):
                last = eng.matmul(psA[32:40, 0:512], ws1v[:, kk], qtv[:, kk],
                                  start=(kk == 0), stop=(kk == NK - 1))
            last.then_inc(s_pe.h, 1)
            s_pe.n += 1
        last = None
        for kk in range(NK):
            for half in range(2):
                c0_ = kk * T + half * 512
                last = eng.matmul(psB[0:40, half * 512:half * 512 + 512],
                                  encv[:, kk], h3[:, c0_:c0_ + 512],
                                  start=(kk == 0), stop=(kk == NK - 1))
        last.then_inc(s_pe.h, 1)
        s_pe.n += 1
    assert s_pe.n == PE_P3

    @block.vector
    def _(eng):
        for c in range(8):
            eng.wait_ge(s_pe.h, PE_GB + c + 1)
            eng.tensor_scalar(out=gt_sb[0:40, c * 512:(c + 1) * 512],
                              in0=psA[0:40, 0:512], scalar1=0.5, scalar2=None,
                              op0=ALU.mult).then_inc(s_dv.h, 1)
            s_dv.n += 1
        eng.wait_ge(s_pe.h, PE_P3)
        eng.tensor_copy(e_bf[0:40, :], psB[0:40, :]).then_inc(s_dv.h, 1)
        s_dv.n += 1
    assert s_dv.n == DV_P3, s_dv.n

    @block.gpsimd
    def _(eng):
        for m in range(NM):
            if m < 2:
                eng.wait_ge(s_pe.h, PE_P3)
            else:
                eng.wait_ge(s_pe.h, PE_P3 + 2 * (m - 2) + 1)
            stW.tick(eng.dma_start(out=wbuf[:, (m % 2) * 4096:
                                           (m % 2) * 4096 + 4096],
                                   in_=wbT[m]))

    @block.tensor
    def _(eng):
        eng.wait_ge(s_dv.h, DV_P3)
        for m in range(NM):
            ps = psA if m % 2 == 0 else psB
            if m >= 1:
                eng.wait_ge(s_ac.h, AC_P3 + 2 * m)
                last = None
                for half in range(2):
                    sl = slice(half * 512, half * 512 + 512)
                    c0_ = (m - 1) * T + half * 512
                    eng.matmul(ps_sum[:, sl], ones_bf[:, 0:1],
                               ffn[:, c0_:c0_ + 512], start=(m == 1),
                               stop=False)
                    last = eng.matmul(ps_sq[:, sl], ones_bf[:, 0:1],
                                      sq_t[:, sl], start=(m == 1), stop=False)
                last.then_inc(s_pe.h, 1)
                s_pe.n += 1
            stW.wait(eng, NM + m)
            if m >= 2:
                eng.wait_ge(s_ac.h, AC_P3 + 2 * (m - 1))
            for kk in range(NK):
                for half in range(2):
                    c0_ = kk * T + half * 512
                    eng.matmul(ps[:, half * 512:half * 512 + 512],
                               w4[:, m % 2, kk], h3[:, c0_:c0_ + 512],
                               start=(kk == 0), stop=False)
            last = None
            for half in range(2):
                sl = slice(half * 512, half * 512 + 512)
                last = eng.matmul(ps[:, sl],
                                  gt_sb[0:40, m * 128:(m + 1) * 128],
                                  e_bf[0:40, sl], start=False, stop=True)
            last.then_inc(s_pe.h, 1)
            s_pe.n += 1
        eng.wait_ge(s_ac.h, AC_P3 + 2 * NM)
        last = None
        for half in range(2):
            sl = slice(half * 512, half * 512 + 512)
            c0_ = (NM - 1) * T + half * 512
            eng.matmul(ps_sum[:, sl], ones_bf[:, 0:1], ffn[:, c0_:c0_ + 512],
                       start=False, stop=True)
            last = eng.matmul(ps_sq[:, sl], ones_bf[:, 0:1], sq_t[:, sl],
                              start=False, stop=True)
        last.then_inc(s_pe.h, 1)
        s_pe.n += 1
    assert s_pe.n == PE_R3

    @block.scalar
    def _(eng):
        for m in range(NM):
            eng.wait_ge(s_pe.h, PE_P3 + 2 * m + 1)
            ps = psA if m % 2 == 0 else psB
            eng.activation(out=ffn[:, m * T:(m + 1) * T],
                           in_=ps[:], func=AF.Identity,
                           bias=bb_sb[:, m:m + 1]).then_inc(s_ac.h, 1)
            eng.activation(out=sq_t[:], in_=ps[:], func=AF.Square,
                           bias=bb_sb[:, m:m + 1]).then_inc(s_ac.h, 1)
            s_ac.n += 2
    assert s_ac.n == AC_R3

    rows_ln(PE_R3)
    assert (s_dv.n, s_ac.n, s_pe.n) == (DV_P4, AC_END, PE_END)

    @block.sync
    def _(eng):
        for m in range(NM):
            if m < 2:
                eng.wait_ge(s_dv.h, DV_P4)
            else:
                eng.wait_ge(s_dv.h, DV_P4 + m - 1)
            stX.tick(eng.dma_start(out=x_t[m % 2][:],
                                   in_=x3_scr[m * 128:(m + 1) * 128, :]))

    @block.vector
    def _(eng):
        for m in range(NM):
            stX.wait(eng, 2 * NM + m)
            if m >= 1:
                stOut.wait(eng, m - 1)
            if m >= 2:
                eng.wait_ge(s_dv.h, DV_P4 + m - 1)
            seq(eng, s_dq, eng.tensor_tensor(out=t1_b[m % 2][:],
                                             in0=ffn[:, m * T:(m + 1) * T],
                                             in1=bc1_b[:], op=ALU.mult))
            seq(eng, s_dq, eng.tensor_tensor(out=x1f[:], in0=x_t[m % 2][:],
                                             in1=t1_b[m % 2][:], op=ALU.add))
            eng.tensor_tensor(out=x1f[:], in0=x1f[:], in1=bc2_b[:],
                              op=ALU.add).then_inc(s_dv.h, 1)
            s_dv.n += 1
    assert s_dv.n == DV_END

    @block.scalar
    def _(eng):
        for m in range(NM):
            eng.wait_ge(s_dv.h, DV_P4 + m + 1)
            stOut.tick(eng.dma_start(out=x4_out[m * 128:(m + 1) * 128, :],
                                     in_=x1f[:]))
        stOut.wait_all(eng)

    return k


def build_l3():
    k = K()
    nc = k.nc
    S, NCH, NH, TC, NTC = 4096, 32, 8, 512, 8
    TOPK = 819.0
    ISQ = float(1.0 / np.sqrt(128.0))

    x_in = k.din("x_in", [4096, S], F32)
    wqk = k.din("wqk", [16, 128, 32 * 128], F32)
    wvT = k.din("wvT", [32, 128, 1024], F32)
    bqk = k.din("bqk", [128, 16], F32)
    bv_row = k.din("bv_row", [1, 1024], F32)
    idn_in = k.din("idn_in", [128, 128], BF16)
    ctx_out = k.dout("ctx_out", [1024, S], BF16)

    q_scr = k.dscratch("q_scr", [1024, S], BF16)
    k_scr = k.dscratch("k_scr", [1024, S], BF16)
    vT_scr = k.dscratch("vT_scr", [NH, 128, 32 * 128], BF16)

    witer = [8.0 * 2.0 ** (-i) for i in range(12)]
    OFF12 = 8.0 * (1.0 - 2.0 ** -10)
    cdram, csb = k.setup_consts(
        [EPS, 1.0 / 4096.0, -1.0, -0.5, ISQ, -8.0, 1229.25, -OFF12]
        + [0.5 * w for w in witer])

    x_sb = k.sbuf("x_sb", [128, 32 * TC], F32)
    hc_sb = k.sbuf("hc_sb", [128, 32 * TC], BF16)
    wbuf = k.sbuf("wbuf3", [128, 2 * 4096], BF16)
    vbuf = k.sbuf("vbuf3", [128, 2 * 512], BF16)
    s4_sb = k.sbuf("s4_sb", [128, 4 * S], BF16)
    kh_sb = k.sbuf("kh_sb", [128, S], BF16)
    qh_sb = k.sbuf("qh_sb", [128, S], BF16)
    vt_sb = k.sbuf("vt_sb", [128, 32 * 128], BF16)
    hsc = k.sbuf("hsc", [128, 512], BF16)
    pmt_sb = k.sbuf("pmt_sb", [128, 32 * 128], BF16)
    ctx_sb = k.sbuf("ctx_sb", [128, 2 * 128], BF16)
    e_sb = hc_sb[:, 0:S]
    bv_bc = k.sbuf("bv_bc", [128, 1024], BF16)
    idn_sb = k.sbuf("idn_sb", [128, 128], BF16)
    sq_b = k.sbuf("sq_b3", [128, TC], BF16)
    qkev = k.sbuf("qkev", [128, 2 * TC], BF16)
    vtev = k.sbuf("vtev", [128, 2 * 512], BF16)
    bqk_sb = k.sbuf("bqk_sb", [128, 16], F32)
    bvr_sb = k.sbuf("bvr_sb", [1, 1024], F32)
    ones_bf = k.sbuf("ones_bf3", [128, 1], BF16)
    ones_f = k.sbuf("ones_f3", [128, 1], F32)
    onesr_f = k.sbuf("onesr_f3", [1, 128], F32)
    bc1_b = k.sbuf("bc1_b3", [128, TC], BF16)
    bc2_b = k.sbuf("bc2_b3", [128, TC], BF16)
    selw = k.sbuf("selw", [128, 48], F32)
    rows_f = k.sbuf("rows_f3", [1, 3 * TC], F32)

    lo_c = selw[:, 0:4]
    hi_c = selw[:, 4:8]
    mid_c = selw[:, 8:12]
    cnt_c = selw[:, 12:16]
    sel_c = selw[:, 16:20]
    ntau_c = selw[:, 20:24]
    z_c = selw[:, 24:28]
    iz_c = selw[:, 28:32]
    nmid_c = selw[:, 32:36]
    sgn_c = selw[:, 36:38]
    accr0 = selw[:, 38:42]
    accr1 = selw[:, 42:46]
    r_mean = rows_f[0:1, 0:TC]
    r_rstd = rows_f[0:1, TC:2 * TC]
    r_tmp = rows_f[0:1, 2 * TC:3 * TC]

    psF = k.psum("psF", [128, 3072], F32)
    psT = k.psum("psT", [128, 2048], BF16)
    psQ = psF[:, 0:512]
    psR = psF[:, 512:1024]
    psV = psF[:, 1024:2048]
    ps_sum = psF[0:1, 2048:2560]
    ps_sq = psF[0:1, 2560:3072]
    psS = psF[:, 0:2048]
    psC = psF[:, 2048:2176]

    s_pe = k.sem("s_pe")
    s_dv = k.sem("s_dv")
    s_ac = k.sem("s_ac")
    s_dq = k.sem("s_dq")
    s_aq = k.sem("s_aq")
    stPre = Stream(k, "stPre", depth=1)
    stX = Stream(k, "stX")
    stW = Stream(k, "stW")
    stV = Stream(k, "stV")
    stQK = Stream(k, "stQK")
    stVS = Stream(k, "stVS", depth=8)
    stHL = Stream(k, "stHL", depth=3)
    stCO = Stream(k, "stCO")
    block = k.block()

    @block.sync
    def _(eng):
        stPre.tick(eng.dma_start(out=csb[:], in_=cdram[:]))
        stPre.tick(eng.dma_start(out=bqk_sb[:], in_=bqk[:]))
        stPre.tick(eng.dma_start(out=bvr_sb[:], in_=bv_row[:]))
        stPre.tick(eng.dma_start(out=idn_sb[:], in_=idn_in[:]))

    s_in = k.sem("s_in")

    @block.gpsimd
    def _(eng):
        eng.memset(ones_bf[:], 1.0)
        eng.memset(ones_f[:], 1.0)
        eng.memset(onesr_f[:], 1.0).then_inc(s_in.h, 1)
        s_in.n += 1

    @block.tensor
    def _(eng):
        stPre.wait_all(eng)
        eng.wait_ge(s_in.h, 1)
        eng.matmul(psV[:, 0:512], onesr_f[:1, :], bvr_sb[:1, 0:512],
                   start=True, stop=True)
        eng.matmul(psV[:, 512:1024], onesr_f[:1, :], bvr_sb[:1, 512:1024],
                   start=True, stop=True).then_inc(s_pe.h, 1)
        s_pe.n += 1

    @block.vector
    def _(eng):
        eng.wait_ge(s_pe.h, 1)
        eng.tensor_copy(bv_bc[:], psV[:]).then_inc(s_dv.h, 1)
        s_dv.n += 1
    PE0, DV0, AC0 = 1, 1, 0

    PE_TC, DV_TC, AC_TC = 113, 59, 33

    def pe0(tc):
        return PE0 + tc * PE_TC

    def dv0(tc):
        return DV0 + tc * DV_TC

    def ac0(tc):
        return AC0 + tc * AC_TC

    @block.sync
    def _(eng):
        for tc in range(NTC):
            for m in range(NCH):
                if tc >= 1:
                    eng.wait_ge(s_dv.h, dv0(tc - 1) + 3 + m + 1)
                stX.tick(eng.dma_start(
                    out=x_sb[:, m * TC:(m + 1) * TC],
                    in_=x_in[m * 128:(m + 1) * 128, tc * TC:(tc + 1) * TC]))

    @block.gpsimd
    def _(eng):
        for tc in range(NTC):
            for m in range(16):
                i = tc * 16 + m
                if i >= 2:
                    tc2, m2 = divmod(i - 2, 16)
                    eng.wait_ge(s_pe.h, pe0(tc2) + 33 + m2 + 1)
                stW.tick(eng.dma_start(
                    out=wbuf[:, (m % 2) * 4096:(m % 2) * 4096 + 4096],
                    in_=wqk[m]))
            for half in range(2):
                for kk in range(32):
                    i = tc * 64 + half * 32 + kk
                    if i >= 2:
                        tc2, r2 = divmod(i - 2, 64)
                        eng.wait_ge(s_pe.h, pe0(tc2) + 49 + r2 + 1)
                    stV.tick(eng.dma_start(
                        out=vbuf[:, (i % 2) * 512:(i % 2) * 512 + 512],
                        in_=wvT[kk][:, half * 512:half * 512 + 512]))

    @block.scalar
    def _(eng):
        for tc in range(NTC):
            for m in range(NCH):
                if m == 0:
                    for sx in stX.sems:
                        eng.wait_ge(sx.h, 16 * 16 * (tc + 1))
                if not (tc == 0 and m == 0):
                    prev = pe0(tc) + m if m >= 1 else pe0(tc - 1) + NCH
                    eng.wait_ge(s_pe.h, prev)
                eng.activation(out=sq_b[:], in_=x_sb[:, m * TC:(m + 1) * TC],
                               func=AF.Square).then_inc(s_ac.h, 1)
                s_ac.n += 1
            eng.wait_ge(s_dv.h, dv0(tc) + 1)
            seq(eng, s_aq, eng.activation(out=r_tmp[:], in_=r_rstd[:],
                                          func=AF.Ln, bias=EPS))
            eng.activation(out=r_rstd[:], in_=r_tmp[:], func=AF.Exp,
                           scale=-0.5).then_inc(s_ac.h, 1)
            s_ac.n += 1
            for m in range(16):
                eng.wait_ge(s_dv.h, dv0(tc) + 3 + NCH + m + 1)
                scr = q_scr if m < 8 else k_scr
                row = (m % 8) * 128
                stQK.tick(eng.dma_start(
                    out=scr[row:row + 128, tc * TC:(tc + 1) * TC],
                    in_=qkev[:, (m % 2) * TC:(m % 2) * TC + TC]))
            for half in range(2):
                for tq in range(4):
                    eng.wait_ge(s_dv.h,
                                dv0(tc) + 3 + NCH + 16 + 4 * half + tq + 1)
                    for hh in range(4):
                        h = half * 4 + hh
                        stVS.tick(eng.dma_start(
                            out=vT_scr[h][:, (tc * 4 + tq) * 128:
                                          (tc * 4 + tq + 1) * 128],
                            in_=vtev[:, (tq % 2) * 512 + hh * 128:
                                     (tq % 2) * 512 + (hh + 1) * 128]))
    assert s_ac.n == AC0 + NTC * AC_TC

    @block.tensor
    def _(eng):
        for tc in range(NTC):
            for m in range(NCH):
                eng.wait_ge(s_ac.h, ac0(tc) + m + 1)
                if tc >= 1 and m == 0:
                    eng.wait_ge(s_dv.h, dv0(tc - 1) + 1)
                eng.matmul(ps_sum, ones_f[:, 0:1],
                           x_sb[:, m * TC:(m + 1) * TC],
                           start=(m == 0), stop=(m == NCH - 1))
                eng.matmul(ps_sq, ones_bf[:, 0:1], sq_b[:],
                           start=(m == 0),
                           stop=(m == NCH - 1)).then_inc(s_pe.h, 1)
                s_pe.n += 1
            eng.wait_ge(s_dv.h, dv0(tc) + 2)
            eng.matmul(psQ, onesr_f[:1, :], r_rstd[:, :],
                       start=True, stop=True)
            eng.matmul(psR, onesr_f[:1, :], r_tmp[:, :],
                       start=True, stop=True).then_inc(s_pe.h, 1)
            s_pe.n += 1
            for m in range(16):
                eng.wait_ge(s_dv.h, dv0(tc) + 3 + NCH)
                stW.wait(eng, tc * 16 + m)
                if m >= 2:
                    eng.wait_ge(s_dv.h, dv0(tc) + 3 + NCH + m - 1)
                elif m < 2:
                    pass
                ps = psQ if m % 2 == 0 else psR
                last = None
                for kk in range(32):
                    last = eng.matmul(
                        ps, wbuf[:, (m % 2) * 4096 + kk * 128:
                                 (m % 2) * 4096 + (kk + 1) * 128],
                        hc_sb[:, kk * TC:kk * TC + 512],
                        start=(kk == 0), stop=(kk == 31))
                last.then_inc(s_pe.h, 1)
                s_pe.n += 1
            for half in range(2):
                eng.wait_ge(s_dv.h, dv0(tc) + 3 + NCH + 16 + 4 * half)
                for kk in range(32):
                    i = tc * 64 + half * 32 + kk
                    stV.wait(eng, i)
                    last = None
                    for tq in range(4):
                        last = eng.matmul(
                            psF[:, tq * 512:(tq + 1) * 512],
                            hc_sb[:, kk * TC + tq * 128:
                                  kk * TC + (tq + 1) * 128],
                            vbuf[:, (i % 2) * 512:(i % 2) * 512 + 512],
                            start=(kk == 0), stop=(kk == 31))
                    last.then_inc(s_pe.h, 1)
                    s_pe.n += 1
    assert s_pe.n == PE0 + NTC * PE_TC

    @block.vector
    def _(eng):
        for tc in range(NTC):
            eng.wait_ge(s_pe.h, pe0(tc) + NCH)
            seq(eng, s_dq, eng.tensor_scalar(
                out=r_mean[:], in0=ps_sum, scalar1=1.0 / 4096.0,
                scalar2=None, op0=ALU.mult))
            seq(eng, s_dq, eng.tensor_scalar(
                out=r_tmp[:], in0=ps_sq, scalar1=1.0 / 4096.0,
                scalar2=None, op0=ALU.mult))
            seq(eng, s_dq, eng.tensor_tensor(
                out=r_rstd[:], in0=r_mean[:], in1=r_mean[:], op=ALU.mult))
            eng.tensor_tensor(out=r_rstd[:], in0=r_tmp[:], in1=r_rstd[:],
                              op=ALU.subtract).then_inc(s_dv.h, 1)
            s_dv.n += 1
            eng.wait_ge(s_ac.h, ac0(tc) + NCH + 1)
            seq(eng, s_dq, eng.tensor_tensor(out=r_tmp[:], in0=r_mean[:],
                                             in1=r_rstd[:], op=ALU.mult))
            eng.tensor_scalar(out=r_tmp[:], in0=r_tmp[:], scalar1=-1.0,
                              scalar2=None, op0=ALU.mult).then_inc(s_dv.h, 1)
            s_dv.n += 1
            eng.wait_ge(s_pe.h, pe0(tc) + NCH + 1)
            seq(eng, s_dq, eng.tensor_copy(bc1_b[:], psQ))
            seq(eng, s_dq, eng.tensor_copy(bc2_b[:], psR))
            eng.nop().then_inc(s_dv.h, 1)
            s_dv.n += 1
            for m in range(NCH):
                if tc >= 1 and m == 0:
                    eng.wait_ge(s_pe.h, pe0(tc - 1) + PE_TC)
                if m >= 1:
                    eng.wait_ge(s_dv.h, dv0(tc) + 3 + m)
                seq(eng, s_dq, eng.tensor_tensor(
                    out=hsc[:], in0=x_sb[:, m * TC:(m + 1) * TC],
                    in1=bc1_b[:], op=ALU.mult))
                eng.tensor_tensor(out=hc_sb[:, m * TC:(m + 1) * TC],
                                  in0=hsc[:], in1=bc2_b[:],
                                  op=ALU.add).then_inc(s_dv.h, 1)
                s_dv.n += 1
            for m in range(16):
                eng.wait_ge(s_pe.h, pe0(tc) + NCH + 1 + m + 1)
                stQK.wait(eng, tc * 16 + m - 2)
                ps = psQ if m % 2 == 0 else psR
                if m < 8:
                    eng.tensor_scalar(out=qkev[:, (m % 2) * TC:
                                               (m % 2) * TC + TC],
                                      in0=ps, scalar1=bqk_sb[:, m:m + 1],
                                      scalar2=ISQ, op0=ALU.add,
                                      op1=ALU.mult).then_inc(s_dv.h, 1)
                else:
                    eng.tensor_scalar(out=qkev[:, (m % 2) * TC:
                                               (m % 2) * TC + TC],
                                      in0=ps, scalar1=bqk_sb[:, m:m + 1],
                                      scalar2=None,
                                      op0=ALU.add).then_inc(s_dv.h, 1)
                s_dv.n += 1
            for half in range(2):
                eng.wait_ge(s_pe.h, pe0(tc) + 49 + 32 * (half + 1))
                for tq in range(4):
                    u = (tc * 2 + half) * 4 + tq
                    for hh in range(4):
                        stVS.wait(eng, (u - 2) * 4 + hh)
                    eng.tensor_tensor(
                        out=vtev[:, (tq % 2) * 512:(tq % 2) * 512 + 512],
                        in0=psF[:, tq * 512:(tq + 1) * 512],
                        in1=bv_bc[:, half * 512:half * 512 + 512],
                        op=ALU.add).then_inc(s_dv.h, 1)
                    s_dv.n += 1
    assert s_dv.n == DV0 + NTC * DV_TC

    PE_QKV = PE0 + NTC * PE_TC
    DV_QKV = DV0 + NTC * DV_TC
    AC_QKV = AC0 + NTC * AC_TC

    ITER = 10
    SEL = ITER + 1
    PE_G, DV_G, AC_G = 28, 8 + (ITER + 1) + 28, 1 + 8
    nmid_marks = [0] * (NH * 8)

    def pe1(h, g):
        return PE_QKV + (h * 8 + g) * PE_G

    def dv1(h, g):
        return DV_QKV + (h * 8 + g) * DV_G

    def ac1(h, g):
        return AC_QKV + (h * 8 + g) * AC_G

    def sbase(h, g):
        return PE_QKV + (h * 8 + g) * PE_G

    def tbase(h, g):
        return PE_QKV + (h * 8 + g) * PE_G + 8

    @block.sync
    def _(eng):
        for h in range(NH):
            if h == 0:
                stQK.wait_all(eng)
                stVS.wait_all(eng)
            else:
                eng.wait_ge(s_pe.h, pe1(h, 0))
            stHL.tick(eng.dma_start(out=kh_sb[:],
                                    in_=k_scr[h * 128:(h + 1) * 128, :]))
            stHL.tick(eng.dma_start(out=qh_sb[:],
                                    in_=q_scr[h * 128:(h + 1) * 128, :]))
            stHL.tick(eng.dma_start(out=vt_sb[:], in_=vT_scr[h]))

    @block.tensor
    def _(eng):
        def emit_scores(h, g):
            base_dv = dv1(h, g)
            for j in range(4):
                qb = g * 4 + j
                for r in range(2):
                    k8 = 2 * j + r
                    if k8 == 0:
                        pair = h * 8 + g
                        eng.wait_ge(s_dv.h,
                                    DV_QKV + (pair - 1) * DV_G + 8
                                    if pair else DV_QKV)
                    else:
                        eng.wait_ge(s_dv.h, base_dv + k8)
                    last = None
                    for c in range(4):
                        cc = r * 4 + c
                        last = eng.matmul(
                            psS[:, c * 512:(c + 1) * 512],
                            qh_sb[:, qb * 128:(qb + 1) * 128],
                            kh_sb[:, cc * 512:(cc + 1) * 512],
                            start=True, stop=True)
                    last.then_inc(s_pe.h, 1)
                    s_pe.n += 1

        def emit_trctx(h, g):
            base_dv = dv1(h, g)
            for j in range(4):
                for tg in range(4):
                    eng.wait_ge(s_ac.h, ac1(h, g) + 1 + 2 * j + 2)
                    if h + g + j + tg > 0:
                        eng.wait_ge(s_dv.h,
                                    base_dv + 8 + SEL + 7 * j + 2 + tg - 1)
                    last = None
                    for t8 in range(8):
                        kt = tg * 8 + t8
                        last = eng.transpose(
                            psT[:, (tg % 2) * 1024 + t8 * 128:
                                (tg % 2) * 1024 + (t8 + 1) * 128],
                            s4_sb[:, (g * 4 + j) % 4 * S + kt * 128:
                                  (g * 4 + j) % 4 * S + (kt + 1) * 128],
                            idn_sb[:])
                    last.then_inc(s_pe.h, 1)
                    s_pe.n += 1
                eng.wait_ge(s_dv.h, base_dv + 8 + SEL + 7 * j + 6)
                last = None
                for kt in range(32):
                    last = eng.matmul(psC,
                                      vt_sb[:, kt * 128:(kt + 1) * 128],
                                      pmt_sb[:, kt * 128:(kt + 1) * 128],
                                      start=(kt == 0), stop=(kt == 31))
                last.then_inc(s_pe.h, 1)
                s_pe.n += 1

        for h in range(NH):
            for i in range(3):
                stHL.wait(eng, 3 * h + i)
            for g in range(8):
                assert s_pe.n == sbase(h, g), (h, g, s_pe.n)
                emit_scores(h, g)
                assert s_pe.n == tbase(h, g), (h, g, s_pe.n)
                emit_trctx(h, g)
    assert s_pe.n == PE_QKV + NH * 8 * PE_G

    @block.vector
    def _(eng):
        for h in range(NH):
            for g in range(8):
                base_pe = pe1(h, g)
                base_dv = dv1(h, g)
                for j in range(4):
                    for r in range(2):
                        eng.wait_ge(s_pe.h, sbase(h, g) + 2 * j + r + 1)
                        acc = (accr0 if r == 0 else accr1)[:, j:j + 1]
                        eng.tensor_scalar(
                            out=s4_sb[:, j * S + r * 2048:j * S + r * 2048 + 2048],
                            in0=psS, scalar1=1.0, scalar2=0.0,
                            op0=ALU.mult, op1=ALU.add,
                            accum_out=acc).then_inc(s_dv.h, 1)
                        s_dv.n += 1
                eng.wait_ge(s_dv.h, dv1(h, g) + 8)
                seq(eng, s_dq, eng.tensor_tensor(
                    out=lo_c, in0=accr0, in1=accr1, op=ALU.add))
                seq(eng, s_dq, eng.tensor_scalar(
                    out=lo_c, in0=lo_c, scalar1=1.0 / 4096.0, scalar2=-2.0,
                    op0=ALU.mult, op1=ALU.add))
                nmid_marks[h * 8 + g] = s_dq.n
                lo01 = lo_c[:, 0:2]
                mid01 = mid_c[:, 0:2]
                sel01 = sel_c[:, 0:2]
                z01 = z_c[:, 0:2]
                for it in range(ITER):
                    w = 8.0 * 2.0 ** (-it)
                    seq(eng, s_dq, eng.tensor_scalar(
                        out=mid01, in0=lo01, scalar1=w, scalar2=None,
                        op0=ALU.add))
                    for j in range(2):
                        seq(eng, s_dq, eng.tensor_scalar(
                            out=e_sb[:], in0=s4_sb[:, j * S:(j + 1) * S],
                            scalar1=mid_c[:, j:j + 1], scalar2=0.0,
                            op0=ALU.is_ge, op1=ALU.add,
                            accum_out=cnt_c[:, j:j + 1]))
                    seq(eng, s_dq, eng.tensor_scalar(
                        out=sel01, in0=cnt_c[:, 0:2], scalar1=TOPK,
                        scalar2=None, op0=ALU.is_ge))
                    seq(eng, s_dq, eng.scalar_tensor_tensor(
                        out=lo01, in0=sel01, scalar=w, in1=lo01,
                        op0=ALU.mult, op1=ALU.add))
                    eng.nop().then_inc(s_dv.h, 1)
                    s_dv.n += 1
                seq(eng, s_dq, eng.tensor_scalar(
                    out=ntau_c[:, 0:2], in0=lo01, scalar1=-1.0, scalar2=None,
                    op0=ALU.mult))
                eng.nop().then_inc(s_dv.h, 1)
                s_dv.n += 1
                for j in range(4):
                    qb = g * 4 + j
                    eng.wait_ge(s_ac.h, ac1(h, g) + 1 + 2 * j + 1)
                    seq(eng, s_dq, eng.scalar_tensor_tensor(
                        out=s4_sb[:, j * S:(j + 1) * S], in0=e_sb[:],
                        scalar=1.0, in1=e_sb[:],
                        op0=ALU.is_ge, op1=ALU.mult,
                        accum_out=z_c[:, j:j + 1]))
                    eng.nop().then_inc(s_dv.h, 1)
                    s_dv.n += 1
                    seq(eng, s_dq, eng.reciprocal(iz_c[:, j:j + 1],
                                                  z_c[:, j:j + 1]))
                    eng.nop().then_inc(s_dv.h, 1)
                    s_dv.n += 1
                    for tg in range(4):
                        eng.wait_ge(s_pe.h,
                                    tbase(h, g) + 5 * j + tg + 1)
                        eng.tensor_copy(
                            pmt_sb[:, tg * 1024:(tg + 1) * 1024],
                            psT[:, (tg % 2) * 1024:(tg % 2) * 1024 + 1024]
                        ).then_inc(s_dv.h, 1)
                        s_dv.n += 1
                    eng.wait_ge(s_pe.h, tbase(h, g) + 5 * j + 5)
                    stCO.wait(eng, (h * 32 + qb) - 2)
                    eng.tensor_copy(ctx_sb[:, (qb % 2) * 128:
                                           (qb % 2) * 128 + 128],
                                    psC).then_inc(s_dv.h, 1)
                    s_dv.n += 1
    assert s_dv.n == DV_QKV + NH * 8 * DV_G, s_dv.n

    @block.scalar
    def _(eng):
        for h in range(NH):
            for g in range(8):
                eng.wait_ge(s_dv.h, dv1(h, g) + 8)
                eng.wait_ge(s_dq.h, nmid_marks[h * 8 + g])
                lo23 = lo_c[:, 2:4]
                lo23b = selw[:, 46:48]
                cur = lo23
                for it in range(ITER):
                    w = 8.0 * 2.0 ** (-it)
                    nxt = lo23b if cur is lo23 else lo23
                    seq(eng, s_aq, eng.activation(
                        out=nmid_c[:, 2:4], in_=cur,
                        func=AF.Identity, scale=-1.0, bias=-8.0))
                    seq(eng, s_aq, eng.activation(
                        out=x_sb[:, 0:4096], in_=s4_sb[:, 2 * S:3 * S],
                        func=AF.Sign, bias=nmid_c[:, 2:3],
                        accum_out=sgn_c[:, 0:1]))
                    seq(eng, s_aq, eng.activation(
                        out=x_sb[:, 4096:8192], in_=s4_sb[:, 3 * S:4 * S],
                        func=AF.Sign, bias=nmid_c[:, 3:4],
                        accum_out=sgn_c[:, 1:2]))
                    seq(eng, s_aq, eng.activation(
                        out=sel_c[:, 2:4], in_=sgn_c,
                        func=AF.Sign, scale=0.5, bias=1229.25))
                    seq(eng, s_aq, eng.activation(
                        out=nxt[:, 0:1], in_=sel_c[:, 2:3],
                        func=AF.Identity, scale=0.5 * w, bias=cur[:, 0:1]))
                    seq(eng, s_aq, eng.activation(
                        out=nxt[:, 1:2], in_=sel_c[:, 3:4],
                        func=AF.Identity, scale=0.5 * w, bias=cur[:, 1:2]))
                    cur = nxt
                seq(eng, s_aq, eng.activation(
                    out=ntau_c[:, 2:4], in_=cur,
                    func=AF.Identity, scale=-1.0, bias=-OFF12))
                eng.nop().then_inc(s_ac.h, 1)
                s_ac.n += 1
                for j in range(4):
                    qb = g * 4 + j
                    eng.wait_ge(s_dv.h, dv1(h, g) + 8 + SEL)
                    if j >= 1:
                        eng.wait_ge(s_dv.h, dv1(h, g) + 8 + SEL + 7 * (j - 1) + 1)
                    eng.activation(out=e_sb[:], in_=s4_sb[:, j * S:(j + 1) * S],
                                   func=AF.Exp,
                                   bias=ntau_c[:, j:j + 1]).then_inc(s_ac.h, 1)
                    s_ac.n += 1
                    eng.wait_ge(s_dv.h, dv1(h, g) + 8 + SEL + 7 * j + 2)
                    eng.activation(out=s4_sb[:, j * S:(j + 1) * S],
                                   in_=s4_sb[:, j * S:(j + 1) * S],
                                   func=AF.Identity,
                                   scale=iz_c[:, j:j + 1]).then_inc(s_ac.h, 1)
                    s_ac.n += 1
                    eng.wait_ge(s_dv.h, dv1(h, g) + 8 + SEL + 7 * j + 7)
                    stCO.tick(eng.dma_start(
                        out=ctx_out[h * 128:(h + 1) * 128,
                                    qb * 128:(qb + 1) * 128],
                        in_=ctx_sb[:, (qb % 2) * 128:(qb % 2) * 128 + 128]))
        stCO.wait_all(eng)
    assert s_ac.n == AC_QKV + NH * 8 * AC_G

    return k


def build_l1a(T=1024):
    k = K()
    nc = k.nc
    NM = 32

    x_in = k.din("x_in", [4096, T], F32)
    nw_in = k.din("nw_in", [128, 32], F32)
    h_out = k.dout("h_out", [4096, T], BF16)
    t_out = k.dout("t_out", [4096, T], BF16)

    cdram, csb = k.setup_consts([EPS, 1.0 / 4096.0, -1.0, -0.5])

    t_sb = k.sbuf("t_sb", [128, 32 * T], BF16)
    af = k.sbuf("af1", [128, 6 * T], F32)
    sq_t = k.sbuf("sq_t1", [128, T], BF16)
    hsc = k.sbuf("hsc1", [128, T], BF16)
    nw_sb = k.sbuf("nw_sb", [128, 32], F32)
    ones_bf = k.sbuf("ones_bf1", [128, 1], BF16)
    ones_f = k.sbuf("ones_f1", [128, 1], F32)
    onesr_f = k.sbuf("onesr_f1", [1, 128], F32)
    bc1_b = k.sbuf("bc1_b1", [128, T], BF16)
    bc2_b = k.sbuf("bc2_b1", [128, T], BF16)

    x_t = [af[:, 0:T], af[:, T:2 * T]]
    r_mean = af[0:1, 3 * T:4 * T]
    r_rstd = af[0:1, 4 * T:5 * T]
    r_tmp = af[0:1, 5 * T:6 * T]

    psA = k.psum("psA1", [128, T], F32)
    psB = k.psum("psB1", [128, T], F32)
    ps_sum = k.psum("ps_sum1", [1, T], F32)
    ps_sq = k.psum("ps_sq1", [1, T], F32)

    s_pe = k.sem("s_pe")
    s_dv = k.sem("s_dv")
    s_ac = k.sem("s_ac")
    s_dq = k.sem("s_dq")
    s_aq = k.sem("s_aq")
    s_in = k.sem("s_in")
    stX = Stream(k, "stX")
    stPre = Stream(k, "stPre", depth=1)
    stOut = Stream(k, "stOut")
    stT = Stream(k, "stT")
    block = k.block()


    @block.sync
    def _(eng):
        stPre.tick(eng.dma_start(out=csb[:], in_=cdram[:]))
        stPre.tick(eng.dma_start(out=nw_sb[:], in_=nw_in[:]))
        for m in range(NM):
            if m >= 2:
                eng.wait_ge(s_pe.h, m - 1)
            stX.tick(eng.dma_start(out=x_t[m % 2][:],
                                   in_=x_in[m * 128:(m + 1) * 128, :]))
        for m in range(NM):
            if m < 2:
                eng.wait_ge(s_pe.h, NM)
            else:
                eng.wait_ge(s_dv.h, 3 + m - 1)
            stX.tick(eng.dma_start(out=x_t[m % 2][:],
                                   in_=x_in[m * 128:(m + 1) * 128, :]))

    @block.gpsimd
    def _(eng):
        eng.memset(ones_bf[:], 1.0)
        eng.memset(ones_f[:], 1.0)
        eng.memset(onesr_f[:], 1.0).then_inc(s_in.h, 1)
        s_in.n += 1

    @block.scalar
    def _(eng):
        for m in range(NM):
            stX.wait(eng, m)
            if m >= 1:
                eng.wait_ge(s_pe.h, m)
            eng.activation(out=sq_t[:], in_=x_t[m % 2][:],
                           func=AF.Square).then_inc(s_ac.h, 1)
            s_ac.n += 1
        eng.wait_ge(s_dv.h, 1)
        seq(eng, s_aq, eng.activation(out=r_tmp[:], in_=r_rstd[:],
                                      func=AF.Ln, bias=EPS))
        eng.activation(out=r_rstd[:], in_=r_tmp[:], func=AF.Exp,
                       scale=-0.5).then_inc(s_ac.h, 1)
        s_ac.n += 1
        for m in range(NM):
            eng.wait_ge(s_dv.h, 3 + m + 1)
            if m >= 1:
                eng.wait_ge(s_pe.h, 33 + m)
            eng.activation(out=sq_t[:], in_=t_sb[:, m * T:(m + 1) * T],
                           func=AF.Square).then_inc(s_ac.h, 1)
            s_ac.n += 1
        eng.wait_ge(s_dv.h, 3 + 32 + 1)
        seq(eng, s_aq, eng.activation(out=r_tmp[:], in_=r_rstd[:],
                                      func=AF.Ln, bias=EPS))
        eng.activation(out=r_rstd[:], in_=r_tmp[:], func=AF.Exp,
                       scale=-0.5).then_inc(s_ac.h, 1)
        s_ac.n += 1

    @block.tensor
    def _(eng):
        stPre.wait_all(eng)
        eng.wait_ge(s_in.h, 1)
        for m in range(NM):
            eng.wait_ge(s_ac.h, m + 1)
            stX.wait(eng, m)
            last = None
            for half in range(2):
                sl = slice(half * 512, half * 512 + 512)
                eng.matmul(ps_sum[:, sl], ones_f[:, 0:1], x_t[m % 2][:, sl],
                           start=(m == 0), stop=(m == NM - 1))
                last = eng.matmul(ps_sq[:, sl], ones_bf[:, 0:1], sq_t[:, sl],
                                  start=(m == 0), stop=(m == NM - 1))
            last.then_inc(s_pe.h, 1)
            s_pe.n += 1
        eng.wait_ge(s_dv.h, 2)
        eng.matmul(psA[:, 0:512], onesr_f[:1, :], r_rstd[:, 0:512],
                   start=True, stop=True)
        eng.matmul(psA[:, 512:T], onesr_f[:1, :], r_rstd[:, 512:T],
                   start=True, stop=True)
        eng.matmul(psB[:, 0:512], onesr_f[:1, :], r_tmp[:, 0:512],
                   start=True, stop=True)
        eng.matmul(psB[:, 512:T], onesr_f[:1, :], r_tmp[:, 512:T],
                   start=True, stop=True).then_inc(s_pe.h, 1)
        s_pe.n += 1
        for m in range(NM):
            eng.wait_ge(s_ac.h, 33 + m + 1)
            eng.wait_ge(s_dv.h, 3 + m + 1)
            if m == 0:
                eng.wait_ge(s_dv.h, 3 + 1)
            last = None
            for half in range(2):
                sl = slice(half * 512, half * 512 + 512)
                last = eng.matmul(ps_sq[:, sl], ones_bf[:, 0:1], sq_t[:, sl],
                                  start=(m == 0), stop=(m == NM - 1))
            last.then_inc(s_pe.h, 1)
            s_pe.n += 1
        eng.wait_ge(s_dv.h, 3 + 32 + 2)
        eng.matmul(psA[:, 0:512], onesr_f[:1, :], r_rstd[:, 0:512],
                   start=True, stop=True)
        eng.matmul(psA[:, 512:T], onesr_f[:1, :], r_rstd[:, 512:T],
                   start=True, stop=True).then_inc(s_pe.h, 1)
        s_pe.n += 1

    @block.vector
    def _(eng):
        eng.wait_ge(s_pe.h, NM)
        seq(eng, s_dq, eng.tensor_scalar(
            out=r_mean[:], in0=ps_sum[:], scalar1=1.0 / 4096.0,
            scalar2=None, op0=ALU.mult))
        seq(eng, s_dq, eng.tensor_scalar(
            out=r_tmp[:], in0=ps_sq[:], scalar1=1.0 / 4096.0,
            scalar2=None, op0=ALU.mult))
        seq(eng, s_dq, eng.tensor_tensor(
            out=r_rstd[:], in0=r_mean[:], in1=r_mean[:], op=ALU.mult))
        eng.tensor_tensor(out=r_rstd[:], in0=r_tmp[:], in1=r_rstd[:],
                          op=ALU.subtract).then_inc(s_dv.h, 1)
        s_dv.n += 1
        eng.wait_ge(s_ac.h, 33)
        seq(eng, s_dq, eng.tensor_tensor(out=r_tmp[:], in0=r_mean[:],
                                         in1=r_rstd[:], op=ALU.mult))
        eng.tensor_scalar(out=r_tmp[:], in0=r_tmp[:], scalar1=-1.0,
                          scalar2=None, op0=ALU.mult).then_inc(s_dv.h, 1)
        s_dv.n += 1
        eng.wait_ge(s_pe.h, NM + 1)
        seq(eng, s_dq, eng.tensor_copy(bc1_b[:], psA[:]))
        seq(eng, s_dq, eng.tensor_copy(bc2_b[:], psB[:]))
        eng.nop().then_inc(s_dv.h, 1)
        s_dv.n += 1
        for m in range(NM):
            stX.wait(eng, NM + m)
            if m >= 1:
                eng.wait_ge(s_dv.h, 3 + m)
            seq(eng, s_dq, eng.tensor_tensor(
                out=hsc[:], in0=x_t[m % 2][:], in1=bc1_b[:], op=ALU.mult))
            eng.tensor_tensor(out=t_sb[:, m * T:(m + 1) * T], in0=hsc[:],
                              in1=bc2_b[:], op=ALU.add).then_inc(s_dv.h, 1)
            s_dv.n += 1
        eng.wait_ge(s_pe.h, NM + 1 + NM)
        seq(eng, s_dq, eng.tensor_scalar(
            out=r_rstd[:], in0=ps_sq[:], scalar1=1.0 / 4096.0,
            scalar2=None, op0=ALU.mult))
        eng.nop().then_inc(s_dv.h, 1)
        s_dv.n += 1
        eng.wait_ge(s_ac.h, 66)
        eng.nop().then_inc(s_dv.h, 1)
        s_dv.n += 1
        eng.wait_ge(s_pe.h, NM + 1 + NM + 1)
        seq(eng, s_dq, eng.tensor_copy(bc1_b[:], psA[:]))
        eng.nop().then_inc(s_dv.h, 1)
        s_dv.n += 1
        for m in range(NM):
            if m >= 1:
                stOut.wait(eng, m - 1)
            seq(eng, s_dq, eng.scalar_tensor_tensor(
                out=hsc[:], in0=t_sb[:, m * T:(m + 1) * T],
                scalar=nw_sb[:, m:m + 1], in1=bc1_b[:],
                op0=ALU.mult, op1=ALU.mult))
            eng.nop().then_inc(s_dv.h, 1)
            s_dv.n += 1

    @block.scalar
    def _(eng):
        for m in range(NM):
            eng.wait_ge(s_dv.h, 3 + 32 + 3 + m + 1)
            stOut.tick(eng.dma_start(out=h_out[m * 128:(m + 1) * 128, :],
                                     in_=hsc[:]))
            stT.tick(eng.dma_start(out=t_out[m * 128:(m + 1) * 128, :],
                                   in_=t_sb[:, m * T:(m + 1) * T]))
        stOut.wait_all(eng)
        stT.wait_all(eng)

    return k


def build_l1b():
    k = K()
    nc = k.nc
    NMT = 19
    GRPS = [(0, 3), (3, 3), (6, 3), (9, 3), (12, 3), (15, 3), (18, 1)]
    NG = len(GRPS)
    TB = 1024
    NTB = 8
    NH = 16
    BIG = -1.0e30

    h_in = k.din("h_in", [128, 32 * 8192], BF16)
    wg = k.din("wg", [19, 128, 32 * 128], F32)
    convw = k.din("convw", [128, 40], F32)
    convb = k.din("convb", [128, 10], F32)
    dtb = k.din("dtb", [128, 1], F32)
    alog = k.din("alog", [128, 1], F32)
    dp_row = k.din("dp_row", [1, 16], F32)
    gnw = k.din("gnw", [128, 8], F32)
    triu = k.din("triu", [128, 128], F32)
    mtril = k.din("mtril", [128, 128], F32)
    idn_f = k.din("idn_f", [128, 128], F32)
    idn_b = k.din("idn_b", [128, 128], BF16)
    y_out = k.dout("y_out", [1024, 8192], BF16)

    cdram, csb = k.setup_consts([EPS, 1.0 / 1024.0, -1.0, -0.5, 1.0])

    proj = k.sbuf("proj", [128, NMT * TB], BF16)
    wbuf = k.sbuf("wbufM", [128, 2 * 3 * 4096], BF16)
    hk = k.sbuf("hkM", [128, 2 * TB], BF16)
    cin = k.sbuf("cin", [128, 1027 + 1], BF16)
    tails = k.sbuf("tails", [128, 30], BF16)
    acc_f = k.sbuf("acc_f", [128, TB], F32)
    convw_sb = k.sbuf("convw_sb", [128, 40], F32)
    convb_sb = k.sbuf("convb_sb", [128, 10], F32)
    dtb_sb = k.sbuf("dtb_sb", [128, 1], F32)
    acol_sb = k.sbuf("acol_sb", [128, 2], F32)
    dtf = k.sbuf("dtf", [128, TB], F32)
    dtp_f = k.sbuf("dtp_f", [128, TB], F32)
    adt_f = k.sbuf("adt_f", [128, TB], F32)
    dp_sb = k.sbuf("dp_sb", [1, 16], F32)
    dpbc_sb = k.sbuf("dpbc_sb", [128, 16], F32)
    gnw_sb = k.sbuf("gnw_sb", [128, 8], F32)
    triu_sb = k.sbuf("triu_sb", [128, 128], F32)
    mtril_sb = k.sbuf("mtril_sb", [128, 128], F32)
    idnf_sb = k.sbuf("idnf_sb", [128, 128], F32)
    idnb_sb = k.sbuf("idnb_sb", [128, 128], BF16)
    adtT_sb = k.sbuf("adtT_sb", [128, 16], F32)
    dtpT_sb = k.sbuf("dtpT_sb", [128, 16], F32)
    acsT_sb = k.sbuf("acsT_sb", [128, 16], F32)
    nacsT_sb = k.sbuf("nacsT_sb", [128, 16], F32)
    acsbc_sb = k.sbuf("acsbc_sb", [128, 16], F32)
    dsT_sb = k.sbuf("dsT_sb", [128, 16], F32)
    eAT_sb = k.sbuf("eAT_sb", [128, 16], F32)
    egbc_sb = k.sbuf("egbc_sb", [128, 16], F32)
    acsl_sb = k.sbuf("acsl_sb", [128, 1], F32)
    rowh_sb = k.sbuf("rowh_sb", [1, 128], F32)
    junk16 = k.sbuf("junk16", [128, 128], F32)
    MT_b = k.sbuf("MT_b", [128, 128], BF16)
    LT_b = k.sbuf("LT_b", [128, 128], BF16)
    MLT_b = k.sbuf("MLT_b", [128, 128], BF16)
    BT_sb = k.sbuf("BT_sb", [128, 128], BF16)
    BTds_b = k.sbuf("BTds_b", [128, 128], BF16)
    hidT_b = k.sbuf("hidT_b", [128, 64], BF16)
    hdtT_b = k.sbuf("hdtT_b", [128, 64], BF16)
    ys1_b = k.sbuf("ys1_b", [128, 64], BF16)
    yT_b = k.sbuf("yT_b", [128, 64], BF16)
    prev_f = k.sbuf("prev_f", [128, NH * 64], F32)
    prev_b = k.sbuf("prev_b", [128, NH * 64], BF16)
    y_asm = k.sbuf("y_asm", [128, 8 * TB], BF16)
    grow = k.sbuf("grow", [1, 2 * TB], F32)
    gbc_b = k.sbuf("gbc_b", [128, TB], BF16)
    gsq_b = k.sbuf("gsq_b", [128, TB], BF16)
    yst_b = k.sbuf("yst_b", [128, 2 * TB], BF16)
    gsil_b = k.sbuf("gsil_b", [128, 2 * TB], BF16)
    ones_bf = k.sbuf("ones_bfM", [128, 1], BF16)
    onesr_f = k.sbuf("onesr_fM", [1, 128], F32)

    g_ms = grow[0:1, 0:TB]
    g_tmp = grow[0:1, TB:2 * TB]

    psP = k.psum("psP", [128, 2048], F32)
    psQ2 = k.psum("psQ2", [128, 1024], F32)
    psR2 = k.psum("psR2", [128, 512], F32)
    psT_b = k.psum("psT_bM", [128, 1024], BF16)
    ps_bc = psP[:, 0:128]
    ps_mt = psP[:, 512:640]
    ps_yd = psP[:, 1024:1152]
    ps_yo = psP[:, 1536:1664]
    ps_s = psQ2[:, 0:128]
    ps_m16 = psQ2[:, 512:640]
    ps_row = psR2[0:1, 0:128]
    ps_gs = psQ2[0:1, 0:1024]
    ps_gb = psP[:, 0:1024]

    s_pe = k.sem("s_pe")
    s_dv = k.sem("s_dv")
    s_ac = k.sem("s_ac")
    s_dq = k.sem("s_dq")
    s_aq = k.sem("s_aq")
    s_in = k.sem("s_in")
    stPre = Stream(k, "stPre", depth=1)
    stWg = Stream(k, "stWg", depth=6)
    stHk = Stream(k, "stHk", depth=2)
    stYo = Stream(k, "stYo", depth=2)
    block = k.block()

    PE_TB = 224 + 696 + 9
    DV_TB = 7 + 10 + 1 + 696 + 8 + 10
    AC_TB = 10 + 1 + 136 + 8 + 9
    PE0, DV0, AC0 = 1, 2, 1

    def peB(tb):
        return PE0 + tb * PE_TB

    def dvB(tb):
        return DV0 + tb * DV_TB

    def acB(tb):
        return AC0 + tb * AC_TB

    wg_idx = {}
    _ct = 0
    for _tb in range(NTB):
        for _gi, (_m0, _gl) in enumerate(GRPS):
            for _mm in range(_gl):
                wg_idx[(_tb, _gi, _mm)] = _ct
                _ct += 1

    def peD(tb, c8):
        return peB(tb) + 224 + c8 * 87

    def dvD(tb, c8):
        return dvB(tb) + 18 + c8 * 87

    def acD(tb, c8):
        return acB(tb) + 11 + c8 * 17

    @block.sync
    def _(eng):
        stPre.tick(eng.dma_start(out=csb[:], in_=cdram[:]))
        stPre.tick(eng.dma_start(out=convw_sb[:], in_=convw[:]))
        stPre.tick(eng.dma_start(out=convb_sb[:], in_=convb[:]))
        stPre.tick(eng.dma_start(out=dtb_sb[:], in_=dtb[:]))
        stPre.tick(eng.dma_start(out=acol_sb[:, 0:1], in_=alog[:]))
        stPre.tick(eng.dma_start(out=dp_sb[:], in_=dp_row[:]))
        stPre.tick(eng.dma_start(out=gnw_sb[:], in_=gnw[:]))
        stPre.tick(eng.dma_start(out=triu_sb[:], in_=triu[:]))
        stPre.tick(eng.dma_start(out=mtril_sb[:], in_=mtril[:]))
        stPre.tick(eng.dma_start(out=idnf_sb[:], in_=idn_f[:]))
        stPre.tick(eng.dma_start(out=idnb_sb[:], in_=idn_b[:]))

    @block.gpsimd
    def _(eng):
        eng.memset(ones_bf[:], 1.0)
        eng.memset(onesr_f[:], 1.0).then_inc(s_in.h, 1)
        s_in.n += 1

    @block.scalar
    def _(eng):
        stPre.wait_all(eng)
        eng.activation(out=acol_sb[:, 1:2], in_=acol_sb[:, 0:1],
                       func=AF.Exp).then_inc(s_ac.h, 1)
        s_ac.n += 1

    @block.tensor
    def _(eng):
        stPre.wait_all(eng)
        eng.wait_ge(s_in.h, 1)
        eng.matmul(ps_m16[:, 0:16], onesr_f[:1, :], dp_sb[:1, :],
                   start=True, stop=True).then_inc(s_pe.h, 1)
        s_pe.n += 1

    @block.vector
    def _(eng):
        eng.wait_ge(s_pe.h, 1)
        eng.wait_ge(s_ac.h, 1)
        seq(eng, s_dq, eng.tensor_copy(dpbc_sb[:], ps_m16[:, 0:16]))
        eng.nop().then_inc(s_dv.h, 1)
        s_dv.n += 1
        seq(eng, s_dq, eng.tensor_scalar(out=acol_sb[:, 1:2],
                                         in0=acol_sb[:, 1:2], scalar1=-1.0,
                                         scalar2=None, op0=ALU.mult))
        seq(eng, s_dq, eng.memset(prev_f[:], 0.0))
        seq(eng, s_dq, eng.memset(prev_b[:], 0.0))
        seq(eng, s_dq, eng.memset(tails[:], 0.0))
        eng.nop().then_inc(s_dv.h, 1)
        s_dv.n += 1
    assert (s_pe.n, s_dv.n, s_ac.n) == (PE0, DV0, AC0)

    @block.gpsimd
    def _(eng):
        for tb in range(NTB):
            for gi, (m0, gl) in enumerate(GRPS):
                u = tb * NG + gi
                if u >= 2:
                    tb2, g2 = divmod(u - 2, NG)
                    eng.wait_ge(s_pe.h, peB(tb2) + (g2 + 1) * 32)
                for mm in range(gl):
                    stWg.tick(eng.dma_start(
                        out=wbuf[:, (u % 2) * 12288 + mm * 4096:
                                 (u % 2) * 12288 + (mm + 1) * 4096],
                        in_=wg[m0 + mm]))

    @block.sync
    def _(eng):
        for tb in range(NTB):
            for gi in range(NG):
                for kk in range(32):
                    i = (tb * NG + gi) * 32 + kk
                    if i >= 2:
                        tb2, r2 = divmod(i - 2, NG * 32)
                        eng.wait_ge(s_pe.h, peB(tb2) + r2 + 1)
                    stHk.tick(eng.dma_start(
                        out=hk[:, (i % 2) * TB:(i % 2) * TB + TB],
                        in_=h_in[:, kk * 8192 + tb * TB:
                                 kk * 8192 + (tb + 1) * TB]))

    @block.tensor
    def _(eng):
        for tb in range(NTB):
            bnew = (tb % 4 == 0)
            for gi, (m0, gl) in enumerate(GRPS):
                u = tb * NG + gi
                for mm in range(gl):
                    stWg.wait(eng, wg_idx[(tb, gi, mm)])
                if gi == 0:
                    if tb >= 1:
                        eng.wait_ge(s_dv.h, dvB(tb - 1) + DV_TB)
                    else:
                        eng.wait_ge(s_dv.h, DV0)
                else:
                    eng.wait_ge(s_dv.h, dvB(tb) + gi)
                for kk in range(32):
                    i = u * 32 + kk
                    stHk.wait(eng, i)
                    last = None
                    for mm in range(gl):
                        pv = (psP[:, 0:1024] if mm == 0 else
                              psP[:, 1024:2048] if mm == 1 else psQ2[:, :])
                        for half in range(2):
                            last = eng.matmul(
                                pv[:, half * 512:half * 512 + 512],
                                wbuf[:, (u % 2) * 12288 + mm * 4096 +
                                     kk * 128:(u % 2) * 12288 + mm * 4096 +
                                     (kk + 1) * 128],
                                hk[:, (i % 2) * TB + half * 512:
                                   (i % 2) * TB + half * 512 + 512],
                                start=(kk == 0), stop=(kk == 31))
                    last.then_inc(s_pe.h, 1)
                    s_pe.n += 1
            for c8 in range(8):
                pe_base = peD(tb, c8)
                dv_base = dvD(tb, c8)
                ac_base = acD(tb, c8)
                t0 = c8 * 128
                Bt = proj[:, 8 * TB + t0:8 * TB + t0 + 128]
                Ct = proj[:, 9 * TB + t0:9 * TB + t0 + 128]
                eng.wait_ge(s_dv.h, dvB(tb) + NG + 10 + 1)
                if tb + c8 > 0:
                    eng.wait_ge(s_dv.h, dv_base)
                eng.transpose(ps_m16[:, 0:16], adt_f[0:16, t0:t0 + 128],
                              idnf_sb[0:16, 0:16]).then_inc(s_pe.h, 1)
                s_pe.n += 1
                eng.wait_ge(s_dv.h, dv_base + 1)
                eng.transpose(ps_m16[:, 0:16], dtp_f[0:16, t0:t0 + 128],
                              idnf_sb[0:16, 0:16]).then_inc(s_pe.h, 1)
                s_pe.n += 1
                eng.wait_ge(s_dv.h, dv_base + 2)
                eng.matmul(ps_m16[:, 0:16], triu_sb[:], adtT_sb[:, 0:16],
                           start=True, stop=True).then_inc(s_pe.h, 1)
                s_pe.n += 1
                eng.wait_ge(s_dv.h, dv_base + 3)
                eng.transpose(ps_row[0:1, 0:16], acsl_sb[0:16, 0:1],
                              idnf_sb[0:16, 0:16]).then_inc(s_pe.h, 1)
                s_pe.n += 1
                eng.wait_ge(s_dv.h, dv_base + 4)
                eng.matmul(ps_m16[:, 0:16], onesr_f[:1, :],
                           rowh_sb[:1, 0:16],
                           start=True, stop=True).then_inc(s_pe.h, 1)
                s_pe.n += 1
                eng.wait_ge(s_dv.h, dv_base + 5)
                eng.matmul(ps_mt[:], Bt, Ct,
                           start=True, stop=True).then_inc(s_pe.h, 1)
                s_pe.n += 1
                eng.wait_ge(s_dv.h, dv_base + 6)
                eng.transpose(psT_b[:, 0:128], Bt,
                              idnb_sb[:]).then_inc(s_pe.h, 1)
                s_pe.n += 1
                for h in range(NH):
                    hb = dv_base + 7 + h * 5
                    pb = pe_base + 7 + h * 5
                    eng.wait_ge(s_dv.h, dv_base + 7)
                    if h >= 1:
                        eng.wait_ge(s_dv.h, hb - 4)
                    eng.matmul(ps_row[0:1, :], adtT_sb[:, h:h + 1],
                               triu_sb[:],
                               start=True, stop=True).then_inc(s_pe.h, 1)
                    s_pe.n += 1
                    eng.wait_ge(s_dv.h, hb + 1)
                    if h >= 1:
                        eng.wait_ge(s_ac.h, ac_base + 1 + h)
                    eng.matmul(ps_bc[:], onesr_f[:1, :], rowh_sb[:1, :],
                               start=True, stop=False)
                    eng.matmul(ps_bc[:], idnf_sb[:], mtril_sb[:],
                               start=False, stop=True).then_inc(s_pe.h, 1)
                    s_pe.n += 1
                    eng.wait_ge(s_dv.h, hb + 2)
                    mt = h // 2
                    rb = (h % 2) * 64
                    eng.transpose(psT_b[:, 256:320],
                                  proj[rb:rb + 64, mt * TB + t0:
                                       mt * TB + t0 + 128],
                                  idnb_sb[rb:rb + 64, rb:rb + 64]
                                  ).then_inc(s_pe.h, 1)
                    s_pe.n += 1
                    eng.wait_ge(s_dv.h, hb + 3)
                    eng.matmul(ps_yd[:, 0:64], MLT_b[:], hdtT_b[:],
                               start=True, stop=True)
                    eng.matmul(ps_yo[:, 0:64], Ct, prev_b[:, h * 64:
                                                          (h + 1) * 64],
                               start=True, stop=True)
                    eng.matmul(ps_s[:, 0:64], BTds_b[:], hdtT_b[:],
                               start=True, stop=True).then_inc(s_pe.h, 1)
                    s_pe.n += 1
                    eng.wait_ge(s_dv.h, hb + 4)
                    eng.transpose(psT_b[0:64, 512:640], yT_b[:],
                                  idnb_sb[:]).then_inc(s_pe.h, 1)
                    s_pe.n += 1
            for mt in range(8):
                eng.wait_ge(s_ac.h, acB(tb) + 11 + 136 + 8 + mt + 1)
                last = None
                for half in range(2):
                    sl = slice(half * 512, half * 512 + 512)
                    last = eng.matmul(ps_gs[:, sl], ones_bf[:, 0:1],
                                      gsq_b[:, sl], start=(mt == 0),
                                      stop=(mt == 7))
                last.then_inc(s_pe.h, 1)
                s_pe.n += 1
            eng.wait_ge(s_dv.h, dvB(tb) + 18 + 696 + 8 + 1)
            eng.wait_ge(s_ac.h, acB(tb) + 11 + 136 + 8 + 9)
            eng.matmul(ps_gb[:, 0:512], onesr_f[:1, :], g_ms[:, 0:512],
                       start=True, stop=True)
            eng.matmul(ps_gb[:, 512:1024], onesr_f[:1, :], g_ms[:, 512:1024],
                       start=True, stop=True).then_inc(s_pe.h, 1)
            s_pe.n += 1
    assert s_pe.n == PE0 + NTB * PE_TB, s_pe.n

    @block.vector
    def _(eng):
        for tb in range(NTB):
            if tb == 4:
                seq(eng, s_dq, eng.memset(prev_f[:], 0.0))
                seq(eng, s_dq, eng.memset(prev_b[:], 0.0))
                seq(eng, s_dq, eng.memset(tails[:], 0.0))
            for gi, (m0, gl) in enumerate(GRPS):
                eng.wait_ge(s_pe.h, peB(tb) + (gi + 1) * 32)
                for mm in range(gl):
                    pv = (psP[:, 0:1024] if mm == 0 else
                          psP[:, 1024:2048] if mm == 1 else psQ2[:, :])
                    m = m0 + mm
                    seq(eng, s_dq, eng.tensor_copy(
                        proj[:, m * TB:(m + 1) * TB], pv[:]))
                eng.nop().then_inc(s_dv.h, 1)
                s_dv.n += 1
            for ct in range(10):
                if ct >= 1:
                    eng.wait_ge(s_ac.h, acB(tb) + ct)
                seq(eng, s_dq, eng.tensor_copy(cin[:, 0:3],
                                               tails[:, ct * 3:ct * 3 + 3]))
                seq(eng, s_dq, eng.tensor_copy(cin[:, 3:1027],
                                               proj[:, ct * TB:(ct + 1) * TB]))
                seq(eng, s_dq, eng.tensor_copy(tails[:, ct * 3:ct * 3 + 3],
                                               cin[:, 1024:1027]))
                seq(eng, s_dq, eng.tensor_scalar(
                    out=acc_f[:], in0=cin[:, 0:1024],
                    scalar1=convw_sb[:, ct * 4:ct * 4 + 1], scalar2=None,
                    op0=ALU.mult))
                for tap in range(1, 4):
                    seq(eng, s_dq, eng.scalar_tensor_tensor(
                        out=acc_f[:], in0=cin[:, tap:tap + 1024],
                        scalar=convw_sb[:, ct * 4 + tap:ct * 4 + tap + 1],
                        in1=acc_f[:], op0=ALU.mult, op1=ALU.add))
                eng.nop().then_inc(s_dv.h, 1)
                s_dv.n += 1
            eng.wait_ge(s_ac.h, acB(tb) + 11)
            seq(eng, s_dq, eng.tensor_scalar(
                out=adt_f[0:16, :], in0=dtp_f[0:16, :],
                scalar1=acol_sb[0:16, 1:2], scalar2=None, op0=ALU.mult))
            eng.nop().then_inc(s_dv.h, 1)
            s_dv.n += 1
            for c8 in range(8):
                dv_base = dvD(tb, c8)
                pe_base = peD(tb, c8)
                ac_base = acD(tb, c8)
                t0 = c8 * 128
                eng.wait_ge(s_pe.h, pe_base + 1)
                seq(eng, s_dq, eng.tensor_copy(adtT_sb[:, 0:16],
                                               ps_m16[:, 0:16]))
                eng.nop().then_inc(s_dv.h, 1)
                s_dv.n += 1
                eng.wait_ge(s_pe.h, pe_base + 2)
                seq(eng, s_dq, eng.tensor_copy(dtpT_sb[:, 0:16],
                                               ps_m16[:, 0:16]))
                eng.nop().then_inc(s_dv.h, 1)
                s_dv.n += 1
                eng.wait_ge(s_pe.h, pe_base + 3)
                seq(eng, s_dq, eng.tensor_copy(acsT_sb[:, 0:16],
                                               ps_m16[:, 0:16]))
                seq(eng, s_dq, eng.tensor_scalar(
                    out=nacsT_sb[:, 0:16], in0=acsT_sb[:, 0:16], scalar1=-1.0,
                    scalar2=None, op0=ALU.mult))
                seq(eng, s_dq, eng.tensor_scalar(
                    out=junk16[0:16, :], in0=adt_f[0:16, t0:t0 + 128],
                    scalar1=1.0, scalar2=0.0, op0=ALU.mult, op1=ALU.add,
                    accum_out=acsl_sb[0:16, 0:1]))
                eng.nop().then_inc(s_dv.h, 1)
                s_dv.n += 1
                eng.wait_ge(s_pe.h, pe_base + 4)
                seq(eng, s_dq, eng.tensor_copy(rowh_sb[0:1, 0:16],
                                               ps_row[0:1, 0:16]))
                eng.nop().then_inc(s_dv.h, 1)
                s_dv.n += 1
                eng.wait_ge(s_pe.h, pe_base + 5)
                seq(eng, s_dq, eng.tensor_copy(acsbc_sb[:, 0:16],
                                               ps_m16[:, 0:16]))
                seq(eng, s_dq, eng.tensor_tensor(
                    out=junk16[:, 0:16], in0=acsbc_sb[:, 0:16],
                    in1=acsT_sb[:, 0:16], op=ALU.subtract))
                eng.nop().then_inc(s_dv.h, 1)
                s_dv.n += 1
                eng.wait_ge(s_pe.h, pe_base + 6)
                seq(eng, s_dq, eng.tensor_copy(MT_b[:], ps_mt[:]))
                eng.nop().then_inc(s_dv.h, 1)
                s_dv.n += 1
                eng.wait_ge(s_pe.h, pe_base + 7)
                seq(eng, s_dq, eng.tensor_copy(BT_sb[:], psT_b[:, 0:128]))
                eng.nop().then_inc(s_dv.h, 1)
                s_dv.n += 1
                for h in range(NH):
                    hb = dv_base + 7 + h * 5
                    pb = pe_base + 7 + h * 5
                    eng.wait_ge(s_pe.h, pb + 1)
                    seq(eng, s_dq, eng.tensor_copy(rowh_sb[:], ps_row[:]))
                    eng.nop().then_inc(s_dv.h, 1)
                    s_dv.n += 1
                    eng.wait_ge(s_ac.h, ac_base + 1 + h + 1)
                    seq(eng, s_dq, eng.tensor_tensor(
                        out=MLT_b[:], in0=MT_b[:], in1=LT_b[:], op=ALU.mult))
                    eng.nop().then_inc(s_dv.h, 1)
                    s_dv.n += 1
                    eng.wait_ge(s_pe.h, pb + 3)
                    seq(eng, s_dq, eng.tensor_copy(hidT_b[:],
                                                   psT_b[:, 256:320]))
                    seq(eng, s_dq, eng.tensor_scalar(
                        out=hdtT_b[:], in0=hidT_b[:],
                        scalar1=dtpT_sb[:, h:h + 1], scalar2=None,
                        op0=ALU.mult))
                    seq(eng, s_dq, eng.tensor_scalar(
                        out=BTds_b[:], in0=BT_sb[:],
                        scalar1=dsT_sb[:, h:h + 1], scalar2=None,
                        op0=ALU.mult))
                    eng.nop().then_inc(s_dv.h, 1)
                    s_dv.n += 1
                    eng.wait_ge(s_pe.h, pb + 4)
                    seq(eng, s_dq, eng.scalar_tensor_tensor(
                        out=ys1_b[:], in0=hidT_b[:],
                        scalar=dpbc_sb[:, h:h + 1], in1=ps_yd[:, 0:64],
                        op0=ALU.mult, op1=ALU.add))
                    seq(eng, s_dq, eng.scalar_tensor_tensor(
                        out=yT_b[:], in0=ps_yo[:, 0:64],
                        scalar=eAT_sb[:, h:h + 1], in1=ys1_b[:],
                        op0=ALU.mult, op1=ALU.add))
                    seq(eng, s_dq, eng.scalar_tensor_tensor(
                        out=prev_f[:, h * 64:(h + 1) * 64],
                        in0=prev_f[:, h * 64:(h + 1) * 64],
                        scalar=egbc_sb[:, h:h + 1], in1=ps_s[:, 0:64],
                        op0=ALU.mult, op1=ALU.add))
                    seq(eng, s_dq, eng.tensor_copy(
                        prev_b[:, h * 64:(h + 1) * 64],
                        prev_f[:, h * 64:(h + 1) * 64]))
                    eng.nop().then_inc(s_dv.h, 1)
                    s_dv.n += 1
                    eng.wait_ge(s_pe.h, pb + 5)
                    mt = h // 2
                    rb = (h % 2) * 64
                    seq(eng, s_dq, eng.tensor_copy(
                        y_asm[rb:rb + 64, mt * TB + t0:mt * TB + t0 + 128],
                        psT_b[0:64, 512:640]))
                    eng.nop().then_inc(s_dv.h, 1)
                    s_dv.n += 1
            for mt in range(8):
                eng.wait_ge(s_ac.h, acB(tb) + 11 + 136 + mt + 1)
                seq(eng, s_dq, eng.tensor_tensor(
                    out=y_asm[:, mt * TB:(mt + 1) * TB],
                    in0=y_asm[:, mt * TB:(mt + 1) * TB],
                    in1=gsil_b[:, (mt % 2) * TB:(mt % 2) * TB + TB],
                    op=ALU.mult))
                eng.nop().then_inc(s_dv.h, 1)
                s_dv.n += 1
            eng.wait_ge(s_pe.h, peB(tb) + 224 + 696 + 8)
            seq(eng, s_dq, eng.tensor_scalar(
                out=g_ms[:], in0=ps_gs[:], scalar1=1.0 / 1024.0,
                scalar2=None, op0=ALU.mult))
            eng.nop().then_inc(s_dv.h, 1)
            s_dv.n += 1
            eng.wait_ge(s_pe.h, peB(tb) + PE_TB)
            seq(eng, s_dq, eng.tensor_copy(gbc_b[:], ps_gb[:, 0:1024]))
            eng.nop().then_inc(s_dv.h, 1)
            s_dv.n += 1
            for mt in range(8):
                stYo.wait(eng, tb * 8 + mt - 2)
                seq(eng, s_dq, eng.scalar_tensor_tensor(
                    out=yst_b[:, (mt % 2) * TB:(mt % 2) * TB + TB],
                    in0=y_asm[:, mt * TB:(mt + 1) * TB],
                    scalar=gnw_sb[:, mt:mt + 1], in1=gbc_b[:],
                    op0=ALU.mult, op1=ALU.mult))
                eng.nop().then_inc(s_dv.h, 1)
                s_dv.n += 1
    assert s_dv.n == DV0 + NTB * DV_TB, s_dv.n

    @block.scalar
    def _(eng):
        for tb in range(NTB):
            for ct in range(10):
                eng.wait_ge(s_dv.h, dvB(tb) + NG + ct + 1)
                eng.activation(out=proj[:, ct * TB:(ct + 1) * TB],
                               in_=acc_f[:], func=AF.Silu,
                               bias=convb_sb[:, ct:ct + 1]).then_inc(s_ac.h, 1)
                s_ac.n += 1
            seq(eng, s_aq, eng.activation(
                out=dtf[0:16, :], in_=proj[0:16, 10 * TB:11 * TB],
                func=AF.Exp, bias=dtb_sb[0:16, 0:1]))
            seq(eng, s_aq, eng.activation(
                out=dtp_f[0:16, :], in_=dtf[0:16, :], func=AF.Ln, bias=1.0))
            eng.nop().then_inc(s_ac.h, 1)
            s_ac.n += 1
            for c8 in range(8):
                dv_base = dvD(tb, c8)
                ac_base = acD(tb, c8)
                pe_base = peD(tb, c8)
                eng.wait_ge(s_dv.h, dv_base + 5)
                seq(eng, s_aq, eng.activation(out=dsT_sb[:, 0:16],
                                              in_=junk16[:, 0:16],
                                              func=AF.Exp))
                seq(eng, s_aq, eng.activation(out=eAT_sb[:, 0:16],
                                              in_=acsT_sb[:, 0:16],
                                              func=AF.Exp))
                seq(eng, s_aq, eng.activation(out=egbc_sb[:, 0:16],
                                              in_=acsbc_sb[:, 0:16],
                                              func=AF.Exp))
                eng.nop().then_inc(s_ac.h, 1)
                s_ac.n += 1
                for h in range(NH):
                    pb = pe_base + 7 + h * 5
                    eng.wait_ge(s_pe.h, pb + 2)
                    eng.activation(out=LT_b[:], in_=ps_bc[:], func=AF.Exp,
                                   bias=nacsT_sb[:, h:h + 1]
                                   ).then_inc(s_ac.h, 1)
                    s_ac.n += 1
            for mt in range(8):
                eng.wait_ge(s_dv.h, dvB(tb) + 18 + 696)
                if mt >= 2:
                    eng.wait_ge(s_dv.h, dvB(tb) + 18 + 696 + mt - 1)
                eng.activation(out=gsil_b[:, (mt % 2) * TB:(mt % 2) * TB + TB],
                               in_=proj[:, (11 + mt) * TB:(12 + mt) * TB],
                               func=AF.Silu).then_inc(s_ac.h, 1)
                s_ac.n += 1
            for mt in range(8):
                eng.wait_ge(s_dv.h, dvB(tb) + 18 + 696 + mt + 1)
                if mt >= 1:
                    eng.wait_ge(s_pe.h, peB(tb) + 224 + 696 + mt)
                eng.activation(out=gsq_b[:], in_=y_asm[:, mt * TB:(mt + 1) * TB],
                               func=AF.Square).then_inc(s_ac.h, 1)
                s_ac.n += 1
            eng.wait_ge(s_dv.h, dvB(tb) + 18 + 696 + 8 + 1)
            seq(eng, s_aq, eng.activation(out=g_tmp[:], in_=g_ms[:],
                                          func=AF.Ln, bias=EPS))
            eng.activation(out=g_ms[:], in_=g_tmp[:], func=AF.Exp,
                           scale=-0.5).then_inc(s_ac.h, 1)
            s_ac.n += 1
            for mt in range(8):
                eng.wait_ge(s_dv.h, dvB(tb) + 18 + 696 + 8 + 2 + mt + 1)
                stYo.tick(eng.dma_start(
                    out=y_out[mt * 128:(mt + 1) * 128,
                              tb * TB:(tb + 1) * TB],
                    in_=yst_b[:, (mt % 2) * TB:(mt % 2) * TB + TB]))
        stYo.wait_all(eng)
    assert s_ac.n == AC0 + NTB * AC_TB, s_ac.n

    return k


from concourse.bass_utils import run_bass_kernel_spmd

NCORES = 8


def build_Q(n=4096):
    t = np.arange(n)
    Q = np.zeros((n, n), np.float64)
    Q[:, 0] = 1.0 / np.sqrt(n)
    f = np.arange(1, n // 2)
    ang = 2 * np.pi * np.outer(t, f) / n
    Q[:, 2 * f - 1] = np.sqrt(2.0 / n) * np.cos(ang)
    Q[:, 2 * f] = np.sqrt(2.0 / n) * np.sin(ang)
    Q[:, n - 1] = ((-1.0) ** t) / np.sqrt(n)
    return Q.astype(np.float32)


def _run(build_fn, in_maps):
    k = build_fn()
    res = run_bass_kernel_spmd(k.nc, in_maps, core_ids=list(range(NCORES)))
    return res.results


def kernel(**inp):
    f32 = np.float32
    g = lambda n: np.ascontiguousarray(np.asarray(inp[n], f32))
    x = g("L0_attn_x")
    x_fm = np.ascontiguousarray(x.reshape(8192, 4096).T)
    xs = [np.ascontiguousarray(x_fm[:, c * 1024:(c + 1) * 1024])
          for c in range(NCORES)]

    nw = np.ascontiguousarray(g("L0_attn_norm_w").reshape(32, 128).T)
    maps = [{"x_in": xs[c], "nw_in": nw} for c in range(NCORES)]
    res = _run(lambda: build_l1a(1024), maps)
    h_full = np.concatenate([np.asarray(r["h_out"]) for r in res], axis=1)
    t_parts = [np.asarray(r["t_out"]) for r in res]
    h_tiled = np.ascontiguousarray(
        h_full.reshape(32, 128, 8192).transpose(1, 0, 2)).reshape(128, 32 * 8192)

    ipw = g("L0_attn_in_proj_w")
    cw = g("L0_attn_conv_w")
    cb = g("L0_attn_conv_b")
    dtbias = g("L0_attn_dt_bias")
    alog = g("L0_attn_A_log")
    Dp = g("L0_attn_Dp")
    gnw = g("L0_attn_gnorm_w")
    triu = (np.arange(128)[:, None] <= np.arange(128)[None, :]).astype(f32)
    mtrilc = np.where(np.arange(128)[:, None] <= np.arange(128)[None, :],
                      0.0, -1e30).astype(f32)
    idnf = np.eye(128, dtype=f32)
    idnb = np.eye(128).astype(ml_dtypes.bfloat16)
    pad = lambda a, n: np.pad(a, ((0, n - a.shape[0]),) +
                              ((0, 0),) * (a.ndim - 1))
    maps = []
    for gi in range(8):
        hid = ipw[8192 + gi * 1024:8192 + (gi + 1) * 1024]
        Bg = ipw[16384 + gi * 128:16384 + (gi + 1) * 128]
        Cg = ipw[17408 + gi * 128:17408 + (gi + 1) * 128]
        dtg = ipw[18432 + gi * 16:18432 + (gi + 1) * 16]
        gate = ipw[gi * 1024:(gi + 1) * 1024]
        wg_rows = np.concatenate([hid, Bg, Cg, pad(dtg, 128), gate], 0)
        cwg = np.concatenate([cw[gi * 1024:(gi + 1) * 1024],
                              cw[8192 + gi * 128:8192 + (gi + 1) * 128],
                              cw[9216 + gi * 128:9216 + (gi + 1) * 128]], 0)
        cbg = np.concatenate([cb[gi * 1024:(gi + 1) * 1024],
                              cb[8192 + gi * 128:8192 + (gi + 1) * 128],
                              cb[9216 + gi * 128:9216 + (gi + 1) * 128]], 0)
        maps.append({
            "h_in": h_tiled,
            "wg": tile_w(np.ascontiguousarray(wg_rows.T), 32, 19),
            "convw": np.ascontiguousarray(
                cwg.reshape(10, 128, 4).transpose(1, 0, 2).reshape(128, 40)),
            "convb": np.ascontiguousarray(cbg.reshape(10, 128).T),
            "dtb": pad(dtbias[gi * 16:(gi + 1) * 16].reshape(16, 1), 128),
            "alog": pad(alog[gi * 16:(gi + 1) * 16].reshape(16, 1), 128),
            "dp_row": np.ascontiguousarray(
                Dp[gi * 16:(gi + 1) * 16].reshape(1, 16)),
            "gnw": np.ascontiguousarray(
                gnw[gi * 1024:(gi + 1) * 1024].reshape(8, 128).T),
            "triu": triu, "mtril": mtrilc, "idn_f": idnf, "idn_b": idnb,
        })
    res = _run(build_l1b, maps)
    y_full = np.concatenate([np.asarray(r["y_out"]) for r in res], axis=0)

    opT = tile_w(np.ascontiguousarray(g("L0_attn_out_proj_w").T), 64, 32)
    wT2 = tile_w(np.ascontiguousarray(g("L0_ffn_W").T), 32, 32)
    bias2 = np.ascontiguousarray(g("L0_ffn_bias").reshape(32, 128).T)
    A2 = g("L0_ffn_lora_A").reshape(128, 4096)
    B2 = np.ascontiguousarray(
        g("L0_ffn_lora_B").transpose(0, 2, 1).reshape(128, 4096))
    aT2 = tile_a(np.ascontiguousarray(A2.T))
    coef2 = np.ascontiguousarray(
        np.repeat(g("L0_ffn_lora_weights"), 16).reshape(128, 1))
    maps = []
    for c in range(NCORES):
        ysl = np.ascontiguousarray(y_full[:, c * 1024:(c + 1) * 1024])
        y_t = np.ascontiguousarray(
            ysl.reshape(64, 128, 1024).transpose(1, 0, 2)).reshape(128, 64 * 1024)
        maps.append({"y_in": y_t, "x_in": xs[c], "tln_in": t_parts[c],
                     "opT": opT, "wT": wT2,
                     "bias_in": bias2, "aT": aT2, "bT": B2,
                     "coef_in": coef2})
    res = _run(lambda: build_l2(1024), maps)
    x2_full = np.concatenate([np.asarray(r["x2_out"]) for r in res], axis=1)
    x2s = [np.ascontiguousarray(x2_full[:, c * 1024:(c + 1) * 1024])
           for c in range(NCORES)]

    wq, wk, wv = g("L1_attn_wq"), g("L1_attn_wk"), g("L1_attn_wv")
    bq, bk, bv = g("L1_attn_bq"), g("L1_attn_bk"), g("L1_attn_bv")
    maps = []
    for c in range(NCORES):
        b = c // 4
        hb = 8 * (c % 4)
        r0, r1 = hb * 128, hb * 128 + 1024
        wqk_rows = np.concatenate([wq[r0:r1], wk[r0:r1]], 0)
        maps.append({
            "x_in": np.ascontiguousarray(x2_full[:, b * 4096:(b + 1) * 4096]),
            "wqk": tile_w(np.ascontiguousarray(wqk_rows.T), 32, 16),
            "wvT": np.ascontiguousarray(
                np.ascontiguousarray(wv[r0:r1].T).reshape(32, 128, 1024)),
            "bqk": np.ascontiguousarray(
                np.concatenate([bq[r0:r1], bk[r0:r1]]).reshape(16, 128).T),
            "bv_row": np.ascontiguousarray(bv[r0:r1].reshape(1, 1024)),
            "idn_in": idnb,
        })
    res = _run(build_l3, maps)
    ctx_full = [np.zeros((4096, 4096), ml_dtypes.bfloat16) for _ in range(2)]
    for c in range(NCORES):
        b = c // 4
        ctx_full[b][(c % 4) * 1024:(c % 4 + 1) * 1024, :] = \
            np.asarray(res[c]["ctx_out"])

    Q = build_Q(4096)
    QT4 = np.ascontiguousarray(Q.T)
    qt = np.ascontiguousarray(
        QT4.reshape(32, 128, 8, 512).transpose(2, 1, 0, 3)
    ).reshape(8, 128, 32 * 512).astype(ml_dtypes.bfloat16)
    woT = tile_w(np.ascontiguousarray(g("L1_attn_wo").T), 32, 32)
    wbT4 = tile_w(np.ascontiguousarray(g("L1_ffn_W_base").T), 32, 32)
    bo4 = np.ascontiguousarray(g("L1_attn_bo").reshape(32, 128).T)
    bb4 = np.ascontiguousarray(g("L1_ffn_b_base").reshape(32, 128).T)
    encT = np.ascontiguousarray(
        np.concatenate([g("L1_ffn_enc0"), g("L1_ffn_enc1")], 0).T)
    c0 = g("L1_ffn_coefs0")
    c1 = g("L1_ffn_coefs1")
    m0 = np.asarray(inp["L1_ffn_mask0"]).astype(np.uint8)
    m1 = np.asarray(inp["L1_ffn_mask1"]).astype(np.uint8)
    c0t = np.stack([tile_a(np.ascontiguousarray(c0[e])) for e in range(3)])
    c1t = np.stack([tile_a(np.ascontiguousarray(c1[e])) for e in range(3)])
    m0t = np.stack([tile_a(np.ascontiguousarray(m0[e])) for e in range(3)])
    m1t = np.stack([tile_a(np.ascontiguousarray(m1[e])) for e in range(3)])
    encTt = tile_a(encT)
    maps = []
    for c in range(NCORES):
        b = c // 4
        csl = np.ascontiguousarray(
            ctx_full[b][:, (c % 4) * 1024:(c % 4 + 1) * 1024])
        ctx_t = np.ascontiguousarray(
            csl.reshape(32, 128, 1024).transpose(1, 0, 2)
        ).reshape(128, 32 * 1024)
        maps.append({"ctx_in": ctx_t, "x_in": x2s[c], "woT": woT,
                     "wbT": wbT4, "bo_in": bo4, "bb_in": bb4,
                     "encT_in": encTt, "qt_in": qt, "c0_in": c0t,
                     "c1_in": c1t, "m0_in": m0t, "m1_in": m1t})
    res = _run(lambda: build_l4(1024), maps)
    out_fm = np.concatenate(
        [np.asarray(r["x4_out"], dtype=np.float32) for r in res], axis=1)
    return np.ascontiguousarray(out_fm.T).reshape(2, 4096, 4096)


# version 08 (final)
# speedup vs baseline: 1.2567x; 1.0376x over previous
import numpy as np
import ml_dtypes
from contextlib import ExitStack
import concourse.bass as bass
import concourse.mybir as mybir

F32 = mybir.dt.float32
BF16 = mybir.dt.bfloat16
U8 = mybir.dt.uint8
AF = mybir.ActivationFunctionType
ALU = mybir.AluOpType
EPS = 1e-5


class SemC:
    def __init__(self, h):
        self.h = h
        self.n = 0

    def tick(self, instr, k=1):
        instr.then_inc(self.h, k)
        self.n += k
        return self.n


class K:
    def __init__(self):
        self.nc = bass.Bass("TRN2", target_bir_lowering=False)
        self.ctx = ExitStack()
        self.sems = {}
        self._uid = 0
        self._scopes = []

    def _u(self, name):
        self._uid += 1
        return f"{name}_u{self._uid}"

    def push(self):
        self._scopes.append(ExitStack())

    def pop(self):
        self._scopes.pop().close()

    @property
    def _cur(self):
        return self._scopes[-1] if self._scopes else self.ctx

    def sbuf(self, name, shape, dt=F32):
        return self._cur.enter_context(
            self.nc.sbuf_tensor(self._u(name), shape, dt))

    def psum(self, name, shape, dt=F32):
        return self._cur.enter_context(
            self.nc.psum_tensor(self._u(name), shape, dt))

    def sem(self, name):
        name = self._u(name)
        h = self.ctx.enter_context(self.nc.semaphore(name))
        s = SemC(h)
        self.sems[name] = s
        return s

    def din(self, name, shape, dt=F32):
        return self.nc.dram_tensor(name, shape, dt, kind="ExternalInput").ap()

    def dout(self, name, shape, dt=F32):
        return self.nc.dram_tensor(name, shape, dt, kind="ExternalOutput").ap()

    def dscratch(self, name, shape, dt=F32, addr_space="Local"):
        return self.nc.dram_tensor(name, shape, dt, kind="Internal",
                                   addr_space=addr_space).ap()

    def setup_consts(self, values):
        vals = sorted(set(float(v) for v in values))
        self.cvals = vals
        data = np.tile(np.asarray(vals, np.float32)[None, :], (128, 1))
        self.cdram = self.nc.inline_tensor(data, "const_bank").ap()
        self.csb = self.sbuf("const_bank_sb", [128, len(vals)], F32)
        for i, v in enumerate(vals):
            self.nc.const_aps.register(F32, v, self.csb[:, i:i + 1])
        return self.cdram, self.csb

    def ccol(self, v):
        i = self.cvals.index(float(v))
        return self.csb[:, i:i + 1]

    def block(self):
        return self.ctx.enter_context(self.nc.Block())


def bfcast(x):
    return np.ascontiguousarray(x).astype(ml_dtypes.bfloat16)


def bf2f(x):
    return np.asarray(x).astype(np.float32)


class Stream:
    def __init__(self, k, name, depth=2, inc=16):
        self.sems = [k.sem(f"{name}{j}") for j in range(depth)]
        self.inc = inc
        self.depth = depth
        self.count = 0

    def tick(self, instr):
        s = self.sems[self.count % self.depth]
        instr.then_inc(s.h, self.inc)
        s.n += self.inc
        self.count += 1

    def wait(self, eng, i):
        if i < 0:
            return
        eng.wait_ge(self.sems[i % self.depth].h,
                    self.inc * (i // self.depth + 1))

    def wait_all(self, eng):
        for s in self.sems:
            if s.n:
                eng.wait_ge(s.h, s.n)


def seq(eng, s, inst):
    inst.then_inc(s.h, 1)
    s.n += 1
    eng.wait_ge(s.h, s.n)

def tile_w(wT, NK, NM):
    import numpy as np
    K, M = wT.shape
    A = wT.reshape(NK, 128, NM, 128)
    return np.ascontiguousarray(A.transpose(2, 1, 0, 3)).reshape(NM, 128, NK * 128)

def tile_a(x):
    import numpy as np
    R = x.shape[0] // 128
    A = x.reshape(R, 128, x.shape[1])
    return np.ascontiguousarray(A.transpose(1, 0, 2)).reshape(128, R * x.shape[1])


"""Launch builders for the unfused checkpoint.

Conventions:
- feature-major activations: [channels(128-tiles) x tokens] in SBUF
- weights in HBM pre-transposed to [in x out] fp32, DMA-cast to bf16
- residual stream fp32 in HBM
- DMA completion tracking via klib.Stream (parity-split semaphores)
"""


def build_l2(T=1024, upto='full'):
    k = K()
    nc = k.nc
    NM, NK1, NK3 = 32, 64, 32

    y_in = k.din("y_in", [128, 64 * T], BF16)
    x_in = k.din("x_in", [4096, T], F32)
    tln_in = k.din("tln_in", [4096, T], BF16)
    opT = k.din("opT", [32, 128, 64 * 128], F32)
    wT = k.din("wT", [32, 128, 32 * 128], F32)
    bias_in = k.din("bias_in", [128, 32], F32)
    aT = k.din("aT", [128, 32 * 128], F32)
    bT = k.din("bT", [128, 4096], F32)
    coef_in = k.din("coef_in", [128, 1], F32)
    x2_out = k.dout("x2_out", [4096, T], F32)

    ao_scr = (k.dout("ao_scr", [4096, T], BF16) if upto == 'p1'
              else k.dscratch("ao_scr", [4096, T], BF16))
    x1_scr = (k.dout("x1_scr", [4096, T], F32) if upto == 'p2'
              else k.dscratch("x1_scr", [4096, T], F32))

    cdram, csb = k.setup_consts([EPS, 1.0 / 4096.0, -1.0, -0.5, 2.0])

    y_sb = k.sbuf("y_sb", [128, 64 * T], BF16)
    wbuf = k.sbuf("wbuf", [128, 2 * 64 * 128], BF16)
    ab = k.sbuf("ab", [128, 11 * T], BF16)
    af = k.sbuf("af", [128, 6 * T], F32)
    bias_sb = k.sbuf("bias_sb", [128, 32], F32)
    coef_sb = k.sbuf("coef_sb", [128, 2], F32)
    ones_bf = k.sbuf("ones_bf", [128, 1], BF16)
    ones_f = k.sbuf("ones_f", [128, 1], F32)
    onesr_f = k.sbuf("onesr_f", [1, 128], F32)

    ao_t = [ab[:, 0:T], ab[:, T:2 * T]]
    sq_t = ab[:, 2 * T:3 * T]
    t1_b = [ab[:, 3 * T:4 * T], ab[:, 4 * T:5 * T]]
    bc1_b = ab[:, 5 * T:6 * T]
    bc2_b = ab[:, 6 * T:7 * T]
    lh_bf = None
    a_sb = ab[:, 7 * T:11 * T]
    bT_sb = ab[:, 7 * T:11 * T]
    x_t = [af[:, 0:T], af[:, T:2 * T]]
    x1f = af[:, 2 * T:3 * T]
    r_mean = af[0:1, 3 * T:4 * T]
    r_rstd = af[0:1, 4 * T:5 * T]
    r_tmp = af[0:1, 5 * T:6 * T]

    psA = k.psum("psA", [128, T], F32)
    psB = k.psum("psB", [128, T], F32)
    ps_sum = k.psum("ps_sum", [1, T], F32)
    ps_sq = k.psum("ps_sq", [1, T], F32)

    s_pe = k.sem("s_pe")
    s_dq = k.sem("s_dq")
    s_aq = k.sem("s_aq")
    s_dv = k.sem("s_dv")
    s_ac = k.sem("s_ac")
    stPre = Stream(k, "stPre", depth=1)
    stPreG = Stream(k, "stPreG", depth=1)
    stW = Stream(k, "stW")
    stX = Stream(k, "stX")
    stAO = Stream(k, "stAO")
    stSao = Stream(k, "stSao")
    stSx1 = Stream(k, "stSx1")
    stOut = Stream(k, "stOut")
    stT = Stream(k, "stT")

    block = k.block()

    PE_P1, PE_R1 = 0, 64
    PE_P2, PE_R2 = 65, 97
    PE_LH, PE_P3 = 98, 99
    PE_R3, PE_END = 163, 164
    DV_P1, DV_R1 = 0, 32
    DV_P2, DV_R2 = 35, 67
    DV_H1, DV_LH = 70, 102
    DV_R3, DV_P4 = 103, 106
    DV_END = 138
    AC_P1, AC_R1 = 0, 32
    AC_P2, AC_R2 = 33, 65
    AC_P3, AC_R3 = 66, 130
    AC_END = 131

    def rows_ln(wait_pe):
        @block.vector
        def _(eng):
            eng.wait_ge(s_pe.h, wait_pe)
            seq(eng, s_dq, eng.tensor_scalar(
                out=r_mean[:], in0=ps_sum[:], scalar1=1.0 / 4096.0,
                scalar2=None, op0=ALU.mult))
            seq(eng, s_dq, eng.tensor_scalar(
                out=r_tmp[:], in0=ps_sq[:], scalar1=1.0 / 4096.0,
                scalar2=None, op0=ALU.mult))
            seq(eng, s_dq, eng.tensor_tensor(
                out=r_rstd[:], in0=r_mean[:], in1=r_mean[:], op=ALU.mult))
            eng.tensor_tensor(out=r_rstd[:], in0=r_tmp[:], in1=r_rstd[:],
                              op=ALU.subtract).then_inc(s_dv.h, 1)
            s_dv.n += 1
        dv1 = s_dv.n

        @block.scalar
        def _(eng):
            eng.wait_ge(s_dv.h, dv1)
            seq(eng, s_aq, eng.activation(out=r_tmp[:], in_=r_rstd[:],
                                          func=AF.Ln, bias=EPS))
            eng.activation(out=r_rstd[:], in_=r_tmp[:], func=AF.Exp,
                           scale=-0.5).then_inc(s_ac.h, 1)
            s_ac.n += 1
        ac1 = s_ac.n

        @block.vector
        def _(eng):
            eng.wait_ge(s_ac.h, ac1)
            seq(eng, s_dq, eng.tensor_tensor(out=r_tmp[:], in0=r_mean[:],
                                             in1=r_rstd[:], op=ALU.mult))
            eng.tensor_scalar(out=r_tmp[:], in0=r_tmp[:], scalar1=-1.0,
                              scalar2=None, op0=ALU.mult).then_inc(s_dv.h, 1)
            s_dv.n += 1
        dv2 = s_dv.n

        @block.tensor
        def _(eng):
            eng.wait_ge(s_dv.h, dv2)
            eng.matmul(psA[:, 0:512], onesr_f[:1, :], r_rstd[:, 0:512],
                       start=True, stop=True)
            eng.matmul(psA[:, 512:T], onesr_f[:1, :], r_rstd[:, 512:T],
                       start=True, stop=True)
            eng.matmul(psB[:, 0:512], onesr_f[:1, :], r_tmp[:, 0:512],
                       start=True, stop=True)
            eng.matmul(psB[:, 512:T], onesr_f[:1, :], r_tmp[:, 512:T],
                       start=True, stop=True).then_inc(s_pe.h, 1)
            s_pe.n += 1
        pe1 = s_pe.n

        @block.vector
        def _(eng):
            eng.wait_ge(s_pe.h, pe1)
            seq(eng, s_dq, eng.tensor_copy(bc1_b[:], psA[:]))
            seq(eng, s_dq, eng.tensor_copy(bc2_b[:], psB[:]))
            eng.nop().then_inc(s_dv.h, 1)
            s_dv.n += 1

    lh_bf = wbuf[:, 8192:8192 + T]
    tln_t = [bc1_b, bc2_b]
    w4 = wbuf.ap().rearrange("p (j q c) -> p j q c", j=2, c=128)
    w4b = wbuf[:, 0:8192].rearrange("p (j q c) -> p j q c", j=2, c=128)
    yv = y_sb.ap().rearrange("p (q t) -> p q t", t=T)
    h1 = y_sb[:, 0:32 * T]
    ffn = y_sb[:, 32 * T:64 * T]

    @block.sync
    def _(eng):
        stPre.tick(eng.dma_start(out=csb[:], in_=cdram[:]))
        stPre.tick(eng.dma_start(out=bias_sb[:], in_=bias_in[:]))
        stPre.tick(eng.dma_start(out=coef_sb[:, 0:1], in_=coef_in[:]))
        stPre.tick(eng.dma_start(out=y_sb[:], in_=y_in[:]))

    @block.gpsimd
    def _(eng):
        eng.memset(ones_bf[:], 1.0)
        eng.memset(ones_f[:], 1.0)
        eng.memset(onesr_f[:], 1.0)
        stPreG.tick(eng.dma_start(out=a_sb, in_=aT[:]))

    @block.sync
    def _(eng):
        for m in range(NM):
            if m >= 2:
                eng.wait_ge(s_dv.h, DV_P1 + m - 1)
            stT.tick(eng.dma_start(out=tln_t[m % 2][:],
                                   in_=tln_in[m * 128:(m + 1) * 128, :]))

    @block.gpsimd
    def _(eng):
        for m in range(NM):
            if m >= 2:
                eng.wait_ge(s_pe.h, PE_P1 + 2 * (m - 2) + 1)
            stW.tick(eng.dma_start(out=wbuf[:, (m % 2) * 8192:
                                            (m % 2) * 8192 + 8192],
                                   in_=opT[m]))

    @block.tensor
    def _(eng):
        stPre.wait_all(eng)
        stPreG.wait_all(eng)
        for m in range(NM):
            ps = psA if m % 2 == 0 else psB
            if m >= 1:
                eng.wait_ge(s_dv.h, DV_P1 + m)
                eng.wait_ge(s_ac.h, AC_P1 + m)
                last = None
                for half in range(2):
                    sl = slice(half * 512, half * 512 + 512)
                    eng.matmul(ps_sum[:, sl], ones_bf[:, 0:1],
                               ao_t[(m - 1) % 2][:, sl],
                               start=(m == 1), stop=False)
                    last = eng.matmul(ps_sq[:, sl], ones_bf[:, 0:1],
                                      sq_t[:, sl], start=(m == 1), stop=False)
                last.then_inc(s_pe.h, 1)
                s_pe.n += 1
            stW.wait(eng, m)
            if m >= 2:
                eng.wait_ge(s_dv.h, DV_P1 + m - 1)
                eng.wait_ge(s_ac.h, AC_P1 + m - 1)
            last = None
            for kk in range(NK1):
                for half in range(2):
                    sl = slice(half * 512, half * 512 + 512)
                    last = eng.matmul(ps[:, sl], w4[:, m % 2, kk],
                                      yv[:, kk, sl],
                                      start=(kk == 0), stop=(kk == NK1 - 1))
            last.then_inc(s_pe.h, 1)
            s_pe.n += 1
        eng.wait_ge(s_dv.h, DV_P1 + NM)
        eng.wait_ge(s_ac.h, AC_P1 + NM)
        last = None
        for half in range(2):
            sl = slice(half * 512, half * 512 + 512)
            eng.matmul(ps_sum[:, sl], ones_bf[:, 0:1],
                       ao_t[(NM - 1) % 2][:, sl], start=False, stop=True)
            last = eng.matmul(ps_sq[:, sl], ones_bf[:, 0:1], sq_t[:, sl],
                              start=False, stop=True)
        last.then_inc(s_pe.h, 1)
        s_pe.n += 1
    assert s_pe.n == PE_R1, s_pe.n

    @block.vector
    def _(eng):
        for m in range(NM):
            eng.wait_ge(s_pe.h, PE_P1 + 2 * m + 1)
            stT.wait(eng, m)
            stSao.wait(eng, m - 2)
            if m >= 2:
                eng.wait_ge(s_ac.h, AC_P1 + m - 1)
            ps = psA if m % 2 == 0 else psB
            eng.tensor_tensor(out=ao_t[m % 2][:], in0=ps[:],
                              in1=tln_t[m % 2][:],
                              op=ALU.add).then_inc(s_dv.h, 1)
            s_dv.n += 1
    assert s_dv.n == DV_R1

    @block.scalar
    def _(eng):
        for m in range(NM):
            eng.wait_ge(s_dv.h, DV_P1 + m + 1)
            eng.activation(out=sq_t[:], in_=ao_t[m % 2][:],
                           func=AF.Square).then_inc(s_ac.h, 1)
            s_ac.n += 1
            stSao.tick(eng.dma_start(out=ao_scr[m * 128:(m + 1) * 128, :],
                                     in_=ao_t[m % 2][:]))
    assert s_ac.n == AC_R1

    if upto == 'p1':
        @block.scalar
        def _(eng):
            stSao.wait_all(eng)
        return k

    rows_ln(PE_R1)
    assert (s_dv.n, s_ac.n, s_pe.n) == (DV_P2, AC_P2, PE_P2)

    @block.sync
    def _(eng):
        for m in range(NM):
            if m < 2:
                eng.wait_ge(s_pe.h, PE_R1)
                stSao.wait(eng, NM - 2 + m)
            else:
                eng.wait_ge(s_dv.h, DV_P2 + m - 1)
            stSao.wait(eng, m)
            stX.tick(eng.dma_start(out=x_t[m % 2][:],
                                   in_=x_in[m * 128:(m + 1) * 128, :]))
            stAO.tick(eng.dma_start(out=ao_t[m % 2][:],
                                    in_=ao_scr[m * 128:(m + 1) * 128, :]))

    @block.vector
    def _(eng):
        for m in range(NM):
            stX.wait(eng, m)
            stAO.wait(eng, m)
            if m >= 1:
                stSx1.wait(eng, m - 1)
                eng.wait_ge(s_ac.h, AC_P2 + m)
                eng.wait_ge(s_pe.h, PE_P2 + m)
            if m >= 2:
                eng.wait_ge(s_dv.h, DV_P2 + m - 1)
            seq(eng, s_dq, eng.tensor_tensor(out=t1_b[m % 2][:],
                                             in0=ao_t[m % 2][:],
                                             in1=bc1_b[:], op=ALU.mult))
            seq(eng, s_dq, eng.tensor_tensor(out=x1f[:], in0=x_t[m % 2][:],
                                             in1=t1_b[m % 2][:], op=ALU.add))
            eng.tensor_tensor(out=x1f[:], in0=x1f[:], in1=bc2_b[:],
                              op=ALU.add).then_inc(s_dv.h, 1)
            s_dv.n += 1
    assert s_dv.n == DV_R2

    @block.scalar
    def _(eng):
        for m in range(NM):
            eng.wait_ge(s_dv.h, DV_P2 + m + 1)
            if m >= 1:
                eng.wait_ge(s_pe.h, PE_P2 + m)
            eng.activation(out=sq_t[:], in_=x1f[:],
                           func=AF.Square).then_inc(s_ac.h, 1)
            s_ac.n += 1
            stSx1.tick(eng.dma_start(out=x1_scr[m * 128:(m + 1) * 128, :],
                                     in_=x1f[:]))
    assert s_ac.n == AC_R2

    @block.tensor
    def _(eng):
        for m in range(NM):
            eng.wait_ge(s_dv.h, DV_P2 + m + 1)
            eng.wait_ge(s_ac.h, AC_P2 + m + 1)
            last = None
            for half in range(2):
                sl = slice(half * 512, half * 512 + 512)
                eng.matmul(ps_sum[:, sl], ones_f[:, 0:1], x1f[:, sl],
                           start=(m == 0), stop=(m == NM - 1))
                last = eng.matmul(ps_sq[:, sl], ones_bf[:, 0:1], sq_t[:, sl],
                                  start=(m == 0), stop=(m == NM - 1))
            last.then_inc(s_pe.h, 1)
            s_pe.n += 1
    assert s_pe.n == PE_R2

    if upto == 'p2':
        @block.scalar
        def _(eng):
            stSx1.wait_all(eng)
        return k

    rows_ln(PE_R2)
    assert (s_dv.n, s_ac.n, s_pe.n) == (DV_H1, AC_P3, PE_LH)

    @block.sync
    def _(eng):
        for m in range(NM):
            stSx1.wait(eng, m)
            if m < 2:
                eng.wait_ge(s_dv.h, DV_R2)
            else:
                eng.wait_ge(s_dv.h, DV_H1 + m - 1)
            stX.tick(eng.dma_start(out=x_t[m % 2][:],
                                   in_=x1_scr[m * 128:(m + 1) * 128, :]))

    @block.vector
    def _(eng):
        for m in range(NM):
            stX.wait(eng, NM + m)
            if m >= 2:
                eng.wait_ge(s_dv.h, DV_H1 + m - 1)
            seq(eng, s_dq, eng.tensor_tensor(out=t1_b[m % 2][:],
                                             in0=x_t[m % 2][:],
                                             in1=bc1_b[:], op=ALU.mult))
            eng.tensor_tensor(out=h1[:, m * T:(m + 1) * T], in0=t1_b[m % 2][:],
                              in1=bc2_b[:], op=ALU.add).then_inc(s_dv.h, 1)
            s_dv.n += 1
    assert s_dv.n == DV_LH

    @block.tensor
    def _(eng):
        eng.wait_ge(s_dv.h, DV_LH)
        last = None
        for kk in range(NK3):
            for half in range(2):
                c0 = kk * T + half * 512
                last = eng.matmul(psA[:, half * 512:half * 512 + 512],
                                  a_sb[:, kk * 128:(kk + 1) * 128],
                                  h1[:, c0:c0 + 512],
                                  start=(kk == 0), stop=(kk == NK3 - 1))
        last.then_inc(s_pe.h, 1)
        s_pe.n += 1
    assert s_pe.n == PE_P3

    @block.vector
    def _(eng):
        eng.wait_ge(s_pe.h, PE_P3)
        seq(eng, s_dq, eng.tensor_scalar(out=coef_sb[:, 1:2],
                                         in0=coef_sb[:, 0:1], scalar1=2.0,
                                         scalar2=None, op0=ALU.mult))
        eng.tensor_scalar(out=lh_bf[:], in0=psA[:], scalar1=coef_sb[:, 1:2],
                          scalar2=None, op0=ALU.mult).then_inc(s_dv.h, 1)
        s_dv.n += 1
    assert s_dv.n == DV_R3

    @block.gpsimd
    def _(eng):
        eng.wait_ge(s_pe.h, PE_P3)
        stPreG.tick(eng.dma_start(out=bT_sb[:], in_=bT[:]))
        for m in range(NM):
            if m >= 2:
                eng.wait_ge(s_pe.h, PE_P3 + 2 * (m - 2) + 1)
            stW.tick(eng.dma_start(out=wbuf[:, (m % 2) * 4096:
                                            (m % 2) * 4096 + 4096],
                                   in_=wT[m]))

    @block.tensor
    def _(eng):
        eng.wait_ge(s_dv.h, DV_R3)
        stPreG.wait_all(eng)
        for m in range(NM):
            ps = psA if m % 2 == 0 else psB
            if m >= 1:
                eng.wait_ge(s_ac.h, AC_P3 + 2 * m)
                last = None
                for half in range(2):
                    sl = slice(half * 512, half * 512 + 512)
                    c0 = (m - 1) * T + half * 512
                    eng.matmul(ps_sum[:, sl], ones_bf[:, 0:1],
                               ffn[:, c0:c0 + 512], start=(m == 1), stop=False)
                    last = eng.matmul(ps_sq[:, sl], ones_bf[:, 0:1],
                                      sq_t[:, sl], start=(m == 1), stop=False)
                last.then_inc(s_pe.h, 1)
                s_pe.n += 1
            stW.wait(eng, NM + m)
            if m >= 2:
                eng.wait_ge(s_ac.h, AC_P3 + 2 * (m - 1))
            for kk in range(NK3):
                for half in range(2):
                    c0 = kk * T + half * 512
                    eng.matmul(ps[:, half * 512:half * 512 + 512],
                               w4b[:, m % 2, kk], h1[:, c0:c0 + 512],
                               start=(kk == 0), stop=False)
            last = None
            for half in range(2):
                sl = slice(half * 512, half * 512 + 512)
                last = eng.matmul(ps[:, sl], bT_sb[:, m * 128:(m + 1) * 128],
                                  lh_bf[:, sl], start=False, stop=True)
            last.then_inc(s_pe.h, 1)
            s_pe.n += 1
        eng.wait_ge(s_ac.h, AC_P3 + 2 * NM)
        last = None
        for half in range(2):
            sl = slice(half * 512, half * 512 + 512)
            c0 = (NM - 1) * T + half * 512
            eng.matmul(ps_sum[:, sl], ones_bf[:, 0:1], ffn[:, c0:c0 + 512],
                       start=False, stop=True)
            last = eng.matmul(ps_sq[:, sl], ones_bf[:, 0:1], sq_t[:, sl],
                              start=False, stop=True)
        last.then_inc(s_pe.h, 1)
        s_pe.n += 1
    assert s_pe.n == PE_R3

    @block.scalar
    def _(eng):
        for m in range(NM):
            eng.wait_ge(s_pe.h, PE_P3 + 2 * m + 1)
            ps = psA if m % 2 == 0 else psB
            eng.activation(out=ffn[:, m * T:(m + 1) * T], in_=ps[:],
                           func=AF.Identity,
                           bias=bias_sb[:, m:m + 1]).then_inc(s_ac.h, 1)
            eng.activation(out=sq_t[:], in_=ps[:], func=AF.Square,
                           bias=bias_sb[:, m:m + 1]).then_inc(s_ac.h, 1)
            s_ac.n += 2
    assert s_ac.n == AC_R3

    rows_ln(PE_R3)
    assert (s_dv.n, s_ac.n, s_pe.n) == (DV_P4, AC_END, PE_END)

    @block.sync
    def _(eng):
        for m in range(NM):
            if m < 2:
                eng.wait_ge(s_dv.h, DV_P4)
            else:
                eng.wait_ge(s_dv.h, DV_P4 + m - 1)
            stX.tick(eng.dma_start(out=x_t[m % 2][:],
                                   in_=x1_scr[m * 128:(m + 1) * 128, :]))

    @block.vector
    def _(eng):
        for m in range(NM):
            stX.wait(eng, 2 * NM + m)
            if m >= 1:
                stOut.wait(eng, m - 1)
            if m >= 2:
                eng.wait_ge(s_dv.h, DV_P4 + m - 1)
            seq(eng, s_dq, eng.tensor_tensor(out=t1_b[m % 2][:],
                                             in0=ffn[:, m * T:(m + 1) * T],
                                             in1=bc1_b[:], op=ALU.mult))
            seq(eng, s_dq, eng.tensor_tensor(out=x1f[:], in0=x_t[m % 2][:],
                                             in1=t1_b[m % 2][:], op=ALU.add))
            eng.tensor_tensor(out=x1f[:], in0=x1f[:], in1=bc2_b[:],
                              op=ALU.add).then_inc(s_dv.h, 1)
            s_dv.n += 1
    assert s_dv.n == DV_END

    @block.scalar
    def _(eng):
        for m in range(NM):
            eng.wait_ge(s_dv.h, DV_P4 + m + 1)
            stOut.tick(eng.dma_start(out=x2_out[m * 128:(m + 1) * 128, :],
                                     in_=x1f[:]))
        stOut.wait_all(eng)

    return k


def build_l4(T=1024):
    k = K()
    nc = k.nc
    NM, NK = 32, 32

    ctx_in = k.din("ctx_in", [128, 32 * T], BF16)
    x_in = k.din("x_in", [4096, T], F32)
    woT = k.din("woT", [32, 128, 32 * 128], F32)
    wbT = k.din("wbT", [32, 128, 32 * 128], F32)
    bo_in = k.din("bo_in", [128, 32], F32)
    bb_in = k.din("bb_in", [128, 32], F32)
    encT_in = k.din("encT_in", [128, 32 * 40], F32)
    qt_in = k.din("qt_in", [8, 128, 32 * 512], BF16)
    c0_in = k.din("c0_in", [3, 128, 32 * 32], F32)
    c1_in = k.din("c1_in", [3, 128, 32 * 8], F32)
    m0_in = k.din("m0_in", [3, 128, 32 * 32], U8)
    m1_in = k.din("m1_in", [3, 128, 32 * 8], U8)
    x4_out = k.dout("x4_out", [4096, T], F32)

    ao_scr = k.dscratch("ao_scr4", [4096, T], BF16)
    x3_scr = k.dscratch("x3_scr", [4096, T], F32)

    cdram, csb = k.setup_consts([EPS, 1.0 / 4096.0, -1.0, -0.5, 0.5])

    big = k.sbuf("big", [128, 64 * T], BF16)
    wbuf = k.sbuf("wbuf4", [128, 2 * 4096], BF16)
    ab = k.sbuf("ab4", [128, 14 * T], BF16)
    af = k.sbuf("af4", [128, 8 * T], F32)
    bo_sb = k.sbuf("bo_sb", [128, 32], F32)
    bb_sb = k.sbuf("bb_sb", [128, 32], F32)
    enc_bf = k.sbuf("enc_bf", [128, 32 * 40], BF16)
    ones_bf = k.sbuf("ones_bf4", [128, 1], BF16)
    ones_f = k.sbuf("ones_f4", [128, 1], F32)
    onesr_f = k.sbuf("onesr_f4", [1, 128], F32)

    ao_t = [ab[:, 0:T], ab[:, T:2 * T]]
    sq_t = ab[:, 2 * T:3 * T]
    t1_b = [ab[:, 3 * T:4 * T], ab[:, 4 * T:5 * T]]
    bc1_b = ab[:, 5 * T:6 * T]
    bc2_b = ab[:, 6 * T:7 * T]
    e_bf = ab[:, 7 * T:8 * T]
    ws0_b = ab[:, 8 * T:9 * T]
    ws1_b = ab[:, 9 * T:9 * T + 256]
    gt_sb = ab[:, 10 * T:14 * T]

    x_t = [af[:, 0:T], af[:, T:2 * T]]
    x1f = af[:, 2 * T:3 * T]
    g1 = af[:, 3 * T:4 * T]
    g2 = af[:, 4 * T:5 * T]
    r_mean = af[0:1, 5 * T:6 * T]
    r_rstd = af[0:1, 6 * T:7 * T]
    r_tmp = af[0:1, 7 * T:8 * T]

    psA = k.psum("psA4", [128, T], F32)
    psB = k.psum("psB4", [128, T], F32)
    ps_sum = k.psum("ps_sum4", [1, T], F32)
    ps_sq = k.psum("ps_sq4", [1, T], F32)

    s_pe = k.sem("s_pe")
    s_dv = k.sem("s_dv")
    s_ac = k.sem("s_ac")
    s_dq = k.sem("s_dq")
    s_aq = k.sem("s_aq")
    stPre = Stream(k, "stPre", depth=1)
    stPreG = Stream(k, "stPreG", depth=1)
    stW = Stream(k, "stW")
    stQT = Stream(k, "stQT", depth=1)
    stGC = Stream(k, "stGC", depth=2)
    stGM = Stream(k, "stGM", depth=2)
    stX = Stream(k, "stX")
    stAO = Stream(k, "stAO")
    stSao = Stream(k, "stSao")
    stSx1 = Stream(k, "stSx1")
    stOut = Stream(k, "stOut")
    block = k.block()

    PE_P1, PE_R1 = 0, 64
    PE_P2, PE_R2 = 65, 97
    PE_GB, PE_P3 = 98, 107
    PE_R3, PE_END = 171, 172
    DV_P1, DV_R1 = 0, 32
    DV_P2, DV_R2 = 35, 67
    DV_H3, DV_GB = 70, 102
    DV_P3 = 116
    DV_R3, DV_P4 = 116, 119
    DV_END = 151
    AC_P1, AC_R1 = 0, 32
    AC_P2, AC_R2 = 33, 65
    AC_P3, AC_R3 = 66, 130
    AC_END = 131

    def rows_ln(wait_pe):
        @block.vector
        def _(eng):
            eng.wait_ge(s_pe.h, wait_pe)
            seq(eng, s_dq, eng.tensor_scalar(
                out=r_mean[:], in0=ps_sum[:], scalar1=1.0 / 4096.0,
                scalar2=None, op0=ALU.mult))
            seq(eng, s_dq, eng.tensor_scalar(
                out=r_tmp[:], in0=ps_sq[:], scalar1=1.0 / 4096.0,
                scalar2=None, op0=ALU.mult))
            seq(eng, s_dq, eng.tensor_tensor(
                out=r_rstd[:], in0=r_mean[:], in1=r_mean[:], op=ALU.mult))
            eng.tensor_tensor(out=r_rstd[:], in0=r_tmp[:], in1=r_rstd[:],
                              op=ALU.subtract).then_inc(s_dv.h, 1)
            s_dv.n += 1
        dv1 = s_dv.n

        @block.scalar
        def _(eng):
            eng.wait_ge(s_dv.h, dv1)
            seq(eng, s_aq, eng.activation(out=r_tmp[:], in_=r_rstd[:],
                                          func=AF.Ln, bias=EPS))
            eng.activation(out=r_rstd[:], in_=r_tmp[:], func=AF.Exp,
                           scale=-0.5).then_inc(s_ac.h, 1)
            s_ac.n += 1
        ac1 = s_ac.n

        @block.vector
        def _(eng):
            eng.wait_ge(s_ac.h, ac1)
            seq(eng, s_dq, eng.tensor_tensor(out=r_tmp[:], in0=r_mean[:],
                                             in1=r_rstd[:], op=ALU.mult))
            eng.tensor_scalar(out=r_tmp[:], in0=r_tmp[:], scalar1=-1.0,
                              scalar2=None, op0=ALU.mult).then_inc(s_dv.h, 1)
            s_dv.n += 1
        dv2 = s_dv.n

        @block.tensor
        def _(eng):
            eng.wait_ge(s_dv.h, dv2)
            eng.matmul(psA[:, 0:512], onesr_f[:1, :], r_rstd[:, 0:512],
                       start=True, stop=True)
            eng.matmul(psA[:, 512:T], onesr_f[:1, :], r_rstd[:, 512:T],
                       start=True, stop=True)
            eng.matmul(psB[:, 0:512], onesr_f[:1, :], r_tmp[:, 0:512],
                       start=True, stop=True)
            eng.matmul(psB[:, 512:T], onesr_f[:1, :], r_tmp[:, 512:T],
                       start=True, stop=True).then_inc(s_pe.h, 1)
            s_pe.n += 1
        pe1 = s_pe.n

        @block.vector
        def _(eng):
            eng.wait_ge(s_pe.h, pe1)
            seq(eng, s_dq, eng.tensor_copy(bc1_b[:], psA[:]))
            seq(eng, s_dq, eng.tensor_copy(bc2_b[:], psB[:]))
            eng.nop().then_inc(s_dv.h, 1)
            s_dv.n += 1

    ctxv = big[:, 0:32 * T].rearrange("p (q t) -> p q t", t=T)
    h3 = big[:, 0:32 * T]
    ffn = big[:, 32 * T:64 * T]
    qtb = big[:, 32 * T:48 * T]

    @block.sync
    def _(eng):
        stPre.tick(eng.dma_start(out=csb[:], in_=cdram[:]))
        stPre.tick(eng.dma_start(out=bo_sb[:], in_=bo_in[:]))
        stPre.tick(eng.dma_start(out=bb_sb[:], in_=bb_in[:]))
        stPre.tick(eng.dma_start(out=big[:, 0:32 * T], in_=ctx_in[:]))

    @block.gpsimd
    def _(eng):
        eng.memset(ones_bf[:], 1.0)
        eng.memset(ones_f[:], 1.0)
        eng.memset(onesr_f[:], 1.0)
        stPreG.tick(eng.dma_start(out=enc_bf[:], in_=encT_in[:]))

    @block.gpsimd
    def _(eng):
        for m in range(NM):
            if m >= 2:
                eng.wait_ge(s_pe.h, PE_P1 + 2 * (m - 2) + 1)
            stW.tick(eng.dma_start(out=wbuf[:, (m % 2) * 4096:
                                           (m % 2) * 4096 + 4096],
                                   in_=woT[m]))

    w4 = wbuf.ap().rearrange("p (j q c) -> p j q c", j=2, c=128)

    @block.tensor
    def _(eng):
        stPre.wait_all(eng)
        stPreG.wait_all(eng)
        for m in range(NM):
            ps = psA if m % 2 == 0 else psB
            if m >= 1:
                eng.wait_ge(s_dv.h, DV_P1 + m)
                eng.wait_ge(s_ac.h, AC_P1 + m)
                last = None
                for half in range(2):
                    sl = slice(half * 512, half * 512 + 512)
                    eng.matmul(ps_sum[:, sl], ones_bf[:, 0:1],
                               ao_t[(m - 1) % 2][:, sl],
                               start=(m == 1), stop=False)
                    last = eng.matmul(ps_sq[:, sl], ones_bf[:, 0:1],
                                      sq_t[:, sl], start=(m == 1), stop=False)
                last.then_inc(s_pe.h, 1)
                s_pe.n += 1
            stW.wait(eng, m)
            if m >= 2:
                eng.wait_ge(s_dv.h, DV_P1 + m - 1)
                eng.wait_ge(s_ac.h, AC_P1 + m - 1)
            last = None
            for kk in range(NK):
                for half in range(2):
                    sl = slice(half * 512, half * 512 + 512)
                    last = eng.matmul(ps[:, sl], w4[:, m % 2, kk],
                                      ctxv[:, kk, sl],
                                      start=(kk == 0), stop=(kk == NK - 1))
            last.then_inc(s_pe.h, 1)
            s_pe.n += 1
        eng.wait_ge(s_dv.h, DV_P1 + NM)
        eng.wait_ge(s_ac.h, AC_P1 + NM)
        last = None
        for half in range(2):
            sl = slice(half * 512, half * 512 + 512)
            eng.matmul(ps_sum[:, sl], ones_bf[:, 0:1],
                       ao_t[(NM - 1) % 2][:, sl], start=False, stop=True)
            last = eng.matmul(ps_sq[:, sl], ones_bf[:, 0:1], sq_t[:, sl],
                              start=False, stop=True)
        last.then_inc(s_pe.h, 1)
        s_pe.n += 1
    assert s_pe.n == PE_R1, s_pe.n

    @block.vector
    def _(eng):
        for m in range(NM):
            eng.wait_ge(s_pe.h, PE_P1 + 2 * m + 1)
            stSao.wait(eng, m - 2)
            if m >= 2:
                eng.wait_ge(s_ac.h, AC_P1 + m - 1)
            ps = psA if m % 2 == 0 else psB
            eng.tensor_scalar(out=ao_t[m % 2][:], in0=ps[:],
                              scalar1=bo_sb[:, m:m + 1], scalar2=None,
                              op0=ALU.add).then_inc(s_dv.h, 1)
            s_dv.n += 1
    assert s_dv.n == DV_R1

    @block.scalar
    def _(eng):
        for m in range(NM):
            eng.wait_ge(s_dv.h, DV_P1 + m + 1)
            eng.activation(out=sq_t[:], in_=ao_t[m % 2][:],
                           func=AF.Square).then_inc(s_ac.h, 1)
            s_ac.n += 1
            stSao.tick(eng.dma_start(out=ao_scr[m * 128:(m + 1) * 128, :],
                                     in_=ao_t[m % 2][:]))
    assert s_ac.n == AC_R1

    rows_ln(PE_R1)
    assert (s_dv.n, s_ac.n, s_pe.n) == (DV_P2, AC_P2, PE_P2)

    @block.sync
    def _(eng):
        for m in range(NM):
            if m < 2:
                eng.wait_ge(s_pe.h, PE_R1)
                stSao.wait(eng, NM - 2 + m)
            else:
                eng.wait_ge(s_dv.h, DV_P2 + m - 1)
            stSao.wait(eng, m)
            stX.tick(eng.dma_start(out=x_t[m % 2][:],
                                   in_=x_in[m * 128:(m + 1) * 128, :]))
            stAO.tick(eng.dma_start(out=ao_t[m % 2][:],
                                    in_=ao_scr[m * 128:(m + 1) * 128, :]))

    @block.vector
    def _(eng):
        for m in range(NM):
            stX.wait(eng, m)
            stAO.wait(eng, m)
            if m >= 1:
                stSx1.wait(eng, m - 1)
                eng.wait_ge(s_ac.h, AC_P2 + m)
                eng.wait_ge(s_pe.h, PE_P2 + m)
            if m >= 2:
                eng.wait_ge(s_dv.h, DV_P2 + m - 1)
            seq(eng, s_dq, eng.tensor_tensor(out=t1_b[m % 2][:],
                                             in0=ao_t[m % 2][:],
                                             in1=bc1_b[:], op=ALU.mult))
            seq(eng, s_dq, eng.tensor_tensor(out=x1f[:], in0=x_t[m % 2][:],
                                             in1=t1_b[m % 2][:], op=ALU.add))
            eng.tensor_tensor(out=x1f[:], in0=x1f[:], in1=bc2_b[:],
                              op=ALU.add).then_inc(s_dv.h, 1)
            s_dv.n += 1
    assert s_dv.n == DV_R2

    @block.scalar
    def _(eng):
        for m in range(NM):
            eng.wait_ge(s_dv.h, DV_P2 + m + 1)
            if m >= 1:
                eng.wait_ge(s_pe.h, PE_P2 + m)
            eng.activation(out=sq_t[:], in_=x1f[:],
                           func=AF.Square).then_inc(s_ac.h, 1)
            s_ac.n += 1
            stSx1.tick(eng.dma_start(out=x3_scr[m * 128:(m + 1) * 128, :],
                                     in_=x1f[:]))
    assert s_ac.n == AC_R2

    @block.tensor
    def _(eng):
        for m in range(NM):
            eng.wait_ge(s_dv.h, DV_P2 + m + 1)
            eng.wait_ge(s_ac.h, AC_P2 + m + 1)
            last = None
            for half in range(2):
                sl = slice(half * 512, half * 512 + 512)
                eng.matmul(ps_sum[:, sl], ones_f[:, 0:1], x1f[:, sl],
                           start=(m == 0), stop=(m == NM - 1))
                last = eng.matmul(ps_sq[:, sl], ones_bf[:, 0:1], sq_t[:, sl],
                                  start=(m == 0), stop=(m == NM - 1))
            last.then_inc(s_pe.h, 1)
            s_pe.n += 1
    assert s_pe.n == PE_R2

    rows_ln(PE_R2)
    assert (s_dv.n, s_ac.n, s_pe.n) == (DV_H3, AC_P3, PE_GB)

    @block.sync
    def _(eng):
        for m in range(NM):
            stSx1.wait(eng, m)
            if m < 2:
                eng.wait_ge(s_dv.h, DV_R2)
            else:
                eng.wait_ge(s_dv.h, DV_H3 + m - 1)
            stX.tick(eng.dma_start(out=x_t[m % 2][:],
                                   in_=x3_scr[m * 128:(m + 1) * 128, :]))

    @block.vector
    def _(eng):
        for m in range(NM):
            stX.wait(eng, NM + m)
            if m >= 2:
                eng.wait_ge(s_dv.h, DV_H3 + m - 1)
            seq(eng, s_dq, eng.tensor_tensor(out=t1_b[m % 2][:],
                                             in0=x_t[m % 2][:],
                                             in1=bc1_b[:], op=ALU.mult))
            eng.tensor_tensor(out=h3[:, m * T:(m + 1) * T], in0=t1_b[m % 2][:],
                              in1=bc2_b[:], op=ALU.add).then_inc(s_dv.h, 1)
            s_dv.n += 1
    assert s_dv.n == DV_GB

    @block.sync
    def _(eng):
        for e in range(3):
            eng.wait_ge(s_dv.h, DV_GB + e)
            stGC.tick(eng.dma_start(out=g1[:, 0:T], in_=c0_in[e]))
            stGC.tick(eng.dma_start(out=g2[:, 0:256], in_=c1_in[e]))

    @block.gpsimd
    def _(eng):
        for e in range(3):
            eng.wait_ge(s_dv.h, DV_GB + e)
            stGM.tick(eng.dma_start(out=x_t[0][:, 0:T], in_=m0_in[e]))
            stGM.tick(eng.dma_start(out=x_t[1][:, 0:256], in_=m1_in[e]))

    @block.vector
    def _(eng):
        for e in range(3):
            stGC.wait(eng, 2 * e)
            stGC.wait(eng, 2 * e + 1)
            stGM.wait(eng, 2 * e)
            stGM.wait(eng, 2 * e + 1)
            if e == 0:
                seq(eng, s_dq, eng.tensor_tensor(
                    out=x1f[:, 0:T], in0=g1[:, 0:T], in1=x_t[0][:, 0:T],
                    op=ALU.mult))
                seq(eng, s_dq, eng.tensor_tensor(
                    out=g2[:, 512:768], in0=g2[:, 0:256], in1=x_t[1][:, 0:256],
                    op=ALU.mult))
            else:
                seq(eng, s_dq, eng.tensor_tensor(
                    out=g1[:, 0:T], in0=g1[:, 0:T], in1=x_t[0][:, 0:T],
                    op=ALU.mult))
                seq(eng, s_dq, eng.tensor_tensor(
                    out=x1f[:, 0:T], in0=x1f[:, 0:T], in1=g1[:, 0:T],
                    op=ALU.add))
                seq(eng, s_dq, eng.tensor_tensor(
                    out=g2[:, 0:256], in0=g2[:, 0:256], in1=x_t[1][:, 0:256],
                    op=ALU.mult))
                seq(eng, s_dq, eng.tensor_tensor(
                    out=g2[:, 512:768], in0=g2[:, 512:768], in1=g2[:, 0:256],
                    op=ALU.add))
            eng.nop().then_inc(s_dv.h, 1)
            s_dv.n += 1
        eng.tensor_copy(ws0_b[:], x1f[:, 0:T]).then_inc(s_dv.h, 1)
        s_dv.n += 1
        eng.tensor_copy(ws1_b[:], g2[:, 512:768]).then_inc(s_dv.h, 1)
        s_dv.n += 1
    WS_DV = s_dv.n

    @block.sync
    def _(eng):
        for c in range(8):
            if c >= 1:
                eng.wait_ge(s_pe.h, PE_GB + c)
            else:
                eng.wait_ge(s_ac.h, AC_P3)
                eng.wait_ge(s_dv.h, WS_DV)
            stQT.tick(eng.dma_start(out=qtb[:], in_=qt_in[c]))

    ws0v = ws0_b.rearrange("p (q r) -> p q r", r=32)
    ws1v = ws1_b.rearrange("p (q r) -> p q r", r=8)
    qtv = qtb.rearrange("p (q c) -> p q c", c=512)
    encv = enc_bf.rearrange("p (q r) -> p q r", r=40)

    @block.tensor
    def _(eng):
        eng.wait_ge(s_dv.h, WS_DV)
        for c in range(8):
            stQT.wait(eng, c)
            if c >= 1:
                eng.wait_ge(s_dv.h, WS_DV + c)
            last = None
            for kk in range(NK):
                eng.matmul(psA[0:32, 0:512], ws0v[:, kk], qtv[:, kk],
                           start=(kk == 0), stop=(kk == NK - 1))
            for kk in range(NK):
                last = eng.matmul(psA[32:40, 0:512], ws1v[:, kk], qtv[:, kk],
                                  start=(kk == 0), stop=(kk == NK - 1))
            last.then_inc(s_pe.h, 1)
            s_pe.n += 1
        last = None
        for kk in range(NK):
            for half in range(2):
                c0_ = kk * T + half * 512
                last = eng.matmul(psB[0:40, half * 512:half * 512 + 512],
                                  encv[:, kk], h3[:, c0_:c0_ + 512],
                                  start=(kk == 0), stop=(kk == NK - 1))
        last.then_inc(s_pe.h, 1)
        s_pe.n += 1
    assert s_pe.n == PE_P3

    @block.vector
    def _(eng):
        for c in range(8):
            eng.wait_ge(s_pe.h, PE_GB + c + 1)
            eng.tensor_scalar(out=gt_sb[0:40, c * 512:(c + 1) * 512],
                              in0=psA[0:40, 0:512], scalar1=0.5, scalar2=None,
                              op0=ALU.mult).then_inc(s_dv.h, 1)
            s_dv.n += 1
        eng.wait_ge(s_pe.h, PE_P3)
        eng.tensor_copy(e_bf[0:40, :], psB[0:40, :]).then_inc(s_dv.h, 1)
        s_dv.n += 1
    assert s_dv.n == DV_P3, s_dv.n

    @block.gpsimd
    def _(eng):
        for m in range(NM):
            if m < 2:
                eng.wait_ge(s_pe.h, PE_P3)
            else:
                eng.wait_ge(s_pe.h, PE_P3 + 2 * (m - 2) + 1)
            stW.tick(eng.dma_start(out=wbuf[:, (m % 2) * 4096:
                                           (m % 2) * 4096 + 4096],
                                   in_=wbT[m]))

    @block.tensor
    def _(eng):
        eng.wait_ge(s_dv.h, DV_P3)
        for m in range(NM):
            ps = psA if m % 2 == 0 else psB
            if m >= 1:
                eng.wait_ge(s_ac.h, AC_P3 + 2 * m)
                last = None
                for half in range(2):
                    sl = slice(half * 512, half * 512 + 512)
                    c0_ = (m - 1) * T + half * 512
                    eng.matmul(ps_sum[:, sl], ones_bf[:, 0:1],
                               ffn[:, c0_:c0_ + 512], start=(m == 1),
                               stop=False)
                    last = eng.matmul(ps_sq[:, sl], ones_bf[:, 0:1],
                                      sq_t[:, sl], start=(m == 1), stop=False)
                last.then_inc(s_pe.h, 1)
                s_pe.n += 1
            stW.wait(eng, NM + m)
            if m >= 2:
                eng.wait_ge(s_ac.h, AC_P3 + 2 * (m - 1))
            for kk in range(NK):
                for half in range(2):
                    c0_ = kk * T + half * 512
                    eng.matmul(ps[:, half * 512:half * 512 + 512],
                               w4[:, m % 2, kk], h3[:, c0_:c0_ + 512],
                               start=(kk == 0), stop=False)
            last = None
            for half in range(2):
                sl = slice(half * 512, half * 512 + 512)
                last = eng.matmul(ps[:, sl],
                                  gt_sb[0:40, m * 128:(m + 1) * 128],
                                  e_bf[0:40, sl], start=False, stop=True)
            last.then_inc(s_pe.h, 1)
            s_pe.n += 1
        eng.wait_ge(s_ac.h, AC_P3 + 2 * NM)
        last = None
        for half in range(2):
            sl = slice(half * 512, half * 512 + 512)
            c0_ = (NM - 1) * T + half * 512
            eng.matmul(ps_sum[:, sl], ones_bf[:, 0:1], ffn[:, c0_:c0_ + 512],
                       start=False, stop=True)
            last = eng.matmul(ps_sq[:, sl], ones_bf[:, 0:1], sq_t[:, sl],
                              start=False, stop=True)
        last.then_inc(s_pe.h, 1)
        s_pe.n += 1
    assert s_pe.n == PE_R3

    @block.scalar
    def _(eng):
        for m in range(NM):
            eng.wait_ge(s_pe.h, PE_P3 + 2 * m + 1)
            ps = psA if m % 2 == 0 else psB
            eng.activation(out=ffn[:, m * T:(m + 1) * T],
                           in_=ps[:], func=AF.Identity,
                           bias=bb_sb[:, m:m + 1]).then_inc(s_ac.h, 1)
            eng.activation(out=sq_t[:], in_=ps[:], func=AF.Square,
                           bias=bb_sb[:, m:m + 1]).then_inc(s_ac.h, 1)
            s_ac.n += 2
    assert s_ac.n == AC_R3

    rows_ln(PE_R3)
    assert (s_dv.n, s_ac.n, s_pe.n) == (DV_P4, AC_END, PE_END)

    @block.sync
    def _(eng):
        for m in range(NM):
            if m < 2:
                eng.wait_ge(s_dv.h, DV_P4)
            else:
                eng.wait_ge(s_dv.h, DV_P4 + m - 1)
            stX.tick(eng.dma_start(out=x_t[m % 2][:],
                                   in_=x3_scr[m * 128:(m + 1) * 128, :]))

    @block.vector
    def _(eng):
        for m in range(NM):
            stX.wait(eng, 2 * NM + m)
            if m >= 1:
                stOut.wait(eng, m - 1)
            if m >= 2:
                eng.wait_ge(s_dv.h, DV_P4 + m - 1)
            seq(eng, s_dq, eng.tensor_tensor(out=t1_b[m % 2][:],
                                             in0=ffn[:, m * T:(m + 1) * T],
                                             in1=bc1_b[:], op=ALU.mult))
            seq(eng, s_dq, eng.tensor_tensor(out=x1f[:], in0=x_t[m % 2][:],
                                             in1=t1_b[m % 2][:], op=ALU.add))
            eng.tensor_tensor(out=x1f[:], in0=x1f[:], in1=bc2_b[:],
                              op=ALU.add).then_inc(s_dv.h, 1)
            s_dv.n += 1
    assert s_dv.n == DV_END

    @block.scalar
    def _(eng):
        for m in range(NM):
            eng.wait_ge(s_dv.h, DV_P4 + m + 1)
            stOut.tick(eng.dma_start(out=x4_out[m * 128:(m + 1) * 128, :],
                                     in_=x1f[:]))
        stOut.wait_all(eng)

    return k


def build_l3():
    k = K()
    nc = k.nc
    S, NCH, NH, TC, NTC = 4096, 32, 8, 512, 8
    TOPK = 819.0
    ISQ = float(1.0 / np.sqrt(128.0))

    x_in = k.din("x_in", [4096, S], F32)
    wqk = k.din("wqk", [16, 128, 32 * 128], F32)
    wvT = k.din("wvT", [32, 128, 1024], F32)
    bqk = k.din("bqk", [128, 16], F32)
    bv_row = k.din("bv_row", [1, 1024], F32)
    idn_in = k.din("idn_in", [128, 128], BF16)
    ctx_out = k.dout("ctx_out", [1024, S], BF16)

    q_scr = k.dscratch("q_scr", [1024, S], BF16)
    k_scr = k.dscratch("k_scr", [1024, S], BF16)
    vT_scr = k.dscratch("vT_scr", [NH, 128, 32 * 128], BF16)

    witer = [8.0 * 2.0 ** (-i) for i in range(12)]
    OFF12 = 8.0 * (1.0 - 2.0 ** -10)
    cdram, csb = k.setup_consts(
        [EPS, 1.0 / 4096.0, -1.0, -0.5, ISQ, -8.0, 1229.25, -OFF12]
        + [0.5 * w for w in witer])

    x_sb = k.sbuf("x_sb", [128, 32 * TC], F32)
    hc_sb = k.sbuf("hc_sb", [128, 32 * TC], BF16)
    wbuf = k.sbuf("wbuf3", [128, 2 * 4096], BF16)
    vbuf = k.sbuf("vbuf3", [128, 2 * 512], BF16)
    s4_sb = k.sbuf("s4_sb", [128, 4 * S], BF16)
    kh_sb = k.sbuf("kh_sb", [128, S], BF16)
    qh_sb = k.sbuf("qh_sb", [128, S], BF16)
    vt_sb = k.sbuf("vt_sb", [128, 32 * 128], BF16)
    hsc = k.sbuf("hsc", [128, 512], BF16)
    pmt_sb = k.sbuf("pmt_sb", [128, 32 * 128], BF16)
    ctx_sb = k.sbuf("ctx_sb", [128, 2 * 128], BF16)
    e_sb = hc_sb[:, 0:S]
    bv_bc = k.sbuf("bv_bc", [128, 1024], BF16)
    idn_sb = k.sbuf("idn_sb", [128, 128], BF16)
    sq_b = k.sbuf("sq_b3", [128, TC], BF16)
    qkev = k.sbuf("qkev", [128, 2 * TC], BF16)
    vtev = k.sbuf("vtev", [128, 2 * 512], BF16)
    bqk_sb = k.sbuf("bqk_sb", [128, 16], F32)
    bvr_sb = k.sbuf("bvr_sb", [1, 1024], F32)
    ones_bf = k.sbuf("ones_bf3", [128, 1], BF16)
    ones_f = k.sbuf("ones_f3", [128, 1], F32)
    onesr_f = k.sbuf("onesr_f3", [1, 128], F32)
    bc1_b = k.sbuf("bc1_b3", [128, TC], BF16)
    bc2_b = k.sbuf("bc2_b3", [128, TC], BF16)
    selw = k.sbuf("selw", [128, 48], F32)
    rows_f = k.sbuf("rows_f3", [1, 3 * TC], F32)

    lo_c = selw[:, 0:4]
    hi_c = selw[:, 4:8]
    mid_c = selw[:, 8:12]
    cnt_c = selw[:, 12:16]
    sel_c = selw[:, 16:20]
    ntau_c = selw[:, 20:24]
    z_c = selw[:, 24:28]
    iz_c = selw[:, 28:32]
    nmid_c = selw[:, 32:36]
    sgn_c = selw[:, 36:38]
    accr0 = selw[:, 38:42]
    accr1 = selw[:, 42:46]
    r_mean = rows_f[0:1, 0:TC]
    r_rstd = rows_f[0:1, TC:2 * TC]
    r_tmp = rows_f[0:1, 2 * TC:3 * TC]

    psF = k.psum("psF", [128, 3072], F32)
    psT = k.psum("psT", [128, 2048], BF16)
    psQ = psF[:, 0:512]
    psR = psF[:, 512:1024]
    psV = psF[:, 1024:2048]
    ps_sum = psF[0:1, 2048:2560]
    ps_sq = psF[0:1, 2560:3072]
    psS = psF[:, 0:2048]
    psC = psF[:, 2048:2176]

    s_pe = k.sem("s_pe")
    s_dv = k.sem("s_dv")
    s_ac = k.sem("s_ac")
    s_dq = k.sem("s_dq")
    s_aq = k.sem("s_aq")
    stPre = Stream(k, "stPre", depth=1)
    stX = Stream(k, "stX")
    stW = Stream(k, "stW")
    stV = Stream(k, "stV")
    stQK = Stream(k, "stQK")
    stVS = Stream(k, "stVS", depth=8)
    stHL = Stream(k, "stHL", depth=3)
    stCO = Stream(k, "stCO")
    block = k.block()

    @block.sync
    def _(eng):
        stPre.tick(eng.dma_start(out=csb[:], in_=cdram[:]))
        stPre.tick(eng.dma_start(out=bqk_sb[:], in_=bqk[:]))
        stPre.tick(eng.dma_start(out=bvr_sb[:], in_=bv_row[:]))
        stPre.tick(eng.dma_start(out=idn_sb[:], in_=idn_in[:]))

    s_in = k.sem("s_in")

    @block.gpsimd
    def _(eng):
        eng.memset(ones_bf[:], 1.0)
        eng.memset(ones_f[:], 1.0)
        eng.memset(onesr_f[:], 1.0).then_inc(s_in.h, 1)
        s_in.n += 1

    @block.tensor
    def _(eng):
        stPre.wait_all(eng)
        eng.wait_ge(s_in.h, 1)
        eng.matmul(psV[:, 0:512], onesr_f[:1, :], bvr_sb[:1, 0:512],
                   start=True, stop=True)
        eng.matmul(psV[:, 512:1024], onesr_f[:1, :], bvr_sb[:1, 512:1024],
                   start=True, stop=True).then_inc(s_pe.h, 1)
        s_pe.n += 1

    @block.vector
    def _(eng):
        eng.wait_ge(s_pe.h, 1)
        eng.tensor_copy(bv_bc[:], psV[:]).then_inc(s_dv.h, 1)
        s_dv.n += 1
    PE0, DV0, AC0 = 1, 1, 0

    PE_TC, DV_TC, AC_TC = 113, 59, 33

    def pe0(tc):
        return PE0 + tc * PE_TC

    def dv0(tc):
        return DV0 + tc * DV_TC

    def ac0(tc):
        return AC0 + tc * AC_TC

    @block.sync
    def _(eng):
        for tc in range(NTC):
            for m in range(NCH):
                if tc >= 1:
                    eng.wait_ge(s_dv.h, dv0(tc - 1) + 3 + m + 1)
                stX.tick(eng.dma_start(
                    out=x_sb[:, m * TC:(m + 1) * TC],
                    in_=x_in[m * 128:(m + 1) * 128, tc * TC:(tc + 1) * TC]))

    @block.gpsimd
    def _(eng):
        for tc in range(NTC):
            for m in range(16):
                i = tc * 16 + m
                if i >= 2:
                    tc2, m2 = divmod(i - 2, 16)
                    eng.wait_ge(s_pe.h, pe0(tc2) + 33 + m2 + 1)
                stW.tick(eng.dma_start(
                    out=wbuf[:, (m % 2) * 4096:(m % 2) * 4096 + 4096],
                    in_=wqk[m]))
            for half in range(2):
                for kk in range(32):
                    i = tc * 64 + half * 32 + kk
                    if i >= 2:
                        tc2, r2 = divmod(i - 2, 64)
                        eng.wait_ge(s_pe.h, pe0(tc2) + 49 + r2 + 1)
                    stV.tick(eng.dma_start(
                        out=vbuf[:, (i % 2) * 512:(i % 2) * 512 + 512],
                        in_=wvT[kk][:, half * 512:half * 512 + 512]))

    @block.scalar
    def _(eng):
        for tc in range(NTC):
            for m in range(NCH):
                if m == 0:
                    for sx in stX.sems:
                        eng.wait_ge(sx.h, 16 * 16 * (tc + 1))
                if not (tc == 0 and m == 0):
                    prev = pe0(tc) + m if m >= 1 else pe0(tc - 1) + NCH
                    eng.wait_ge(s_pe.h, prev)
                eng.activation(out=sq_b[:], in_=x_sb[:, m * TC:(m + 1) * TC],
                               func=AF.Square).then_inc(s_ac.h, 1)
                s_ac.n += 1
            eng.wait_ge(s_dv.h, dv0(tc) + 1)
            seq(eng, s_aq, eng.activation(out=r_tmp[:], in_=r_rstd[:],
                                          func=AF.Ln, bias=EPS))
            eng.activation(out=r_rstd[:], in_=r_tmp[:], func=AF.Exp,
                           scale=-0.5).then_inc(s_ac.h, 1)
            s_ac.n += 1
            for m in range(16):
                eng.wait_ge(s_dv.h, dv0(tc) + 3 + NCH + m + 1)
                scr = q_scr if m < 8 else k_scr
                row = (m % 8) * 128
                stQK.tick(eng.dma_start(
                    out=scr[row:row + 128, tc * TC:(tc + 1) * TC],
                    in_=qkev[:, (m % 2) * TC:(m % 2) * TC + TC]))
            for half in range(2):
                for tq in range(4):
                    eng.wait_ge(s_dv.h,
                                dv0(tc) + 3 + NCH + 16 + 4 * half + tq + 1)
                    for hh in range(4):
                        h = half * 4 + hh
                        stVS.tick(eng.dma_start(
                            out=vT_scr[h][:, (tc * 4 + tq) * 128:
                                          (tc * 4 + tq + 1) * 128],
                            in_=vtev[:, (tq % 2) * 512 + hh * 128:
                                     (tq % 2) * 512 + (hh + 1) * 128]))
    assert s_ac.n == AC0 + NTC * AC_TC

    @block.tensor
    def _(eng):
        for tc in range(NTC):
            for m in range(NCH):
                eng.wait_ge(s_ac.h, ac0(tc) + m + 1)
                if tc >= 1 and m == 0:
                    eng.wait_ge(s_dv.h, dv0(tc - 1) + 1)
                eng.matmul(ps_sum, ones_f[:, 0:1],
                           x_sb[:, m * TC:(m + 1) * TC],
                           start=(m == 0), stop=(m == NCH - 1))
                eng.matmul(ps_sq, ones_bf[:, 0:1], sq_b[:],
                           start=(m == 0),
                           stop=(m == NCH - 1)).then_inc(s_pe.h, 1)
                s_pe.n += 1
            eng.wait_ge(s_dv.h, dv0(tc) + 2)
            eng.matmul(psQ, onesr_f[:1, :], r_rstd[:, :],
                       start=True, stop=True)
            eng.matmul(psR, onesr_f[:1, :], r_tmp[:, :],
                       start=True, stop=True).then_inc(s_pe.h, 1)
            s_pe.n += 1
            for m in range(16):
                eng.wait_ge(s_dv.h, dv0(tc) + 3 + NCH)
                stW.wait(eng, tc * 16 + m)
                if m >= 2:
                    eng.wait_ge(s_dv.h, dv0(tc) + 3 + NCH + m - 1)
                elif m < 2:
                    pass
                ps = psQ if m % 2 == 0 else psR
                last = None
                for kk in range(32):
                    last = eng.matmul(
                        ps, wbuf[:, (m % 2) * 4096 + kk * 128:
                                 (m % 2) * 4096 + (kk + 1) * 128],
                        hc_sb[:, kk * TC:kk * TC + 512],
                        start=(kk == 0), stop=(kk == 31))
                last.then_inc(s_pe.h, 1)
                s_pe.n += 1
            for half in range(2):
                eng.wait_ge(s_dv.h, dv0(tc) + 3 + NCH + 16 + 4 * half)
                for kk in range(32):
                    i = tc * 64 + half * 32 + kk
                    stV.wait(eng, i)
                    last = None
                    for tq in range(4):
                        last = eng.matmul(
                            psF[:, tq * 512:(tq + 1) * 512],
                            hc_sb[:, kk * TC + tq * 128:
                                  kk * TC + (tq + 1) * 128],
                            vbuf[:, (i % 2) * 512:(i % 2) * 512 + 512],
                            start=(kk == 0), stop=(kk == 31))
                    last.then_inc(s_pe.h, 1)
                    s_pe.n += 1
    assert s_pe.n == PE0 + NTC * PE_TC

    @block.vector
    def _(eng):
        for tc in range(NTC):
            eng.wait_ge(s_pe.h, pe0(tc) + NCH)
            seq(eng, s_dq, eng.tensor_scalar(
                out=r_mean[:], in0=ps_sum, scalar1=1.0 / 4096.0,
                scalar2=None, op0=ALU.mult))
            seq(eng, s_dq, eng.tensor_scalar(
                out=r_tmp[:], in0=ps_sq, scalar1=1.0 / 4096.0,
                scalar2=None, op0=ALU.mult))
            seq(eng, s_dq, eng.tensor_tensor(
                out=r_rstd[:], in0=r_mean[:], in1=r_mean[:], op=ALU.mult))
            eng.tensor_tensor(out=r_rstd[:], in0=r_tmp[:], in1=r_rstd[:],
                              op=ALU.subtract).then_inc(s_dv.h, 1)
            s_dv.n += 1
            eng.wait_ge(s_ac.h, ac0(tc) + NCH + 1)
            seq(eng, s_dq, eng.tensor_tensor(out=r_tmp[:], in0=r_mean[:],
                                             in1=r_rstd[:], op=ALU.mult))
            eng.tensor_scalar(out=r_tmp[:], in0=r_tmp[:], scalar1=-1.0,
                              scalar2=None, op0=ALU.mult).then_inc(s_dv.h, 1)
            s_dv.n += 1
            eng.wait_ge(s_pe.h, pe0(tc) + NCH + 1)
            seq(eng, s_dq, eng.tensor_copy(bc1_b[:], psQ))
            seq(eng, s_dq, eng.tensor_copy(bc2_b[:], psR))
            eng.nop().then_inc(s_dv.h, 1)
            s_dv.n += 1
            for m in range(NCH):
                if tc >= 1 and m == 0:
                    eng.wait_ge(s_pe.h, pe0(tc - 1) + PE_TC)
                if m >= 1:
                    eng.wait_ge(s_dv.h, dv0(tc) + 3 + m)
                seq(eng, s_dq, eng.tensor_tensor(
                    out=hsc[:], in0=x_sb[:, m * TC:(m + 1) * TC],
                    in1=bc1_b[:], op=ALU.mult))
                eng.tensor_tensor(out=hc_sb[:, m * TC:(m + 1) * TC],
                                  in0=hsc[:], in1=bc2_b[:],
                                  op=ALU.add).then_inc(s_dv.h, 1)
                s_dv.n += 1
            for m in range(16):
                eng.wait_ge(s_pe.h, pe0(tc) + NCH + 1 + m + 1)
                stQK.wait(eng, tc * 16 + m - 2)
                ps = psQ if m % 2 == 0 else psR
                if m < 8:
                    eng.tensor_scalar(out=qkev[:, (m % 2) * TC:
                                               (m % 2) * TC + TC],
                                      in0=ps, scalar1=bqk_sb[:, m:m + 1],
                                      scalar2=ISQ, op0=ALU.add,
                                      op1=ALU.mult).then_inc(s_dv.h, 1)
                else:
                    eng.tensor_scalar(out=qkev[:, (m % 2) * TC:
                                               (m % 2) * TC + TC],
                                      in0=ps, scalar1=bqk_sb[:, m:m + 1],
                                      scalar2=None,
                                      op0=ALU.add).then_inc(s_dv.h, 1)
                s_dv.n += 1
            for half in range(2):
                eng.wait_ge(s_pe.h, pe0(tc) + 49 + 32 * (half + 1))
                for tq in range(4):
                    u = (tc * 2 + half) * 4 + tq
                    for hh in range(4):
                        stVS.wait(eng, (u - 2) * 4 + hh)
                    eng.tensor_tensor(
                        out=vtev[:, (tq % 2) * 512:(tq % 2) * 512 + 512],
                        in0=psF[:, tq * 512:(tq + 1) * 512],
                        in1=bv_bc[:, half * 512:half * 512 + 512],
                        op=ALU.add).then_inc(s_dv.h, 1)
                    s_dv.n += 1
    assert s_dv.n == DV0 + NTC * DV_TC

    PE_QKV = PE0 + NTC * PE_TC
    DV_QKV = DV0 + NTC * DV_TC
    AC_QKV = AC0 + NTC * AC_TC

    ITER = 10
    SEL = ITER + 1
    PE_G, DV_G, AC_G = 28, 8 + (ITER + 1) + 28, 1 + 8
    nmid_marks = [0] * (NH * 8)

    def pe1(h, g):
        return PE_QKV + (h * 8 + g) * PE_G

    def dv1(h, g):
        return DV_QKV + (h * 8 + g) * DV_G

    def ac1(h, g):
        return AC_QKV + (h * 8 + g) * AC_G

    def sbase(h, g):
        return PE_QKV + (h * 8 + g) * PE_G

    def tbase(h, g):
        return PE_QKV + (h * 8 + g) * PE_G + 8

    @block.sync
    def _(eng):
        for h in range(NH):
            if h == 0:
                stQK.wait_all(eng)
                stVS.wait_all(eng)
            else:
                eng.wait_ge(s_pe.h, pe1(h, 0))
            stHL.tick(eng.dma_start(out=kh_sb[:],
                                    in_=k_scr[h * 128:(h + 1) * 128, :]))
            stHL.tick(eng.dma_start(out=qh_sb[:],
                                    in_=q_scr[h * 128:(h + 1) * 128, :]))
            stHL.tick(eng.dma_start(out=vt_sb[:], in_=vT_scr[h]))

    @block.tensor
    def _(eng):
        def emit_scores(h, g):
            base_dv = dv1(h, g)
            for j in range(4):
                qb = g * 4 + j
                for r in range(2):
                    k8 = 2 * j + r
                    if k8 == 0:
                        pair = h * 8 + g
                        eng.wait_ge(s_dv.h,
                                    DV_QKV + (pair - 1) * DV_G + 8
                                    if pair else DV_QKV)
                    else:
                        eng.wait_ge(s_dv.h, base_dv + k8)
                    last = None
                    for c in range(4):
                        cc = r * 4 + c
                        last = eng.matmul(
                            psS[:, c * 512:(c + 1) * 512],
                            qh_sb[:, qb * 128:(qb + 1) * 128],
                            kh_sb[:, cc * 512:(cc + 1) * 512],
                            start=True, stop=True)
                    last.then_inc(s_pe.h, 1)
                    s_pe.n += 1

        def emit_trctx(h, g):
            base_dv = dv1(h, g)
            for j in range(4):
                for tg in range(4):
                    eng.wait_ge(s_ac.h, ac1(h, g) + 1 + 2 * j + 2)
                    if h + g + j + tg > 0:
                        eng.wait_ge(s_dv.h,
                                    base_dv + 8 + SEL + 7 * j + 2 + tg - 1)
                    last = None
                    for t8 in range(8):
                        kt = tg * 8 + t8
                        last = eng.transpose(
                            psT[:, (tg % 2) * 1024 + t8 * 128:
                                (tg % 2) * 1024 + (t8 + 1) * 128],
                            s4_sb[:, (g * 4 + j) % 4 * S + kt * 128:
                                  (g * 4 + j) % 4 * S + (kt + 1) * 128],
                            idn_sb[:])
                    last.then_inc(s_pe.h, 1)
                    s_pe.n += 1
                eng.wait_ge(s_dv.h, base_dv + 8 + SEL + 7 * j + 6)
                last = None
                for kt in range(32):
                    last = eng.matmul(psC,
                                      vt_sb[:, kt * 128:(kt + 1) * 128],
                                      pmt_sb[:, kt * 128:(kt + 1) * 128],
                                      start=(kt == 0), stop=(kt == 31))
                last.then_inc(s_pe.h, 1)
                s_pe.n += 1

        for h in range(NH):
            for i in range(3):
                stHL.wait(eng, 3 * h + i)
            for g in range(8):
                assert s_pe.n == sbase(h, g), (h, g, s_pe.n)
                emit_scores(h, g)
                assert s_pe.n == tbase(h, g), (h, g, s_pe.n)
                emit_trctx(h, g)
    assert s_pe.n == PE_QKV + NH * 8 * PE_G

    @block.vector
    def _(eng):
        for h in range(NH):
            for g in range(8):
                base_pe = pe1(h, g)
                base_dv = dv1(h, g)
                for j in range(4):
                    for r in range(2):
                        eng.wait_ge(s_pe.h, sbase(h, g) + 2 * j + r + 1)
                        acc = (accr0 if r == 0 else accr1)[:, j:j + 1]
                        eng.tensor_scalar(
                            out=s4_sb[:, j * S + r * 2048:j * S + r * 2048 + 2048],
                            in0=psS, scalar1=1.0, scalar2=0.0,
                            op0=ALU.mult, op1=ALU.add,
                            accum_out=acc).then_inc(s_dv.h, 1)
                        s_dv.n += 1
                eng.wait_ge(s_dv.h, dv1(h, g) + 8)
                seq(eng, s_dq, eng.tensor_tensor(
                    out=lo_c, in0=accr0, in1=accr1, op=ALU.add))
                seq(eng, s_dq, eng.tensor_scalar(
                    out=lo_c, in0=lo_c, scalar1=1.0 / 4096.0, scalar2=-2.0,
                    op0=ALU.mult, op1=ALU.add))
                nmid_marks[h * 8 + g] = s_dq.n
                lo01 = lo_c[:, 0:2]
                mid01 = mid_c[:, 0:2]
                sel01 = sel_c[:, 0:2]
                z01 = z_c[:, 0:2]
                for it in range(ITER):
                    w = 8.0 * 2.0 ** (-it)
                    seq(eng, s_dq, eng.tensor_scalar(
                        out=mid01, in0=lo01, scalar1=w, scalar2=None,
                        op0=ALU.add))
                    for j in range(2):
                        seq(eng, s_dq, eng.tensor_scalar(
                            out=e_sb[:], in0=s4_sb[:, j * S:(j + 1) * S],
                            scalar1=mid_c[:, j:j + 1], scalar2=0.0,
                            op0=ALU.is_ge, op1=ALU.add,
                            accum_out=cnt_c[:, j:j + 1]))
                    seq(eng, s_dq, eng.tensor_scalar(
                        out=sel01, in0=cnt_c[:, 0:2], scalar1=TOPK,
                        scalar2=None, op0=ALU.is_ge))
                    seq(eng, s_dq, eng.scalar_tensor_tensor(
                        out=lo01, in0=sel01, scalar=w, in1=lo01,
                        op0=ALU.mult, op1=ALU.add))
                    eng.nop().then_inc(s_dv.h, 1)
                    s_dv.n += 1
                seq(eng, s_dq, eng.tensor_scalar(
                    out=ntau_c[:, 0:2], in0=lo01, scalar1=-1.0, scalar2=None,
                    op0=ALU.mult))
                eng.nop().then_inc(s_dv.h, 1)
                s_dv.n += 1
                for j in range(4):
                    qb = g * 4 + j
                    eng.wait_ge(s_ac.h, ac1(h, g) + 1 + 2 * j + 1)
                    seq(eng, s_dq, eng.scalar_tensor_tensor(
                        out=s4_sb[:, j * S:(j + 1) * S], in0=e_sb[:],
                        scalar=1.0, in1=e_sb[:],
                        op0=ALU.is_ge, op1=ALU.mult,
                        accum_out=z_c[:, j:j + 1]))
                    eng.nop().then_inc(s_dv.h, 1)
                    s_dv.n += 1
                    seq(eng, s_dq, eng.reciprocal(iz_c[:, j:j + 1],
                                                  z_c[:, j:j + 1]))
                    eng.nop().then_inc(s_dv.h, 1)
                    s_dv.n += 1
                    for tg in range(4):
                        eng.wait_ge(s_pe.h,
                                    tbase(h, g) + 5 * j + tg + 1)
                        eng.tensor_copy(
                            pmt_sb[:, tg * 1024:(tg + 1) * 1024],
                            psT[:, (tg % 2) * 1024:(tg % 2) * 1024 + 1024]
                        ).then_inc(s_dv.h, 1)
                        s_dv.n += 1
                    eng.wait_ge(s_pe.h, tbase(h, g) + 5 * j + 5)
                    stCO.wait(eng, (h * 32 + qb) - 2)
                    eng.tensor_copy(ctx_sb[:, (qb % 2) * 128:
                                           (qb % 2) * 128 + 128],
                                    psC).then_inc(s_dv.h, 1)
                    s_dv.n += 1
    assert s_dv.n == DV_QKV + NH * 8 * DV_G, s_dv.n

    @block.scalar
    def _(eng):
        for h in range(NH):
            for g in range(8):
                eng.wait_ge(s_dv.h, dv1(h, g) + 8)
                eng.wait_ge(s_dq.h, nmid_marks[h * 8 + g])
                lo23 = lo_c[:, 2:4]
                lo23b = selw[:, 46:48]
                cur = lo23
                for it in range(ITER):
                    w = 8.0 * 2.0 ** (-it)
                    nxt = lo23b if cur is lo23 else lo23
                    seq(eng, s_aq, eng.activation(
                        out=nmid_c[:, 2:4], in_=cur,
                        func=AF.Identity, scale=-1.0, bias=-8.0))
                    seq(eng, s_aq, eng.activation(
                        out=x_sb[:, 0:4096], in_=s4_sb[:, 2 * S:3 * S],
                        func=AF.Sign, bias=nmid_c[:, 2:3],
                        accum_out=sgn_c[:, 0:1]))
                    seq(eng, s_aq, eng.activation(
                        out=x_sb[:, 4096:8192], in_=s4_sb[:, 3 * S:4 * S],
                        func=AF.Sign, bias=nmid_c[:, 3:4],
                        accum_out=sgn_c[:, 1:2]))
                    seq(eng, s_aq, eng.activation(
                        out=sel_c[:, 2:4], in_=sgn_c,
                        func=AF.Sign, scale=0.5, bias=1229.25))
                    seq(eng, s_aq, eng.activation(
                        out=nxt[:, 0:1], in_=sel_c[:, 2:3],
                        func=AF.Identity, scale=0.5 * w, bias=cur[:, 0:1]))
                    seq(eng, s_aq, eng.activation(
                        out=nxt[:, 1:2], in_=sel_c[:, 3:4],
                        func=AF.Identity, scale=0.5 * w, bias=cur[:, 1:2]))
                    cur = nxt
                seq(eng, s_aq, eng.activation(
                    out=ntau_c[:, 2:4], in_=cur,
                    func=AF.Identity, scale=-1.0, bias=-OFF12))
                eng.nop().then_inc(s_ac.h, 1)
                s_ac.n += 1
                for j in range(4):
                    qb = g * 4 + j
                    eng.wait_ge(s_dv.h, dv1(h, g) + 8 + SEL)
                    if j >= 1:
                        eng.wait_ge(s_dv.h, dv1(h, g) + 8 + SEL + 7 * (j - 1) + 1)
                    eng.activation(out=e_sb[:], in_=s4_sb[:, j * S:(j + 1) * S],
                                   func=AF.Exp,
                                   bias=ntau_c[:, j:j + 1]).then_inc(s_ac.h, 1)
                    s_ac.n += 1
                    eng.wait_ge(s_dv.h, dv1(h, g) + 8 + SEL + 7 * j + 2)
                    eng.activation(out=s4_sb[:, j * S:(j + 1) * S],
                                   in_=s4_sb[:, j * S:(j + 1) * S],
                                   func=AF.Identity,
                                   scale=iz_c[:, j:j + 1]).then_inc(s_ac.h, 1)
                    s_ac.n += 1
                    eng.wait_ge(s_dv.h, dv1(h, g) + 8 + SEL + 7 * j + 7)
                    stCO.tick(eng.dma_start(
                        out=ctx_out[h * 128:(h + 1) * 128,
                                    qb * 128:(qb + 1) * 128],
                        in_=ctx_sb[:, (qb % 2) * 128:(qb % 2) * 128 + 128]))
        stCO.wait_all(eng)
    assert s_ac.n == AC_QKV + NH * 8 * AC_G

    return k


def build_l1a(T=1024):
    k = K()
    nc = k.nc
    NM = 32

    x_in = k.din("x_in", [4096, T], F32)
    nw_in = k.din("nw_in", [128, 32], F32)
    h_out = k.dout("h_out", [4096, T], BF16)
    t_out = k.dout("t_out", [4096, T], BF16)

    cdram, csb = k.setup_consts([EPS, 1.0 / 4096.0, -1.0, -0.5])

    t_sb = k.sbuf("t_sb", [128, 32 * T], BF16)
    af = k.sbuf("af1", [128, 6 * T], F32)
    sq_t = k.sbuf("sq_t1", [128, T], BF16)
    hsc = k.sbuf("hsc1", [128, T], BF16)
    nw_sb = k.sbuf("nw_sb", [128, 32], F32)
    ones_bf = k.sbuf("ones_bf1", [128, 1], BF16)
    ones_f = k.sbuf("ones_f1", [128, 1], F32)
    onesr_f = k.sbuf("onesr_f1", [1, 128], F32)
    bc1_b = k.sbuf("bc1_b1", [128, T], BF16)
    bc2_b = k.sbuf("bc2_b1", [128, T], BF16)

    x_t = [af[:, 0:T], af[:, T:2 * T]]
    r_mean = af[0:1, 3 * T:4 * T]
    r_rstd = af[0:1, 4 * T:5 * T]
    r_tmp = af[0:1, 5 * T:6 * T]

    psA = k.psum("psA1", [128, T], F32)
    psB = k.psum("psB1", [128, T], F32)
    ps_sum = k.psum("ps_sum1", [1, T], F32)
    ps_sq = k.psum("ps_sq1", [1, T], F32)

    s_pe = k.sem("s_pe")
    s_dv = k.sem("s_dv")
    s_ac = k.sem("s_ac")
    s_dq = k.sem("s_dq")
    s_aq = k.sem("s_aq")
    s_in = k.sem("s_in")
    stX = Stream(k, "stX")
    stPre = Stream(k, "stPre", depth=1)
    stOut = Stream(k, "stOut")
    stT = Stream(k, "stT")
    block = k.block()


    @block.sync
    def _(eng):
        stPre.tick(eng.dma_start(out=csb[:], in_=cdram[:]))
        stPre.tick(eng.dma_start(out=nw_sb[:], in_=nw_in[:]))
        for m in range(NM):
            if m >= 2:
                eng.wait_ge(s_pe.h, m - 1)
            stX.tick(eng.dma_start(out=x_t[m % 2][:],
                                   in_=x_in[m * 128:(m + 1) * 128, :]))
        for m in range(NM):
            if m < 2:
                eng.wait_ge(s_pe.h, NM)
            else:
                eng.wait_ge(s_dv.h, 3 + m - 1)
            stX.tick(eng.dma_start(out=x_t[m % 2][:],
                                   in_=x_in[m * 128:(m + 1) * 128, :]))

    @block.gpsimd
    def _(eng):
        eng.memset(ones_bf[:], 1.0)
        eng.memset(ones_f[:], 1.0)
        eng.memset(onesr_f[:], 1.0).then_inc(s_in.h, 1)
        s_in.n += 1

    @block.scalar
    def _(eng):
        for m in range(NM):
            stX.wait(eng, m)
            if m >= 1:
                eng.wait_ge(s_pe.h, m)
            eng.activation(out=sq_t[:], in_=x_t[m % 2][:],
                           func=AF.Square).then_inc(s_ac.h, 1)
            s_ac.n += 1
        eng.wait_ge(s_dv.h, 1)
        seq(eng, s_aq, eng.activation(out=r_tmp[:], in_=r_rstd[:],
                                      func=AF.Ln, bias=EPS))
        eng.activation(out=r_rstd[:], in_=r_tmp[:], func=AF.Exp,
                       scale=-0.5).then_inc(s_ac.h, 1)
        s_ac.n += 1
        for m in range(NM):
            eng.wait_ge(s_dv.h, 3 + m + 1)
            if m >= 1:
                eng.wait_ge(s_pe.h, 33 + m)
            eng.activation(out=sq_t[:], in_=t_sb[:, m * T:(m + 1) * T],
                           func=AF.Square).then_inc(s_ac.h, 1)
            s_ac.n += 1
        eng.wait_ge(s_dv.h, 3 + 32 + 1)
        seq(eng, s_aq, eng.activation(out=r_tmp[:], in_=r_rstd[:],
                                      func=AF.Ln, bias=EPS))
        eng.activation(out=r_rstd[:], in_=r_tmp[:], func=AF.Exp,
                       scale=-0.5).then_inc(s_ac.h, 1)
        s_ac.n += 1

    @block.tensor
    def _(eng):
        stPre.wait_all(eng)
        eng.wait_ge(s_in.h, 1)
        for m in range(NM):
            eng.wait_ge(s_ac.h, m + 1)
            stX.wait(eng, m)
            last = None
            for half in range(2):
                sl = slice(half * 512, half * 512 + 512)
                eng.matmul(ps_sum[:, sl], ones_f[:, 0:1], x_t[m % 2][:, sl],
                           start=(m == 0), stop=(m == NM - 1))
                last = eng.matmul(ps_sq[:, sl], ones_bf[:, 0:1], sq_t[:, sl],
                                  start=(m == 0), stop=(m == NM - 1))
            last.then_inc(s_pe.h, 1)
            s_pe.n += 1
        eng.wait_ge(s_dv.h, 2)
        eng.matmul(psA[:, 0:512], onesr_f[:1, :], r_rstd[:, 0:512],
                   start=True, stop=True)
        eng.matmul(psA[:, 512:T], onesr_f[:1, :], r_rstd[:, 512:T],
                   start=True, stop=True)
        eng.matmul(psB[:, 0:512], onesr_f[:1, :], r_tmp[:, 0:512],
                   start=True, stop=True)
        eng.matmul(psB[:, 512:T], onesr_f[:1, :], r_tmp[:, 512:T],
                   start=True, stop=True).then_inc(s_pe.h, 1)
        s_pe.n += 1
        for m in range(NM):
            eng.wait_ge(s_ac.h, 33 + m + 1)
            eng.wait_ge(s_dv.h, 3 + m + 1)
            if m == 0:
                eng.wait_ge(s_dv.h, 3 + 1)
            last = None
            for half in range(2):
                sl = slice(half * 512, half * 512 + 512)
                last = eng.matmul(ps_sq[:, sl], ones_bf[:, 0:1], sq_t[:, sl],
                                  start=(m == 0), stop=(m == NM - 1))
            last.then_inc(s_pe.h, 1)
            s_pe.n += 1
        eng.wait_ge(s_dv.h, 3 + 32 + 2)
        eng.matmul(psA[:, 0:512], onesr_f[:1, :], r_rstd[:, 0:512],
                   start=True, stop=True)
        eng.matmul(psA[:, 512:T], onesr_f[:1, :], r_rstd[:, 512:T],
                   start=True, stop=True).then_inc(s_pe.h, 1)
        s_pe.n += 1

    @block.vector
    def _(eng):
        eng.wait_ge(s_pe.h, NM)
        seq(eng, s_dq, eng.tensor_scalar(
            out=r_mean[:], in0=ps_sum[:], scalar1=1.0 / 4096.0,
            scalar2=None, op0=ALU.mult))
        seq(eng, s_dq, eng.tensor_scalar(
            out=r_tmp[:], in0=ps_sq[:], scalar1=1.0 / 4096.0,
            scalar2=None, op0=ALU.mult))
        seq(eng, s_dq, eng.tensor_tensor(
            out=r_rstd[:], in0=r_mean[:], in1=r_mean[:], op=ALU.mult))
        eng.tensor_tensor(out=r_rstd[:], in0=r_tmp[:], in1=r_rstd[:],
                          op=ALU.subtract).then_inc(s_dv.h, 1)
        s_dv.n += 1
        eng.wait_ge(s_ac.h, 33)
        seq(eng, s_dq, eng.tensor_tensor(out=r_tmp[:], in0=r_mean[:],
                                         in1=r_rstd[:], op=ALU.mult))
        eng.tensor_scalar(out=r_tmp[:], in0=r_tmp[:], scalar1=-1.0,
                          scalar2=None, op0=ALU.mult).then_inc(s_dv.h, 1)
        s_dv.n += 1
        eng.wait_ge(s_pe.h, NM + 1)
        seq(eng, s_dq, eng.tensor_copy(bc1_b[:], psA[:]))
        seq(eng, s_dq, eng.tensor_copy(bc2_b[:], psB[:]))
        eng.nop().then_inc(s_dv.h, 1)
        s_dv.n += 1
        for m in range(NM):
            stX.wait(eng, NM + m)
            if m >= 1:
                eng.wait_ge(s_dv.h, 3 + m)
            seq(eng, s_dq, eng.tensor_tensor(
                out=hsc[:], in0=x_t[m % 2][:], in1=bc1_b[:], op=ALU.mult))
            eng.tensor_tensor(out=t_sb[:, m * T:(m + 1) * T], in0=hsc[:],
                              in1=bc2_b[:], op=ALU.add).then_inc(s_dv.h, 1)
            s_dv.n += 1
        eng.wait_ge(s_pe.h, NM + 1 + NM)
        seq(eng, s_dq, eng.tensor_scalar(
            out=r_rstd[:], in0=ps_sq[:], scalar1=1.0 / 4096.0,
            scalar2=None, op0=ALU.mult))
        eng.nop().then_inc(s_dv.h, 1)
        s_dv.n += 1
        eng.wait_ge(s_ac.h, 66)
        eng.nop().then_inc(s_dv.h, 1)
        s_dv.n += 1
        eng.wait_ge(s_pe.h, NM + 1 + NM + 1)
        seq(eng, s_dq, eng.tensor_copy(bc1_b[:], psA[:]))
        eng.nop().then_inc(s_dv.h, 1)
        s_dv.n += 1
        for m in range(NM):
            if m >= 1:
                stOut.wait(eng, m - 1)
            seq(eng, s_dq, eng.scalar_tensor_tensor(
                out=hsc[:], in0=t_sb[:, m * T:(m + 1) * T],
                scalar=nw_sb[:, m:m + 1], in1=bc1_b[:],
                op0=ALU.mult, op1=ALU.mult))
            eng.nop().then_inc(s_dv.h, 1)
            s_dv.n += 1

    @block.scalar
    def _(eng):
        for m in range(NM):
            eng.wait_ge(s_dv.h, 3 + 32 + 3 + m + 1)
            stOut.tick(eng.dma_start(out=h_out[m * 128:(m + 1) * 128, :],
                                     in_=hsc[:]))
            stT.tick(eng.dma_start(out=t_out[m * 128:(m + 1) * 128, :],
                                   in_=t_sb[:, m * T:(m + 1) * T]))
        stOut.wait_all(eng)
        stT.wait_all(eng)

    return k


def build_l1b():
    k = K()
    nc = k.nc
    NMT = 19
    GRPS = [(0, 3), (3, 3), (6, 3), (9, 3), (12, 3), (15, 3), (18, 1)]
    NG = len(GRPS)
    TB = 1024
    NTB = 8
    NH = 16
    BIG = -1.0e30

    h_in = k.din("h_in", [128, 32 * 8192], BF16)
    wg = k.din("wg", [19, 128, 32 * 128], F32)
    convw = k.din("convw", [128, 40], F32)
    convb = k.din("convb", [128, 10], F32)
    dtb = k.din("dtb", [128, 1], F32)
    alog = k.din("alog", [128, 1], F32)
    dp_row = k.din("dp_row", [1, 16], F32)
    gnw = k.din("gnw", [128, 8], F32)
    triu = k.din("triu", [128, 128], F32)
    mtril = k.din("mtril", [128, 128], F32)
    idn_f = k.din("idn_f", [128, 128], F32)
    idn_b = k.din("idn_b", [128, 128], BF16)
    y_out = k.dout("y_out", [1024, 8192], BF16)

    cdram, csb = k.setup_consts([EPS, 1.0 / 1024.0, -1.0, -0.5, 1.0])

    proj = k.sbuf("proj", [128, NMT * TB], BF16)
    wbuf = k.sbuf("wbufM", [128, 2 * 3 * 4096], BF16)
    hk = k.sbuf("hkM", [128, 2 * TB], BF16)
    cin = k.sbuf("cin", [128, 1027 + 1], BF16)
    tails = k.sbuf("tails", [128, 30], BF16)
    acc_f = k.sbuf("acc_f", [128, TB], F32)
    convw_sb = k.sbuf("convw_sb", [128, 40], F32)
    convb_sb = k.sbuf("convb_sb", [128, 10], F32)
    dtb_sb = k.sbuf("dtb_sb", [128, 1], F32)
    acol_sb = k.sbuf("acol_sb", [128, 2], F32)
    dtf = k.sbuf("dtf", [128, TB], F32)
    dtp_f = k.sbuf("dtp_f", [128, TB], F32)
    adt_f = k.sbuf("adt_f", [128, TB], F32)
    dp_sb = k.sbuf("dp_sb", [1, 16], F32)
    dpbc_sb = k.sbuf("dpbc_sb", [128, 16], F32)
    gnw_sb = k.sbuf("gnw_sb", [128, 8], F32)
    triu_sb = k.sbuf("triu_sb", [128, 128], F32)
    mtril_sb = k.sbuf("mtril_sb", [128, 128], F32)
    idnf_sb = k.sbuf("idnf_sb", [128, 128], F32)
    idnb_sb = k.sbuf("idnb_sb", [128, 128], BF16)
    adtT_sb = k.sbuf("adtT_sb", [128, 16], F32)
    dtpT_sb = k.sbuf("dtpT_sb", [128, 16], F32)
    acsT_sb = k.sbuf("acsT_sb", [128, 16], F32)
    nacsT_sb = k.sbuf("nacsT_sb", [128, 16], F32)
    acsbc_sb = k.sbuf("acsbc_sb", [128, 16], F32)
    dsT_sb = k.sbuf("dsT_sb", [128, 16], F32)
    eAT_sb = k.sbuf("eAT_sb", [128, 16], F32)
    egbc_sb = k.sbuf("egbc_sb", [128, 16], F32)
    acsl_sb = k.sbuf("acsl_sb", [128, 1], F32)
    rowh_sb = k.sbuf("rowh_sb", [1, 256], F32)
    diag_f = k.sbuf("diag_f", [128, 256], F32)
    ones_ff = k.sbuf("ones_ff", [128, 128], F32)
    junk16 = k.sbuf("junk16", [128, 128], F32)
    MT_b = k.sbuf("MT_b", [128, 128], BF16)
    LT_b = k.sbuf("LT_b", [128, 256], BF16)
    MLT_b = k.sbuf("MLT_b", [128, 256], BF16)
    BT_sb = k.sbuf("BT_sb", [128, 128], BF16)
    BTds_b = k.sbuf("BTds_b", [128, 256], BF16)
    hidT_b = k.sbuf("hidT_b", [128, 128], BF16)
    hdtT_b = k.sbuf("hdtT_b", [128, 128], BF16)
    ys1_b = k.sbuf("ys1_b", [128, 128], BF16)
    yT_b = k.sbuf("yT_b", [128, 128], BF16)
    prev_f = k.sbuf("prev_f", [128, NH * 64], F32)
    prev_b = k.sbuf("prev_b", [128, NH * 64], BF16)
    y_asm = k.sbuf("y_asm", [128, 8 * TB], BF16)
    grow = k.sbuf("grow", [1, 2 * TB], F32)
    gbc_b = k.sbuf("gbc_b", [128, TB], BF16)
    gsq_b = k.sbuf("gsq_b", [128, TB], BF16)
    yst_b = k.sbuf("yst_b", [128, 2 * TB], BF16)
    gsil_b = k.sbuf("gsil_b", [128, 2 * TB], BF16)
    ones_bf = k.sbuf("ones_bfM", [128, 1], BF16)
    onesr_f = k.sbuf("onesr_fM", [1, 128], F32)

    g_ms = grow[0:1, 0:TB]
    g_tmp = grow[0:1, TB:2 * TB]

    psP = k.psum("psP", [128, 2048], F32)
    psQ2 = k.psum("psQ2", [128, 1024], F32)
    psR2 = k.psum("psR2", [128, 512], F32)
    psT_b = k.psum("psT_bM", [128, 1024], BF16)
    ps_bc = psP[:, 0:128]
    ps_mt = psP[:, 512:640]
    ps_yd = psP[:, 1024:1152]
    ps_yo = psP[:, 1536:1664]
    ps_s = psQ2[:, 0:128]
    ps_m16 = psQ2[:, 512:640]
    ps_row = psR2[0:1, 0:128]
    rowh_p = [rowh_sb[:, 0:128], rowh_sb[:, 128:256]]
    diag_p = [diag_f[:, 0:128], diag_f[:, 128:256]]
    LT_p = [LT_b[:, 0:128], LT_b[:, 128:256]]
    MLT_p = [MLT_b[:, 0:128], MLT_b[:, 128:256]]
    BTds_p = [BTds_b[:, 0:128], BTds_b[:, 128:256]]
    hidT_p = [hidT_b[:, 0:64], hidT_b[:, 64:128]]
    hdtT_p = [hdtT_b[:, 0:64], hdtT_b[:, 64:128]]
    ys1_p = [ys1_b[:, 0:64], ys1_b[:, 64:128]]
    yT_p = [yT_b[:, 0:64], yT_b[:, 64:128]]
    ps_bc_p = [psP[:, 0:128], psP[:, 128:256]]
    ps_yd_p = [psP[:, 1024:1152], psP[:, 1152:1280]]
    ps_yo_p = [psP[:, 1536:1664], psP[:, 1664:1792]]
    ps_s_p = [psQ2[:, 0:128], psQ2[:, 128:256]]
    ps_row_p = [psR2[0:1, 0:128], psR2[0:1, 128:256]]
    pem = {}
    ps_gs = psQ2[0:1, 0:1024]
    ps_gb = psP[:, 0:1024]

    s_pe = k.sem("s_pe")
    s_dv = k.sem("s_dv")
    s_ac = k.sem("s_ac")
    s_dq = k.sem("s_dq")
    s_aq = k.sem("s_aq")
    s_in = k.sem("s_in")
    stPre = Stream(k, "stPre", depth=1)
    stWg = Stream(k, "stWg", depth=6)
    stHk = Stream(k, "stHk", depth=2)
    stYo = Stream(k, "stYo", depth=2)
    block = k.block()

    PE_TB = 224 + 8 * 71 + 9
    DV_TB = 7 + 10 + 1 + 696 + 8 + 10
    AC_TB = 10 + 1 + 136 + 8 + 9
    PE0, DV0, AC0 = 1, 2, 1

    def peB(tb):
        return PE0 + tb * PE_TB

    def dvB(tb):
        return DV0 + tb * DV_TB

    def acB(tb):
        return AC0 + tb * AC_TB

    wg_idx = {}
    _ct = 0
    for _tb in range(NTB):
        for _gi, (_m0, _gl) in enumerate(GRPS):
            for _mm in range(_gl):
                wg_idx[(_tb, _gi, _mm)] = _ct
                _ct += 1

    def peD(tb, c8):
        return peB(tb) + 224 + c8 * 71

    def dv_fs(h):
        return 3 * h + 2 * (h - 1 if h >= 2 else 0)

    def dv_bs(h):
        return 3 * min(h + 2, 16) + 2 * h

    def dvD(tb, c8):
        return dvB(tb) + 18 + c8 * 87

    def acD(tb, c8):
        return acB(tb) + 11 + c8 * 17

    @block.sync
    def _(eng):
        stPre.tick(eng.dma_start(out=csb[:], in_=cdram[:]))
        stPre.tick(eng.dma_start(out=convw_sb[:], in_=convw[:]))
        stPre.tick(eng.dma_start(out=convb_sb[:], in_=convb[:]))
        stPre.tick(eng.dma_start(out=dtb_sb[:], in_=dtb[:]))
        stPre.tick(eng.dma_start(out=acol_sb[:, 0:1], in_=alog[:]))
        stPre.tick(eng.dma_start(out=dp_sb[:], in_=dp_row[:]))
        stPre.tick(eng.dma_start(out=gnw_sb[:], in_=gnw[:]))
        stPre.tick(eng.dma_start(out=triu_sb[:], in_=triu[:]))
        stPre.tick(eng.dma_start(out=mtril_sb[:], in_=mtril[:]))
        stPre.tick(eng.dma_start(out=idnf_sb[:], in_=idn_f[:]))
        stPre.tick(eng.dma_start(out=idnb_sb[:], in_=idn_b[:]))

    @block.gpsimd
    def _(eng):
        eng.memset(ones_bf[:], 1.0)
        eng.memset(ones_ff[:], 1.0)
        eng.memset(onesr_f[:], 1.0).then_inc(s_in.h, 1)
        s_in.n += 1

    @block.scalar
    def _(eng):
        stPre.wait_all(eng)
        eng.activation(out=acol_sb[:, 1:2], in_=acol_sb[:, 0:1],
                       func=AF.Exp).then_inc(s_ac.h, 1)
        s_ac.n += 1

    @block.tensor
    def _(eng):
        stPre.wait_all(eng)
        eng.wait_ge(s_in.h, 1)
        eng.matmul(ps_m16[:, 0:16], onesr_f[:1, :], dp_sb[:1, :],
                   start=True, stop=True).then_inc(s_pe.h, 1)
        s_pe.n += 1

    @block.vector
    def _(eng):
        eng.wait_ge(s_pe.h, 1)
        eng.wait_ge(s_ac.h, 1)
        seq(eng, s_dq, eng.tensor_copy(dpbc_sb[:], ps_m16[:, 0:16]))
        eng.nop().then_inc(s_dv.h, 1)
        s_dv.n += 1
        seq(eng, s_dq, eng.tensor_scalar(out=acol_sb[:, 1:2],
                                         in0=acol_sb[:, 1:2], scalar1=-1.0,
                                         scalar2=None, op0=ALU.mult))
        seq(eng, s_dq, eng.memset(prev_f[:], 0.0))
        seq(eng, s_dq, eng.memset(prev_b[:], 0.0))
        seq(eng, s_dq, eng.memset(tails[:], 0.0))
        eng.nop().then_inc(s_dv.h, 1)
        s_dv.n += 1
    assert (s_pe.n, s_dv.n, s_ac.n) == (PE0, DV0, AC0)

    @block.gpsimd
    def _(eng):
        for tb in range(NTB):
            for gi, (m0, gl) in enumerate(GRPS):
                u = tb * NG + gi
                if u >= 2:
                    tb2, g2 = divmod(u - 2, NG)
                    eng.wait_ge(s_pe.h, peB(tb2) + (g2 + 1) * 32)
                for mm in range(gl):
                    stWg.tick(eng.dma_start(
                        out=wbuf[:, (u % 2) * 12288 + mm * 4096:
                                 (u % 2) * 12288 + (mm + 1) * 4096],
                        in_=wg[m0 + mm]))

    @block.sync
    def _(eng):
        for tb in range(NTB):
            for gi in range(NG):
                for kk in range(32):
                    i = (tb * NG + gi) * 32 + kk
                    if i >= 2:
                        tb2, r2 = divmod(i - 2, NG * 32)
                        eng.wait_ge(s_pe.h, peB(tb2) + r2 + 1)
                    stHk.tick(eng.dma_start(
                        out=hk[:, (i % 2) * TB:(i % 2) * TB + TB],
                        in_=h_in[:, kk * 8192 + tb * TB:
                                 kk * 8192 + (tb + 1) * TB]))

    @block.tensor
    def _(eng):
        for tb in range(NTB):
            bnew = (tb % 4 == 0)
            for gi, (m0, gl) in enumerate(GRPS):
                u = tb * NG + gi
                for mm in range(gl):
                    stWg.wait(eng, wg_idx[(tb, gi, mm)])
                if gi == 0:
                    if tb >= 1:
                        eng.wait_ge(s_dv.h, dvB(tb - 1) + DV_TB)
                    else:
                        eng.wait_ge(s_dv.h, DV0)
                else:
                    eng.wait_ge(s_dv.h, dvB(tb) + gi)
                for kk in range(32):
                    i = u * 32 + kk
                    stHk.wait(eng, i)
                    last = None
                    for mm in range(gl):
                        pv = (psP[:, 0:1024] if mm == 0 else
                              psP[:, 1024:2048] if mm == 1 else psQ2[:, :])
                        for half in range(2):
                            last = eng.matmul(
                                pv[:, half * 512:half * 512 + 512],
                                wbuf[:, (u % 2) * 12288 + mm * 4096 +
                                     kk * 128:(u % 2) * 12288 + mm * 4096 +
                                     (kk + 1) * 128],
                                hk[:, (i % 2) * TB + half * 512:
                                   (i % 2) * TB + half * 512 + 512],
                                start=(kk == 0), stop=(kk == 31))
                    last.then_inc(s_pe.h, 1)
                    s_pe.n += 1
            for c8 in range(8):
                pe_base = peD(tb, c8)
                dv_base = dvD(tb, c8)
                ac_base = acD(tb, c8)
                t0 = c8 * 128
                Bt = proj[:, 8 * TB + t0:8 * TB + t0 + 128]
                Ct = proj[:, 9 * TB + t0:9 * TB + t0 + 128]
                eng.wait_ge(s_dv.h, dvB(tb) + NG + 10 + 1)
                if tb + c8 > 0:
                    eng.wait_ge(s_dv.h, dv_base)
                eng.transpose(ps_m16[:, 0:16], adt_f[0:16, t0:t0 + 128],
                              idnf_sb[0:16, 0:16]).then_inc(s_pe.h, 1)
                s_pe.n += 1
                eng.wait_ge(s_dv.h, dv_base + 1)
                eng.transpose(ps_m16[:, 0:16], dtp_f[0:16, t0:t0 + 128],
                              idnf_sb[0:16, 0:16]).then_inc(s_pe.h, 1)
                s_pe.n += 1
                eng.wait_ge(s_dv.h, dv_base + 2)
                eng.matmul(ps_m16[:, 0:16], triu_sb[:], adtT_sb[:, 0:16],
                           start=True, stop=True).then_inc(s_pe.h, 1)
                s_pe.n += 1
                eng.wait_ge(s_dv.h, dv_base + 3)
                eng.transpose(ps_row[0:1, 0:16], acsl_sb[0:16, 0:1],
                              idnf_sb[0:16, 0:16]).then_inc(s_pe.h, 1)
                s_pe.n += 1
                eng.wait_ge(s_dv.h, dv_base + 4)
                eng.matmul(ps_m16[:, 0:16], onesr_f[:1, :],
                           rowh_sb[:1, 0:16],
                           start=True, stop=True).then_inc(s_pe.h, 1)
                s_pe.n += 1
                eng.wait_ge(s_dv.h, dv_base + 5)
                eng.matmul(ps_mt[:], Bt, Ct,
                           start=True, stop=True).then_inc(s_pe.h, 1)
                s_pe.n += 1
                eng.wait_ge(s_dv.h, dv_base + 6)
                eng.transpose(psT_b[:, 0:128], Bt,
                              idnb_sb[:]).then_inc(s_pe.h, 1)
                s_pe.n += 1
                def front(h):
                    p = h % 2
                    eng.wait_ge(s_dv.h, dv_base + 7)
                    eng.wait_ge(s_dv.h, dv_base + 7 + dv_fs(h) + 1)
                    if h >= 2:
                        eng.wait_ge(s_ac.h, ac_base + h)
                    eng.matmul(ps_bc_p[p][:], ones_ff[:], diag_p[p][:],
                               start=True, stop=False)
                    eng.matmul(ps_bc_p[p][:], idnf_sb[:], mtril_sb[:],
                               start=False, stop=True).then_inc(s_pe.h, 1)
                    s_pe.n += 1
                    pem[(tb, c8, 3, h)] = s_pe.n
                    if h >= 2:
                        eng.wait_ge(s_dv.h,
                                    dv_base + 7 + dv_fs(h - 2) + 3)
                    mt = h // 2
                    rb = (h % 2) * 64
                    eng.transpose(psT_b[:, 256 + p * 64:320 + p * 64],
                                  proj[rb:rb + 64, mt * TB + t0:
                                       mt * TB + t0 + 128],
                                  idnb_sb[rb:rb + 64, rb:rb + 64]
                                  ).then_inc(s_pe.h, 1)
                    s_pe.n += 1
                    pem[(tb, c8, 6, h)] = s_pe.n

                def back(h):
                    p = h % 2
                    eng.wait_ge(s_dv.h, dv_base + 7 + dv_fs(h) + 3)
                    if h >= 2:
                        eng.wait_ge(s_dv.h, dv_base + 7 + dv_bs(h - 2) + 2)
                    eng.matmul(ps_yd_p[p][:, 0:64], MLT_p[p][:],
                               hdtT_p[p][:], start=True, stop=True)
                    eng.matmul(ps_yo_p[p][:, 0:64], Ct,
                               prev_b[:, h * 64:(h + 1) * 64],
                               start=True, stop=True)
                    eng.matmul(ps_s_p[p][:, 0:64], BTds_p[p][:],
                               hdtT_p[p][:],
                               start=True, stop=True).then_inc(s_pe.h, 1)
                    s_pe.n += 1
                    pem[(tb, c8, 8, h)] = s_pe.n
                    eng.wait_ge(s_dv.h, dv_base + 7 + dv_bs(h) + 1)
                    eng.transpose(psT_b[0:64, 512 + p * 128:640 + p * 128],
                                  yT_p[p][:],
                                  idnb_sb[:]).then_inc(s_pe.h, 1)
                    s_pe.n += 1
                    pem[(tb, c8, 10, h)] = s_pe.n

                front(0)
                front(1)
                for h in range(2, NH):
                    back(h - 2)
                    front(h)
                back(NH - 2)
                back(NH - 1)
            for mt in range(8):
                eng.wait_ge(s_ac.h, acB(tb) + 11 + 136 + 8 + mt + 1)
                last = None
                for half in range(2):
                    sl = slice(half * 512, half * 512 + 512)
                    last = eng.matmul(ps_gs[:, sl], ones_bf[:, 0:1],
                                      gsq_b[:, sl], start=(mt == 0),
                                      stop=(mt == 7))
                last.then_inc(s_pe.h, 1)
                s_pe.n += 1
            eng.wait_ge(s_dv.h, dvB(tb) + 18 + 696 + 8 + 1)
            eng.wait_ge(s_ac.h, acB(tb) + 11 + 136 + 8 + 9)
            eng.matmul(ps_gb[:, 0:512], onesr_f[:1, :], g_ms[:, 0:512],
                       start=True, stop=True)
            eng.matmul(ps_gb[:, 512:1024], onesr_f[:1, :], g_ms[:, 512:1024],
                       start=True, stop=True).then_inc(s_pe.h, 1)
            s_pe.n += 1
    assert s_pe.n == PE0 + NTB * PE_TB, s_pe.n

    @block.vector
    def _(eng):
        for tb in range(NTB):
            if tb == 4:
                seq(eng, s_dq, eng.memset(prev_f[:], 0.0))
                seq(eng, s_dq, eng.memset(prev_b[:], 0.0))
                seq(eng, s_dq, eng.memset(tails[:], 0.0))
            for gi, (m0, gl) in enumerate(GRPS):
                eng.wait_ge(s_pe.h, peB(tb) + (gi + 1) * 32)
                for mm in range(gl):
                    pv = (psP[:, 0:1024] if mm == 0 else
                          psP[:, 1024:2048] if mm == 1 else psQ2[:, :])
                    m = m0 + mm
                    seq(eng, s_dq, eng.tensor_copy(
                        proj[:, m * TB:(m + 1) * TB], pv[:]))
                eng.nop().then_inc(s_dv.h, 1)
                s_dv.n += 1
            for ct in range(10):
                if ct >= 1:
                    eng.wait_ge(s_ac.h, acB(tb) + ct)
                seq(eng, s_dq, eng.tensor_copy(cin[:, 0:3],
                                               tails[:, ct * 3:ct * 3 + 3]))
                seq(eng, s_dq, eng.tensor_copy(cin[:, 3:1027],
                                               proj[:, ct * TB:(ct + 1) * TB]))
                seq(eng, s_dq, eng.tensor_copy(tails[:, ct * 3:ct * 3 + 3],
                                               cin[:, 1024:1027]))
                seq(eng, s_dq, eng.tensor_scalar(
                    out=acc_f[:], in0=cin[:, 0:1024],
                    scalar1=convw_sb[:, ct * 4:ct * 4 + 1], scalar2=None,
                    op0=ALU.mult))
                for tap in range(1, 4):
                    seq(eng, s_dq, eng.scalar_tensor_tensor(
                        out=acc_f[:], in0=cin[:, tap:tap + 1024],
                        scalar=convw_sb[:, ct * 4 + tap:ct * 4 + tap + 1],
                        in1=acc_f[:], op0=ALU.mult, op1=ALU.add))
                eng.nop().then_inc(s_dv.h, 1)
                s_dv.n += 1
            eng.wait_ge(s_ac.h, acB(tb) + 11)
            seq(eng, s_dq, eng.tensor_scalar(
                out=adt_f[0:16, :], in0=dtp_f[0:16, :],
                scalar1=acol_sb[0:16, 1:2], scalar2=None, op0=ALU.mult))
            eng.nop().then_inc(s_dv.h, 1)
            s_dv.n += 1
            for c8 in range(8):
                dv_base = dvD(tb, c8)
                pe_base = peD(tb, c8)
                ac_base = acD(tb, c8)
                t0 = c8 * 128
                eng.wait_ge(s_pe.h, pe_base + 1)
                seq(eng, s_dq, eng.tensor_copy(adtT_sb[:, 0:16],
                                               ps_m16[:, 0:16]))
                eng.nop().then_inc(s_dv.h, 1)
                s_dv.n += 1
                eng.wait_ge(s_pe.h, pe_base + 2)
                seq(eng, s_dq, eng.tensor_copy(dtpT_sb[:, 0:16],
                                               ps_m16[:, 0:16]))
                eng.nop().then_inc(s_dv.h, 1)
                s_dv.n += 1
                eng.wait_ge(s_pe.h, pe_base + 3)
                seq(eng, s_dq, eng.tensor_copy(acsT_sb[:, 0:16],
                                               ps_m16[:, 0:16]))
                seq(eng, s_dq, eng.tensor_scalar(
                    out=nacsT_sb[:, 0:16], in0=acsT_sb[:, 0:16], scalar1=-1.0,
                    scalar2=None, op0=ALU.mult))
                seq(eng, s_dq, eng.tensor_scalar(
                    out=junk16[0:16, :], in0=adt_f[0:16, t0:t0 + 128],
                    scalar1=1.0, scalar2=0.0, op0=ALU.mult, op1=ALU.add,
                    accum_out=acsl_sb[0:16, 0:1]))
                eng.nop().then_inc(s_dv.h, 1)
                s_dv.n += 1
                eng.wait_ge(s_pe.h, pe_base + 4)
                seq(eng, s_dq, eng.tensor_copy(rowh_sb[0:1, 0:16],
                                               ps_row[0:1, 0:16]))
                eng.nop().then_inc(s_dv.h, 1)
                s_dv.n += 1
                eng.wait_ge(s_pe.h, pe_base + 5)
                seq(eng, s_dq, eng.tensor_copy(acsbc_sb[:, 0:16],
                                               ps_m16[:, 0:16]))
                seq(eng, s_dq, eng.tensor_tensor(
                    out=junk16[:, 0:16], in0=acsbc_sb[:, 0:16],
                    in1=acsT_sb[:, 0:16], op=ALU.subtract))
                eng.nop().then_inc(s_dv.h, 1)
                s_dv.n += 1
                eng.wait_ge(s_pe.h, pe_base + 6)
                seq(eng, s_dq, eng.tensor_copy(MT_b[:], ps_mt[:]))
                eng.nop().then_inc(s_dv.h, 1)
                s_dv.n += 1
                eng.wait_ge(s_pe.h, pe_base + 7)
                seq(eng, s_dq, eng.tensor_copy(BT_sb[:], psT_b[:, 0:128]))
                eng.nop().then_inc(s_dv.h, 1)
                s_dv.n += 1
                def dv_front(h):
                    p = h % 2
                    if h >= 2:
                        eng.wait_ge(s_pe.h, pem[(tb, c8, 3, h - 2)])
                    seq(eng, s_dq, eng.tensor_scalar(
                        out=diag_p[p][:], in0=idnf_sb[:],
                        scalar1=acsT_sb[:, h:h + 1], scalar2=None,
                        op0=ALU.mult))
                    eng.nop().then_inc(s_dv.h, 1)
                    s_dv.n += 1
                    eng.wait_ge(s_ac.h, ac_base + 1 + h + 1)
                    seq(eng, s_dq, eng.tensor_tensor(
                        out=MLT_p[p][:], in0=MT_b[:], in1=LT_p[p][:],
                        op=ALU.mult))
                    eng.nop().then_inc(s_dv.h, 1)
                    s_dv.n += 1
                    eng.wait_ge(s_pe.h, pem[(tb, c8, 6, h)])
                    seq(eng, s_dq, eng.tensor_copy(
                        hidT_p[p][:], psT_b[:, 256 + p * 64:320 + p * 64]))
                    seq(eng, s_dq, eng.tensor_scalar(
                        out=hdtT_p[p][:], in0=hidT_p[p][:],
                        scalar1=dtpT_sb[:, h:h + 1], scalar2=None,
                        op0=ALU.mult))
                    seq(eng, s_dq, eng.tensor_scalar(
                        out=BTds_p[p][:], in0=BT_sb[:],
                        scalar1=dsT_sb[:, h:h + 1], scalar2=None,
                        op0=ALU.mult))
                    eng.nop().then_inc(s_dv.h, 1)
                    s_dv.n += 1

                def dv_back(h):
                    p = h % 2
                    eng.wait_ge(s_pe.h, pem[(tb, c8, 8, h)])
                    seq(eng, s_dq, eng.scalar_tensor_tensor(
                        out=ys1_p[p][:], in0=hidT_p[p][:],
                        scalar=dpbc_sb[:, h:h + 1], in1=ps_yd_p[p][:, 0:64],
                        op0=ALU.mult, op1=ALU.add))
                    seq(eng, s_dq, eng.scalar_tensor_tensor(
                        out=yT_p[p][:], in0=ps_yo_p[p][:, 0:64],
                        scalar=eAT_sb[:, h:h + 1], in1=ys1_p[p][:],
                        op0=ALU.mult, op1=ALU.add))
                    seq(eng, s_dq, eng.scalar_tensor_tensor(
                        out=prev_f[:, h * 64:(h + 1) * 64],
                        in0=prev_f[:, h * 64:(h + 1) * 64],
                        scalar=egbc_sb[:, h:h + 1], in1=ps_s_p[p][:, 0:64],
                        op0=ALU.mult, op1=ALU.add))
                    seq(eng, s_dq, eng.tensor_copy(
                        prev_b[:, h * 64:(h + 1) * 64],
                        prev_f[:, h * 64:(h + 1) * 64]))
                    eng.nop().then_inc(s_dv.h, 1)
                    s_dv.n += 1
                    eng.wait_ge(s_pe.h, pem[(tb, c8, 10, h)])
                    mt = h // 2
                    rb = (h % 2) * 64
                    seq(eng, s_dq, eng.tensor_copy(
                        y_asm[rb:rb + 64, mt * TB + t0:mt * TB + t0 + 128],
                        psT_b[0:64, 512 + p * 128:640 + p * 128]))
                    eng.nop().then_inc(s_dv.h, 1)
                    s_dv.n += 1

                base_chk = s_dv.n
                dv_front(0)
                dv_front(1)
                for h in range(2, NH):
                    dv_back(h - 2)
                    dv_front(h)
                dv_back(NH - 2)
                dv_back(NH - 1)
                assert s_dv.n == base_chk + 80, s_dv.n
            for mt in range(8):
                eng.wait_ge(s_ac.h, acB(tb) + 11 + 136 + mt + 1)
                seq(eng, s_dq, eng.tensor_tensor(
                    out=y_asm[:, mt * TB:(mt + 1) * TB],
                    in0=y_asm[:, mt * TB:(mt + 1) * TB],
                    in1=gsil_b[:, (mt % 2) * TB:(mt % 2) * TB + TB],
                    op=ALU.mult))
                eng.nop().then_inc(s_dv.h, 1)
                s_dv.n += 1
            eng.wait_ge(s_pe.h, peB(tb) + 224 + 8 * 71 + 8)
            seq(eng, s_dq, eng.tensor_scalar(
                out=g_ms[:], in0=ps_gs[:], scalar1=1.0 / 1024.0,
                scalar2=None, op0=ALU.mult))
            eng.nop().then_inc(s_dv.h, 1)
            s_dv.n += 1
            eng.wait_ge(s_pe.h, peB(tb) + PE_TB)
            seq(eng, s_dq, eng.tensor_copy(gbc_b[:], ps_gb[:, 0:1024]))
            eng.nop().then_inc(s_dv.h, 1)
            s_dv.n += 1
            for mt in range(8):
                stYo.wait(eng, tb * 8 + mt - 2)
                seq(eng, s_dq, eng.scalar_tensor_tensor(
                    out=yst_b[:, (mt % 2) * TB:(mt % 2) * TB + TB],
                    in0=y_asm[:, mt * TB:(mt + 1) * TB],
                    scalar=gnw_sb[:, mt:mt + 1], in1=gbc_b[:],
                    op0=ALU.mult, op1=ALU.mult))
                eng.nop().then_inc(s_dv.h, 1)
                s_dv.n += 1
    assert s_dv.n == DV0 + NTB * DV_TB, s_dv.n

    @block.scalar
    def _(eng):
        for tb in range(NTB):
            for ct in range(10):
                eng.wait_ge(s_dv.h, dvB(tb) + NG + ct + 1)
                eng.activation(out=proj[:, ct * TB:(ct + 1) * TB],
                               in_=acc_f[:], func=AF.Silu,
                               bias=convb_sb[:, ct:ct + 1]).then_inc(s_ac.h, 1)
                s_ac.n += 1
            seq(eng, s_aq, eng.activation(
                out=dtf[0:16, :], in_=proj[0:16, 10 * TB:11 * TB],
                func=AF.Exp, bias=dtb_sb[0:16, 0:1]))
            seq(eng, s_aq, eng.activation(
                out=dtp_f[0:16, :], in_=dtf[0:16, :], func=AF.Ln, bias=1.0))
            eng.nop().then_inc(s_ac.h, 1)
            s_ac.n += 1
            for c8 in range(8):
                dv_base = dvD(tb, c8)
                ac_base = acD(tb, c8)
                pe_base = peD(tb, c8)
                eng.wait_ge(s_dv.h, dv_base + 5)
                seq(eng, s_aq, eng.activation(out=dsT_sb[:, 0:16],
                                              in_=junk16[:, 0:16],
                                              func=AF.Exp))
                seq(eng, s_aq, eng.activation(out=eAT_sb[:, 0:16],
                                              in_=acsT_sb[:, 0:16],
                                              func=AF.Exp))
                seq(eng, s_aq, eng.activation(out=egbc_sb[:, 0:16],
                                              in_=acsbc_sb[:, 0:16],
                                              func=AF.Exp))
                eng.nop().then_inc(s_ac.h, 1)
                s_ac.n += 1
                for h in range(NH):
                    p = h % 2
                    eng.wait_ge(s_pe.h, pem[(tb, c8, 3, h)])
                    eng.activation(out=LT_p[p][:], in_=ps_bc_p[p][:],
                                   func=AF.Exp,
                                   bias=nacsT_sb[:, h:h + 1]
                                   ).then_inc(s_ac.h, 1)
                    s_ac.n += 1
            for mt in range(8):
                eng.wait_ge(s_dv.h, dvB(tb) + 18 + 696)
                if mt >= 2:
                    eng.wait_ge(s_dv.h, dvB(tb) + 18 + 696 + mt - 1)
                eng.activation(out=gsil_b[:, (mt % 2) * TB:(mt % 2) * TB + TB],
                               in_=proj[:, (11 + mt) * TB:(12 + mt) * TB],
                               func=AF.Silu).then_inc(s_ac.h, 1)
                s_ac.n += 1
            for mt in range(8):
                eng.wait_ge(s_dv.h, dvB(tb) + 18 + 696 + mt + 1)
                if mt >= 1:
                    eng.wait_ge(s_pe.h, peB(tb) + 224 + 8 * 71 + mt)
                eng.activation(out=gsq_b[:], in_=y_asm[:, mt * TB:(mt + 1) * TB],
                               func=AF.Square).then_inc(s_ac.h, 1)
                s_ac.n += 1
            eng.wait_ge(s_dv.h, dvB(tb) + 18 + 696 + 8 + 1)
            seq(eng, s_aq, eng.activation(out=g_tmp[:], in_=g_ms[:],
                                          func=AF.Ln, bias=EPS))
            eng.activation(out=g_ms[:], in_=g_tmp[:], func=AF.Exp,
                           scale=-0.5).then_inc(s_ac.h, 1)
            s_ac.n += 1
            for mt in range(8):
                eng.wait_ge(s_dv.h, dvB(tb) + 18 + 696 + 8 + 2 + mt + 1)
                stYo.tick(eng.dma_start(
                    out=y_out[mt * 128:(mt + 1) * 128,
                              tb * TB:(tb + 1) * TB],
                    in_=yst_b[:, (mt % 2) * TB:(mt % 2) * TB + TB]))
        stYo.wait_all(eng)
    assert s_ac.n == AC0 + NTB * AC_TB, s_ac.n

    return k


from concourse.bass_utils import run_bass_kernel_spmd

NCORES = 8


def build_Q(n=4096):
    t = np.arange(n)
    Q = np.zeros((n, n), np.float64)
    Q[:, 0] = 1.0 / np.sqrt(n)
    f = np.arange(1, n // 2)
    ang = 2 * np.pi * np.outer(t, f) / n
    Q[:, 2 * f - 1] = np.sqrt(2.0 / n) * np.cos(ang)
    Q[:, 2 * f] = np.sqrt(2.0 / n) * np.sin(ang)
    Q[:, n - 1] = ((-1.0) ** t) / np.sqrt(n)
    return Q.astype(np.float32)


def _run(build_fn, in_maps):
    k = build_fn()
    res = run_bass_kernel_spmd(k.nc, in_maps, core_ids=list(range(NCORES)))
    return res.results


def kernel(**inp):
    f32 = np.float32
    g = lambda n: np.ascontiguousarray(np.asarray(inp[n], f32))
    x = g("L0_attn_x")
    x_fm = np.ascontiguousarray(x.reshape(8192, 4096).T)
    xs = [np.ascontiguousarray(x_fm[:, c * 1024:(c + 1) * 1024])
          for c in range(NCORES)]

    nw = np.ascontiguousarray(g("L0_attn_norm_w").reshape(32, 128).T)
    maps = [{"x_in": xs[c], "nw_in": nw} for c in range(NCORES)]
    res = _run(lambda: build_l1a(1024), maps)
    h_full = np.concatenate([np.asarray(r["h_out"]) for r in res], axis=1)
    t_parts = [np.asarray(r["t_out"]) for r in res]
    h_tiled = np.ascontiguousarray(
        h_full.reshape(32, 128, 8192).transpose(1, 0, 2)).reshape(128, 32 * 8192)

    ipw = g("L0_attn_in_proj_w")
    cw = g("L0_attn_conv_w")
    cb = g("L0_attn_conv_b")
    dtbias = g("L0_attn_dt_bias")
    alog = g("L0_attn_A_log")
    Dp = g("L0_attn_Dp")
    gnw = g("L0_attn_gnorm_w")
    triu = (np.arange(128)[:, None] <= np.arange(128)[None, :]).astype(f32)
    mtrilc = np.where(np.arange(128)[:, None] <= np.arange(128)[None, :],
                      0.0, -1e30).astype(f32)
    idnf = np.eye(128, dtype=f32)
    idnb = np.eye(128).astype(ml_dtypes.bfloat16)
    pad = lambda a, n: np.pad(a, ((0, n - a.shape[0]),) +
                              ((0, 0),) * (a.ndim - 1))
    maps = []
    for gi in range(8):
        hid = ipw[8192 + gi * 1024:8192 + (gi + 1) * 1024]
        Bg = ipw[16384 + gi * 128:16384 + (gi + 1) * 128]
        Cg = ipw[17408 + gi * 128:17408 + (gi + 1) * 128]
        dtg = ipw[18432 + gi * 16:18432 + (gi + 1) * 16]
        gate = ipw[gi * 1024:(gi + 1) * 1024]
        wg_rows = np.concatenate([hid, Bg, Cg, pad(dtg, 128), gate], 0)
        cwg = np.concatenate([cw[gi * 1024:(gi + 1) * 1024],
                              cw[8192 + gi * 128:8192 + (gi + 1) * 128],
                              cw[9216 + gi * 128:9216 + (gi + 1) * 128]], 0)
        cbg = np.concatenate([cb[gi * 1024:(gi + 1) * 1024],
                              cb[8192 + gi * 128:8192 + (gi + 1) * 128],
                              cb[9216 + gi * 128:9216 + (gi + 1) * 128]], 0)
        maps.append({
            "h_in": h_tiled,
            "wg": tile_w(np.ascontiguousarray(wg_rows.T), 32, 19),
            "convw": np.ascontiguousarray(
                cwg.reshape(10, 128, 4).transpose(1, 0, 2).reshape(128, 40)),
            "convb": np.ascontiguousarray(cbg.reshape(10, 128).T),
            "dtb": pad(dtbias[gi * 16:(gi + 1) * 16].reshape(16, 1), 128),
            "alog": pad(alog[gi * 16:(gi + 1) * 16].reshape(16, 1), 128),
            "dp_row": np.ascontiguousarray(
                Dp[gi * 16:(gi + 1) * 16].reshape(1, 16)),
            "gnw": np.ascontiguousarray(
                gnw[gi * 1024:(gi + 1) * 1024].reshape(8, 128).T),
            "triu": triu, "mtril": mtrilc, "idn_f": idnf, "idn_b": idnb,
        })
    res = _run(build_l1b, maps)
    y_full = np.concatenate([np.asarray(r["y_out"]) for r in res], axis=0)

    opT = tile_w(np.ascontiguousarray(g("L0_attn_out_proj_w").T), 64, 32)
    wT2 = tile_w(np.ascontiguousarray(g("L0_ffn_W").T), 32, 32)
    bias2 = np.ascontiguousarray(g("L0_ffn_bias").reshape(32, 128).T)
    A2 = g("L0_ffn_lora_A").reshape(128, 4096)
    B2 = np.ascontiguousarray(
        g("L0_ffn_lora_B").transpose(0, 2, 1).reshape(128, 4096))
    aT2 = tile_a(np.ascontiguousarray(A2.T))
    coef2 = np.ascontiguousarray(
        np.repeat(g("L0_ffn_lora_weights"), 16).reshape(128, 1))
    maps = []
    for c in range(NCORES):
        ysl = np.ascontiguousarray(y_full[:, c * 1024:(c + 1) * 1024])
        y_t = np.ascontiguousarray(
            ysl.reshape(64, 128, 1024).transpose(1, 0, 2)).reshape(128, 64 * 1024)
        maps.append({"y_in": y_t, "x_in": xs[c], "tln_in": t_parts[c],
                     "opT": opT, "wT": wT2,
                     "bias_in": bias2, "aT": aT2, "bT": B2,
                     "coef_in": coef2})
    res = _run(lambda: build_l2(1024), maps)
    x2_full = np.concatenate([np.asarray(r["x2_out"]) for r in res], axis=1)
    x2s = [np.ascontiguousarray(x2_full[:, c * 1024:(c + 1) * 1024])
           for c in range(NCORES)]

    wq, wk, wv = g("L1_attn_wq"), g("L1_attn_wk"), g("L1_attn_wv")
    bq, bk, bv = g("L1_attn_bq"), g("L1_attn_bk"), g("L1_attn_bv")
    maps = []
    for c in range(NCORES):
        b = c // 4
        hb = 8 * (c % 4)
        r0, r1 = hb * 128, hb * 128 + 1024
        wqk_rows = np.concatenate([wq[r0:r1], wk[r0:r1]], 0)
        maps.append({
            "x_in": np.ascontiguousarray(x2_full[:, b * 4096:(b + 1) * 4096]),
            "wqk": tile_w(np.ascontiguousarray(wqk_rows.T), 32, 16),
            "wvT": np.ascontiguousarray(
                np.ascontiguousarray(wv[r0:r1].T).reshape(32, 128, 1024)),
            "bqk": np.ascontiguousarray(
                np.concatenate([bq[r0:r1], bk[r0:r1]]).reshape(16, 128).T),
            "bv_row": np.ascontiguousarray(bv[r0:r1].reshape(1, 1024)),
            "idn_in": idnb,
        })
    res = _run(build_l3, maps)
    ctx_full = [np.zeros((4096, 4096), ml_dtypes.bfloat16) for _ in range(2)]
    for c in range(NCORES):
        b = c // 4
        ctx_full[b][(c % 4) * 1024:(c % 4 + 1) * 1024, :] = \
            np.asarray(res[c]["ctx_out"])

    Q = build_Q(4096)
    QT4 = np.ascontiguousarray(Q.T)
    qt = np.ascontiguousarray(
        QT4.reshape(32, 128, 8, 512).transpose(2, 1, 0, 3)
    ).reshape(8, 128, 32 * 512).astype(ml_dtypes.bfloat16)
    woT = tile_w(np.ascontiguousarray(g("L1_attn_wo").T), 32, 32)
    wbT4 = tile_w(np.ascontiguousarray(g("L1_ffn_W_base").T), 32, 32)
    bo4 = np.ascontiguousarray(g("L1_attn_bo").reshape(32, 128).T)
    bb4 = np.ascontiguousarray(g("L1_ffn_b_base").reshape(32, 128).T)
    encT = np.ascontiguousarray(
        np.concatenate([g("L1_ffn_enc0"), g("L1_ffn_enc1")], 0).T)
    c0 = g("L1_ffn_coefs0")
    c1 = g("L1_ffn_coefs1")
    m0 = np.asarray(inp["L1_ffn_mask0"]).astype(np.uint8)
    m1 = np.asarray(inp["L1_ffn_mask1"]).astype(np.uint8)
    c0t = np.stack([tile_a(np.ascontiguousarray(c0[e])) for e in range(3)])
    c1t = np.stack([tile_a(np.ascontiguousarray(c1[e])) for e in range(3)])
    m0t = np.stack([tile_a(np.ascontiguousarray(m0[e])) for e in range(3)])
    m1t = np.stack([tile_a(np.ascontiguousarray(m1[e])) for e in range(3)])
    encTt = tile_a(encT)
    maps = []
    for c in range(NCORES):
        b = c // 4
        csl = np.ascontiguousarray(
            ctx_full[b][:, (c % 4) * 1024:(c % 4 + 1) * 1024])
        ctx_t = np.ascontiguousarray(
            csl.reshape(32, 128, 1024).transpose(1, 0, 2)
        ).reshape(128, 32 * 1024)
        maps.append({"ctx_in": ctx_t, "x_in": x2s[c], "woT": woT,
                     "wbT": wbT4, "bo_in": bo4, "bb_in": bb4,
                     "encT_in": encTt, "qt_in": qt, "c0_in": c0t,
                     "c1_in": c1t, "m0_in": m0t, "m1_in": m1t})
    res = _run(lambda: build_l4(1024), maps)
    out_fm = np.concatenate(
        [np.asarray(r["x4_out"], dtype=np.float32) for r in res], axis=1)
    return np.ascontiguousarray(out_fm.T).reshape(2, 4096, 4096)
